# version 02 (final)
# baseline (speedup 1.0000x reference)
import math
import numpy as np
from contextlib import ExitStack

import concourse.bass as bass
import concourse.mybir as mybir
from concourse.bass_utils import run_bass_kernel_spmd

F32 = mybir.dt.float32
BF16 = mybir.dt.bfloat16
F16 = mybir.dt.float16
I32 = mybir.dt.int32
U16 = mybir.dt.uint16
U32 = mybir.dt.uint32
AF = mybir.ActivationFunctionType
ALU = mybir.AluOpType
AX = mybir.AxisListType.X

NCORES = 8
B, S, D = 2, 4096, 2048
T = B * S
CHUNK = 128
NCHUNK = T // CHUNK
TPC = T // NCORES
NTILE = TPC // CHUNK
NHEAD, NKV, HD = 32, 8, 64
G = NHEAD // NKV
E, K_TOP, I_MOE, CAP = 32, 6, 1024, 768
EPC = E // NCORES
MOE_SCALE = 1.5
SWIGLU_LIMIT = 10.0
I_FFN1 = 5632
EPS_LN = 1e-5
EPS_RMS = 1e-6
NSLOT = E * CAP
TRASH = NSLOT + 16
SSROWS = NSLOT + 160
GUS = 2 * I_FFN1 // NCORES
DNS = D // NCORES


def own_chunks(core):
    per_batch = sorted({core, 15 - core, 16 + core, 31 - core})
    return [32 * b + j for b in range(B) for j in per_batch]


OWN = [own_chunks(c) for c in range(NCORES)]
OWN_ROWS = [np.concatenate([np.arange(j * CHUNK, (j + 1) * CHUNK) for j in OWN[c]])
            for c in range(NCORES)]


def shard_rows(full):
    return [np.ascontiguousarray(full[OWN_ROWS[c]]) for c in range(NCORES)]


def unshard_rows(parts):
    parts = [np.asarray(p) for p in parts]
    out = np.empty((T,) + parts[0].shape[1:], dtype=parts[0].dtype)
    for c in range(NCORES):
        out[OWN_ROWS[c]] = parts[c]
    return out


class Ctx:
    def __init__(self):
        self.nc = bass.Bass("TRN2", target_bir_lowering=False)
        self.stack = ExitStack()

    def sbuf(self, name, shape, dtype):
        return self.stack.enter_context(self.nc.sbuf_tensor(name, shape, dtype))

    def psum(self, name, shape, dtype=F32):
        return self.stack.enter_context(self.nc.psum_tensor(name, shape, dtype))

    def sem(self, name):
        return self.stack.enter_context(self.nc.semaphore(name))

    def block(self):
        return self.nc.Block()

    def close(self):
        self.stack.close()


def emit_ident(g, ident, sem, cnt):
    g.memset(ident[:], 1.0).then_inc(sem, 1)
    cnt[0] += 1
    g.wait_ge(sem, cnt[0])
    g.affine_select(out=ident[:], in_=ident[:], pattern=[[1, ident.shape[1]]],
                    compare_op=ALU.is_equal, fill=0.0, base=0,
                    channel_multiplier=-1).then_inc(sem, 1)
    cnt[0] += 1



P0_FAM = [
    ("q_rows", 2, 2048), ("o_rows", 2, 2048), ("kv_rows", 1, 2048),
    ("sg_rows", 1, 2048), ("su_rows", 1, 2048), ("sd_rows", 2, 1024),
    ("wg_rows", 32, 2048), ("wu_rows", 32, 2048), ("wd_rows", 64, 1024),
]


def build_P0():
    ctx = Ctx()
    nc = ctx.nc
    d_in = {n: nc.dram_tensor(n, [nj, 128, w], F32, kind="ExternalInput").ap()
            for n, nj, w in P0_FAM}
    d_out = {n: nc.dram_tensor("T_" + n, [w // 128, 128, nj * 128], BF16,
                               kind="ExternalOutput").ap() for n, nj, w in P0_FAM}
    d_gup = nc.dram_tensor("gup", [GUS, D // 8], I32, kind="ExternalInput").ap()
    d_gus = nc.dram_tensor("gus", [D // 128, GUS], F16, kind="ExternalInput").ap()
    d_dnp = nc.dram_tensor("dnp", [DNS, I_FFN1 // 8], I32, kind="ExternalInput").ap()
    d_dns = nc.dram_tensor("dns", [I_FFN1 // 128, DNS], F16, kind="ExternalInput").ap()
    d_wgu = nc.dram_tensor("wguT", [D // 128, 128, GUS], BF16,
                           kind="ExternalOutput").ap()
    d_wdn = nc.dram_tensor("wdnT", [I_FFN1 // 128, 128, DNS], BF16,
                           kind="ExternalOutput").ap()

    jobs = []
    for n, nj, w in P0_FAM:
        for j in range(nj):
            jobs.append((d_in[n][j], d_out[n], j, w))
    NJ = len(jobs)
    cum_packs = [0]
    packs = []
    for ji, (_, _, _, w) in enumerate(jobs):
        for pi in range(w // 512):
            packs.append((ji, pi, 4))
        cum_packs.append(len(packs))
    NP = len(packs)

    dq_jobs = []
    for j in range(GUS // 128):
        dq_jobs.append((d_gup[j * 128:(j + 1) * 128, :],
                        d_gus[:, j * 128:(j + 1) * 128], d_wgu, D // 8, D // 128,
                        j * 128))
    for j in range(DNS // 128):
        dq_jobs.append((d_dnp[j * 128:(j + 1) * 128, :],
                        d_dns[:, j * 128:(j + 1) * 128], d_wdn, I_FFN1 // 8,
                        I_FFN1 // 128, j * 128))
    NDQ = len(dq_jobs)

    lds = [ctx.sem(f"ld{i}") for i in range(4)]
    mm = ctx.sem("mm")
    ev = ctx.sem("ev")
    sts = [ctx.sem(f"st{i}") for i in range(2)]
    gp = ctx.sem("gp")
    dqls = [ctx.sem(f"dql{i}") for i in range(2)]
    dqv = ctx.sem("dqv")
    dqm = ctx.sem("dqm")
    dqe = ctx.sem("dqe")
    dqss = [ctx.sem(f"dqs{i}") for i in range(2)]

    ident = ctx.sbuf("ident", [128, 128], BF16)
    ident16 = ctx.sbuf("ident16", [128, 128], F16)
    tbf = [ctx.sbuf(f"tbf{i}", [128, 2048], BF16) for i in range(2)]
    tps = [ctx.psum(f"tps{i}", [128, 512], F32) for i in range(2)]
    tev = [ctx.sbuf(f"tev{i}", [128, 512], BF16) for i in range(2)]

    W8 = I_FFN1
    dq_pk = ctx.sbuf("dq_pk", [128, I_FFN1 // 8], I32)
    dq_s16 = ctx.sbuf("dq_s16", [64, 128], F16)
    dq_sps_f = ctx.psum("dq_sps", [128, 512], F32)
    dq_sc = ctx.sbuf("dq_sc", [128, 64], F32)
    dq_scf = ctx.sbuf("dq_scf", [128, W8 // 4], F32)
    dq_n = ctx.sbuf("dq_n", [128, W8 // 4], U16)
    dq_a = ctx.sbuf("dq_a", [128, W8 // 4], U16)
    dq_A = ctx.sbuf("dq_A", [128, W8 // 4], U16)
    dq_t = ctx.sbuf("dq_t", [128, W8 // 4], BF16)
    dq_Bv = ctx.sbuf("dq_B", [128, W8 // 4], BF16)
    dq_g = ctx.sbuf("dq_g", [128, W8 // 4], U16)
    dq_w = ctx.sbuf("dq_w", [128, W8 // 4], BF16)
    dps = [ctx.psum(f"dps{i}", [128, 512], F32) for i in range(2)]
    dev = [ctx.sbuf(f"dev{i}", [128, 512], BF16) for i in range(2)]

    dq_packs = []
    for dj, (_, _, _, nw, ngrp, _) in enumerate(dq_jobs):
        nb = (2 * nw) // 128
        for pl in range(4):
            b0 = 0
            while b0 < nb:
                nb_here = min(4, nb - b0)
                dq_packs.append((dj, pl, b0, nb_here))
                b0 += nb_here
    NDP = len(dq_packs)
    plane_done = {}
    cnt = 0
    for dj in range(NDQ):
        for pl in range(4):
            cnt += 1
            plane_done[(dj, pl)] = cnt

    with ctx.block() as block:

        @block.gpsimd
        def _(g):
            gcnt = [0]
            emit_ident(g, ident, gp, gcnt)
            emit_ident(g, ident16, gp, gcnt)
            for ji, (src, _, _, w) in enumerate(jobs):
                if ji >= 2:
                    g.wait_ge(mm, cum_packs[ji - 1])
                g.dma_start(out=tbf[ji % 2][:, 0:w], in_=src).then_inc(lds[ji % 4], 16)
            for dj, (pk, sc, _, nw, ngrp, _) in enumerate(dq_jobs):
                if dj > 0:
                    g.wait_ge(dqv, plane_done[(dj - 1, 3)])
                g.dma_start(out=dq_pk[:, 0:nw], in_=pk).then_inc(dqls[0], 16)
                g.dma_start(out=dq_s16[0:ngrp, :], in_=sc).then_inc(dqls[1], 16)

        @block.tensor
        def _(t):
            t.wait_ge(gp, 4)
            for pi, (ji, pk, nb) in enumerate(packs):
                t.wait_ge(lds[ji % 4], 16 * (ji // 4 + 1))
                if pi >= 2:
                    t.wait_ge(ev, pi - 1)
                buf = tps[pi % 2][:].bitcast(BF16)
                src = tbf[ji % 2]
                for k in range(nb):
                    blk = pk * 4 + k
                    inst = t.transpose(buf[:, k * 128:(k + 1) * 128],
                                       src[:, blk * 128:(blk + 1) * 128], ident[:])
                inst.then_inc(mm, 1)
            dq_mm = 0
            for dpi, (dj, pl, b0, nbk) in enumerate(dq_packs):
                if pl == 0 and b0 == 0:
                    ngrp = dq_jobs[dj][4]
                    t.wait_ge(dqls[0], 16 * (dj + 1))
                    t.wait_ge(dqls[1], 16 * (dj + 1))
                    t.transpose(dq_sps_f[:].bitcast(F16)[:, 0:ngrp],
                                dq_s16[0:ngrp, :], ident16[0:ngrp, 0:ngrp]
                                ).then_inc(dqm, 1)
                    dq_mm += 1
                t.wait_ge(dqv, plane_done[(dj, pl)])
                if dpi >= 2:
                    t.wait_ge(dqe, dpi - 1)
                buf = dps[dpi % 2][:].bitcast(BF16)
                for k in range(nbk):
                    blk = b0 + k
                    inst = t.transpose(buf[:, k * 128:(k + 1) * 128],
                                       dq_w[:, blk * 128:(blk + 1) * 128], ident[:])
                inst.then_inc(dqm, 1)
                dq_mm += 1

        @block.vector
        def _(v):
            for pi, (ji, pk, nb) in enumerate(packs):
                v.wait_ge(mm, pi + 1)
                if pi >= 2:
                    v.wait_ge(sts[pi % 2], 16 * (pi // 2))
                v.tensor_copy(tev[pi % 2][:], tps[pi % 2][:].bitcast(BF16)[:, 0:512]
                              ).then_inc(ev, 1)
            dqm_seen = 0
            for dj, (pk, sc, _, nw, ngrp, _) in enumerate(dq_jobs):
                dqm_seen += 1
                v.wait_ge(dqm, dqm_seen)
                for gix in range(ngrp):
                    v.tensor_copy(dq_scf[:, gix * 32:(gix + 1) * 32],
                                  dq_sps_f[:].bitcast(F16)[:, gix:gix + 1]
                                  .to_broadcast([128, 32]))
                u16v = dq_pk[:, 0:nw].bitcast(U16)
                w2 = 2 * nw
                for pl in range(4):
                    v.tensor_scalar(dq_n[:, 0:w2], u16v, 4 * pl, 15,
                                    ALU.logical_shift_right, ALU.bitwise_and)
                    v.tensor_scalar(dq_a[:, 0:w2], dq_n[:, 0:w2], 7, 6,
                                    ALU.bitwise_and, ALU.logical_shift_left)
                    v.tensor_scalar(dq_A[:, 0:w2], dq_a[:, 0:w2], 0x3F80,
                                    None, ALU.add)
                    Abf = dq_A[:, 0:w2].bitcast(BF16)
                    v.tensor_scalar(dq_t[:, 0:w2], Abf, 2.0, -2.0, ALU.mult, ALU.add)
                    v.tensor_tensor(dq_Bv[:, 0:w2], dq_t[:, 0:w2], Abf, ALU.min)
                    v.tensor_scalar(dq_g[:, 0:w2], dq_n[:, 0:w2], 8, 12,
                                    ALU.bitwise_and, ALU.logical_shift_left)
                    v.tensor_scalar(dq_g[:, 0:w2], dq_g[:, 0:w2], 0x3F00,
                                    None, ALU.add)
                    v.tensor_tensor(dq_w[:, 0:w2], dq_Bv[:, 0:w2],
                                    dq_g[:, 0:w2].bitcast(BF16), ALU.mult)
                    vinst = v.tensor_tensor(dq_w[:, 0:w2], dq_w[:, 0:w2],
                                            dq_scf[:, 0:w2], ALU.mult)
                    vinst.then_inc(dqv, 1)
                    npk_plane = ((2 * nw) // 128 + 3) // 4
                    dqm_seen += npk_plane
                    if pl < 3 or dj < NDQ - 1:
                        v.wait_ge(dqm, dqm_seen)

        @block.scalar
        def _(sc):
            for pi, (ji, pk, nb) in enumerate(packs):
                sc.wait_ge(ev, pi + 1)
                dst, ob = jobs[ji][1], jobs[ji][2]
                sb_ap = tev[pi % 2][:].rearrange("p (k c) -> p k c", k=4)
                dr_ap = dst[pk * 4:pk * 4 + 4, :, ob * 128:(ob + 1) * 128
                            ].rearrange("k p c -> p k c")
                sc.dma_start(out=dr_ap, in_=sb_ap).then_inc(sts[pi % 2], 16)
            cum_par = {0: [0], 1: [0]}
            for i, (dj_, pl_, b0_, nbk_) in enumerate(dq_packs):
                cum_par[i % 2].append(cum_par[i % 2][-1] + nbk_)
            for dpi, (dj, pl, b0, nbk) in enumerate(dq_packs):
                sc.wait_ge(dqm, dq_pack_mm_target(dpi, dq_packs, dq_jobs))
                if dpi >= 2:
                    sc.wait_ge(dqss[dpi % 2], 16 * cum_par[dpi % 2][dpi // 2])
                sc.copy(dev[dpi % 2][:, 0:128 * nbk],
                        dps[dpi % 2][:].bitcast(BF16)[:, 0:128 * nbk]
                        ).then_inc(dqe, 1)
                sc.wait_ge(dqe, dpi + 1)
                dst = dq_jobs[dj][2]
                ocol = dq_jobs[dj][5]
                for k in range(nbk):
                    blk = b0 + k
                    dr = dst[4 * blk:4 * blk + 4, pl:pl + 125:4,
                             ocol:ocol + 128].rearrange("a b c -> (a b) c")
                    sc.dma_start(out=dr, in_=dev[dpi % 2][:, k * 128:(k + 1) * 128]
                                 ).then_inc(dqss[dpi % 2], 16)

    ctx.close()
    return nc


def dq_pack_mm_target(dpi, dq_packs, dq_jobs):
    tgt = 0
    for i, (dj, pl, b0, nbk) in enumerate(dq_packs):
        if pl == 0 and b0 == 0:
            tgt += 1
        tgt += 1
        if i == dpi:
            return tgt
    raise IndexError


if __name__ == "__main__":
    nc = build_P0()
    print("P0 built ok")



class FencedEngine:

    _CHAINED = {
        "activation", "sign", "square", "sqrt", "mul", "copy", "add",
        "memzero", "memset", "tensor_copy", "tensor_tensor", "tensor_add",
        "tensor_sub", "tensor_mul", "tensor_max", "tensor_scalar",
        "tensor_single_scalar", "tensor_scalar_add", "tensor_scalar_sub",
        "tensor_scalar_mul", "tensor_scalar_max", "tensor_scalar_min",
        "tensor_relu", "tensor_reduce", "reduce_max", "reduce_sum",
        "scalar_tensor_tensor", "tensor_tensor_scan", "select",
        "copy_predicated", "transpose", "bn_stats", "bn_aggr", "max",
        "max_index", "reciprocal", "stream_shuffle", "match_replace",
        "iota", "affine_select", "pool", "pool_avg", "pool_max",
    }

    def __init__(self, eng, sem, cnt):
        self._eng = eng
        self._sem = sem
        self._cnt = cnt

    def __getattr__(self, name):
        fn = getattr(self._eng, name)
        if name in self._CHAINED:
            def wrapped(*a, **k):
                if self._cnt[0] > 0:
                    self._eng.wait_ge(self._sem, self._cnt[0])
                inst = fn(*a, **k)
                inst.then_inc(self._sem, 1)
                self._cnt[0] += 1
                return inst
            return wrapped
        if name in ("dma_start", "indirect_dma_start", "dma_start_transpose"):
            def wrapped_dma(*a, **k):
                if self._cnt[0] > 0:
                    self._eng.wait_ge(self._sem, self._cnt[0])
                return fn(*a, **k)
            return wrapped_dma
        return fn


class Sched:
    def __init__(self, ctx):
        self.ctx = ctx
        self.ops = {k: [] for k in ("sync", "gpsimd", "tensor", "vector", "scalar")}
        self.count = {}
        self._autofence = {}
        for e in ("gpsimd", "vector", "scalar"):
            self._autofence[e] = (self.sem("af_" + e), [0])

    def sem(self, name):
        s = self.ctx.sem(name)
        self.count[name] = 0
        return s

    def at(self, sem):
        return (sem, self.count[sem.name])

    def add(self, engine, fn, waits=(), inc=None, n=1, is_dma=False):
        waits = [w for w in waits if w is not None and w[1] > 0]
        self.ops[engine].append((fn, waits, inc, n, is_dma))
        if inc is not None:
            self.count[inc.name] += n

    def make_fence(self, name="fence"):
        sem = self.sem(name)
        cnt = [0]
        def fence(eng, inst):
            inst.then_inc(sem, 1)
            cnt[0] += 1
            eng.wait_ge(sem, cnt[0])
        return fence

    def dma(self, engine, out, in_, waits=(), inc=None):
        def fn(eng):
            return eng.dma_start(out=out, in_=in_)
        self.add(engine, fn, waits, inc, 16, is_dma=True)

    def emit(self, eng_name, eng):
        raw = eng
        fenced = eng_name in self._autofence
        if fenced:
            afsem, afcnt = self._autofence[eng_name]
            eng = FencedEngine(raw, afsem, afcnt)
        for fn, waits, inc, n, is_dma in self.ops[eng_name]:
            for sem, val in waits:
                raw.wait_ge(sem, val)
            inst = fn(eng)
            if inc is not None:
                if is_dma:
                    inst.then_inc(inc, n)
                elif fenced:
                    if afcnt[0] > 0:
                        raw.wait_ge(afsem, afcnt[0])
                    raw.nop().then_inc(inc, n)
                else:
                    assert inst is not None, "op with inc returned no instruction"
                    inst.then_inc(inc, n)

    def build(self):
        with self.ctx.block() as block:
            @block.sync
            def _(e):
                self.emit("sync", e)

            @block.gpsimd
            def _(e):
                self.emit("gpsimd", e)

            @block.tensor
            def _(e):
                self.emit("tensor", e)

            @block.vector
            def _(e):
                self.emit("vector", e)

            @block.scalar
            def _(e):
                self.emit("scalar", e)


def emit_ln(sched, vsem, ssem, x_ap, h32_ap, scr, waits=(), apply_waits=()):
    def stats(v):
        v.reduce_sum(scr["cm"][:], x_ap, AX, negate=True)
        v.tensor_scalar_mul(scr["cm"][:], scr["cm"][:], 1.0 / D)
        v.tensor_tensor(scr["sq"][:], x_ap, x_ap, ALU.mult)
        v.reduce_sum(scr["cv"][:], scr["sq"][:], AX)
        v.tensor_scalar_mul(scr["cv"][:], scr["cv"][:], 1.0 / D)
        v.tensor_tensor(scr["ct"][:], scr["cm"][:], scr["cm"][:], ALU.mult)
        v.tensor_tensor(scr["cv"][:], scr["cv"][:], scr["ct"][:], ALU.subtract)
        v.tensor_scalar(scr["cv"][:], scr["cv"][:], EPS_LN, None, ALU.add)
        return v.reciprocal(scr["ct"][:], scr["cv"][:])
    sched.add("vector", stats, list(waits) + [sched.at(ssem)], vsem)

    def apply(sc):
        sc.activation(scr["cr"][:], scr["ct"][:], AF.Sqrt)
        sc.activation(scr["cb"][:], scr["cm"][:], AF.Copy, scale=scr["cr"][:])
        return sc.activation(h32_ap, x_ap, AF.Identity, scale=scr["cr"][:],
                             bias=scr["cb"][:])
    sched.add("scalar", apply, [sched.at(vsem)] + list(apply_waits), ssem)




def build_A(stage=9):
    ctx = Ctx()
    nc = ctx.nc
    d_x = nc.dram_tensor("x", [TPC, D], F32, kind="ExternalInput").ap()
    d_gw = nc.dram_tensor("gate_w", [E, D], F32, kind="ExternalInput").ap()
    d_mb = nc.dram_tensor("mbias", [1, E], F32, kind="ExternalInput").ap()
    d_h = nc.dram_tensor("h", [TPC, D], BF16, kind="ExternalOutput").ap()
    d_cnt = nc.dram_tensor("counts", [1, NTILE * E], F32, kind="ExternalOutput").ap()
    d_idx = nc.dram_tensor("idx6", [TPC, 8], F32, kind="ExternalOutput").ap()
    d_w6 = nc.dram_tensor("w6", [TPC, 8], F32, kind="ExternalOutput").ap()
    d_S = nc.dram_tensor("S", [TPC, E], F32, kind="ExternalOutput").ap()

    sch = Sched(ctx)
    lds = [sch.sem(f"ld{i}") for i in range(4)]
    vs = sch.sem("vs")
    ss = sch.sem("ss")
    ms = sch.sem("ms")
    gs = sch.sem("gs")
    sto = [sch.sem(f"st{i}") for i in range(2)]

    xb = [ctx.sbuf(f"xb{i}", [128, D], F32) for i in range(2)]
    h32 = ctx.sbuf("h32", [128, D], F32)
    hbf = ctx.sbuf("hbf", [128, D], BF16)
    hT = ctx.sbuf("hT", [128, D], F32)
    scr = {"sq": ctx.sbuf("sq", [128, D], F32)}
    for n in ("cm", "cv", "ct", "cr", "cb"):
        scr[n] = ctx.sbuf(n, [128, 1], F32)
    gw = ctx.sbuf("gw", [E, D], F32)
    gwT = ctx.sbuf("gwT", [128, 16 * E], F32)
    mb = ctx.sbuf("mb", [128, E], F32)
    ident = ctx.sbuf("identA", [128, 128], F32)
    iota32 = ctx.sbuf("iota32", [128, E], F32)
    iota_i = ctx.sbuf("iota_i", [128, E], I32)
    ones1 = ctx.sbuf("ones1", [128, 1], F32)
    lgT = ctx.sbuf("lgT", [E, 128], F32)
    lg = ctx.sbuf("lg", [128, E], F32)
    sa = ctx.sbuf("sa", [128, E], F32)
    se = ctx.sbuf("se", [128, E], F32)
    sr = ctx.sbuf("sr", [128, E], F32)
    sv = ctx.sbuf("sv", [128, E], F32)
    sbv = ctx.sbuf("sbv", [128, E], F32)
    Sm = ctx.sbuf("Sm", [128, E], F32)
    mr = ctx.sbuf("mr", [128, E], F32)
    tmp = ctx.sbuf("tmpA", [128, E], F32)
    mx8 = ctx.sbuf("mx8", [128, 8], F32)
    ix8 = ctx.sbuf("ix8", [128, 8], U32)
    ixf = ctx.sbuf("ixf", [128, 8], F32)
    wc = ctx.sbuf("wc", [128, 8], F32)
    w6b = ctx.sbuf("w6b", [128, 8], F32)
    wsum = ctx.sbuf("wsum", [128, 1], F32)
    winv = ctx.sbuf("winv", [128, 1], F32)
    cnt_acc = ctx.sbuf("cnt_acc", [1, NTILE * E], F32)
    dbg = ctx.sbuf("dbg", [128, 8], F32)
    ps_t = [ctx.psum(f"ps_t{i}", [128, 512], F32) for i in range(2)]
    ps_g = ctx.psum("ps_g", [128, 512], F32)
    ps_c = ctx.psum("ps_c", [128, 512], F32)

    def g_ident(g):
        g.memset(ident[:], 1.0).then_inc(gs, 1)
        sch.count["gs"] += 0
        return None
    sch.add("gpsimd", lambda g: g.memset(ident[:], 1.0), inc=gs)
    sch.add("gpsimd", lambda g: g.affine_select(
        out=ident[:], in_=ident[:], pattern=[[1, 128]],
        compare_op=ALU.is_equal, fill=0.0, base=0, channel_multiplier=-1),
        waits=[sch.at(gs)], inc=gs)
    sch.add("gpsimd", lambda g: g.iota(iota_i[:], [[0, 1], [1, E]], base=0,
                                       channel_multiplier=0), inc=gs)
    m_gp = sch.at(gs)
    sch.add("vector", lambda v: v.tensor_copy(iota32[:], iota_i[:]),
            waits=[m_gp], inc=vs)
    sch.add("vector", lambda v: v.memset(ones1[:], 1.0), inc=vs)
    sch.add("vector", lambda v: v.memset(wc[:], 0.0), inc=vs)
    sch.dma("sync", gw[:], d_gw, inc=lds[3])
    sch.dma("sync", mb[:], d_mb[0:1, :].to_broadcast([128, E]), inc=lds[3])
    m_gwload = sch.at(lds[3])
    for k in range(16):
        sch.add("tensor", (lambda kk: lambda t: t.transpose(
            ps_g[:, 0:E], gw[:, kk * 128:(kk + 1) * 128], ident[0:E, 0:E]))(k),
            waits=[m_gwload, m_gp, sch.at(vs)], inc=ms)
        sch.add("vector", (lambda kk: lambda v: v.tensor_copy(
            gwT[:, E * kk:E * (kk + 1)], ps_g[:, 0:E]))(k),
            waits=[sch.at(ms)], inc=vs)
    m_gwT = sch.at(vs)

    m_apply = [None] * NTILE
    m_store = [None, None]
    for t in range(NTILE):
        w = [m_apply[t - 2]] if t >= 2 else []
        sch.dma("sync", xb[t % 2][:], d_x[t * 128:(t + 1) * 128, :],
                waits=w, inc=lds[t % 2])
        m_load = sch.at(lds[t % 2])
        emit_ln(sch, vs, ss, xb[t % 2][:], h32[:], scr,
                waits=[m_load], apply_waits=[sch.at(ms)])
        m_apply[t] = sch.at(ss)
        if stage == 0:
            def dumpcols(v):
                for ci, b_ in enumerate((scr["cm"], scr["cv"], scr["ct"],
                                         scr["cr"], scr["cb"])):
                    inst = v.tensor_copy(dbg[:, ci:ci + 1], b_[:])
                return inst
            sch.add("vector", dumpcols, waits=[m_apply[t]], inc=vs)
            sch.dma("scalar", d_idx[t * 128:(t + 1) * 128, :], dbg[:],
                    waits=[sch.at(vs)], inc=sto[t % 2])
            m_store[t % 2] = sch.at(sto[t % 2])
        sch.add("scalar", lambda sc: sc.copy(hbf[:], h32[:]),
                waits=[sch.at(sto[0]), sch.at(sto[1])], inc=ss)
        sch.dma("scalar", d_h[t * 128:(t + 1) * 128, :], hbf[:],
                waits=[sch.at(ss)], inc=sto[t % 2])
        if stage < 2:
            continue
        for pk in range(4):
            def mk_pack(pp, tt):
                def f(t_):
                    for k in range(4):
                        blk = pp * 4 + k
                        inst = t_.transpose(ps_t[pp % 2][:, k * 128:(k + 1) * 128],
                                            h32[:, blk * 128:(blk + 1) * 128],
                                            ident[:])
                    return inst
                return f
            sch.add("tensor", mk_pack(pk, t), waits=[m_apply[t], sch.at(vs)],
                    inc=ms)
            sch.add("vector", (lambda pp: lambda v: v.tensor_copy(
                hT[:, pp * 512:(pp + 1) * 512], ps_t[pp % 2][:]))(pk),
                waits=[sch.at(ms)], inc=vs)
        m_hT = sch.at(vs)

        def gate_mm(t_):
            for k in range(16):
                inst = t_.matmul(ps_g[0:E, 0:128], gwT[:, E * k:E * (k + 1)],
                                 hT[:, k * 128:(k + 1) * 128],
                                 start=(k == 0), stop=(k == 15))
            return inst
        sch.add("tensor", gate_mm, waits=[m_hT, m_gwT, sch.at(vs)], inc=ms)
        sch.add("vector", lambda v: v.tensor_copy(lgT[:, 0:128], ps_g[0:E, 0:128]),
                waits=[sch.at(ms)], inc=vs)
        sch.add("tensor", lambda t_: t_.transpose(ps_c[:, 0:E], lgT[:, 0:128],
                                                  ident[0:E, 0:E]),
                waits=[sch.at(vs)], inc=ms)
        sch.add("vector", lambda v: v.tensor_copy(lg[:], ps_c[:, 0:E]),
                waits=[sch.at(ms)], inc=vs)
        if stage < 3:
            sch.dma("scalar", d_S[t * 128:(t + 1) * 128, :], lg[:],
                    waits=[sch.at(vs)], inc=sto[t % 2])
            m_store[t % 2] = sch.at(sto[t % 2])
            continue
        def sp1(sc):
            sc.activation(sa[:], lg[:], AF.Abs)
            sc.activation(se[:], sa[:], AF.Exp, scale=-1.0)
            return sc.activation(se[:], se[:], AF.Ln, bias=1.0)
        sch.add("scalar", sp1, waits=[sch.at(vs)], inc=ss)
        def sp2(v):
            v.tensor_scalar_max(sr[:], lg[:], 0.0)
            return v.tensor_tensor(sv[:], sr[:], se[:], ALU.add)
        sch.add("vector", sp2, waits=[sch.at(ss)], inc=vs)
        sch.add("scalar", lambda sc: sc.activation(sa[:], sv[:], AF.Sqrt),
                waits=[sch.at(vs)], inc=ss)
        if stage < 4:
            sch.dma("scalar", d_S[t * 128:(t + 1) * 128, :], sv[:],
                    waits=[sch.at(vs)], inc=sto[t % 2])
            m_store[t % 2] = sch.at(sto[t % 2])
            continue
        def route(v):
            v.tensor_tensor(sbv[:], sa[:], mb[:], ALU.add)
            v.max(mx8[:], sbv[:])
            v.max_index(ix8[:], mx8[:], sbv[:])
            v.tensor_copy(ixf[:], ix8[:])
            for r in range(K_TOP):
                v.tensor_scalar(mr[:], iota32[:], ixf[:, r:r + 1], None,
                                ALU.is_equal)
                if r == 0:
                    v.tensor_copy(Sm[:], mr[:])
                else:
                    v.tensor_tensor(Sm[:], Sm[:], mr[:], ALU.add)
                v.tensor_tensor(tmp[:], mr[:], sa[:], ALU.mult)
                v.reduce_sum(wc[:, r:r + 1], tmp[:], AX)
            v.reduce_sum(wsum[:], wc[:, 0:K_TOP], AX)
            v.reciprocal(winv[:], wsum[:])
            return v.tensor_scalar(w6b[:], wc[:], winv[:, 0:1], MOE_SCALE,
                                   ALU.mult, ALU.mult)
        sch.add("vector", route, waits=[sch.at(ss), m_store[1] or m_store[0] or None,
                                        m_store[0]], inc=vs)
        m_route = sch.at(vs)
        sch.add("tensor", lambda t_: t_.matmul(ps_c[0:1, 128:128 + E], ones1[:],
                                               Sm[:], start=True, stop=True),
                waits=[m_route], inc=ms)
        sch.add("vector", (lambda tt: lambda v: v.tensor_copy(
            cnt_acc[0:1, E * tt:E * (tt + 1)], ps_c[0:1, 128:128 + E]))(t),
            waits=[sch.at(ms)], inc=vs)
        sch.dma("scalar", d_idx[t * 128:(t + 1) * 128, :], ixf[:],
                waits=[m_route], inc=sto[t % 2])
        sch.dma("scalar", d_w6[t * 128:(t + 1) * 128, :], w6b[:],
                waits=[m_route], inc=sto[t % 2])
        sch.dma("scalar", d_S[t * 128:(t + 1) * 128, :], Sm[:],
                waits=[m_route], inc=sto[t % 2])
        m_store[t % 2] = sch.at(sto[t % 2])
    sch.dma("scalar", d_cnt, cnt_acc[:], waits=[sch.at(vs)], inc=sto[0])
    sch.add("sync", lambda e: None,
            waits=[sch.at(sto[0]), sch.at(sto[1])], inc=None)
    sch.build()
    ctx.close()
    return nc


SSROWS2 = 24704
TRASH2 = 24600


def build_AP():
    ctx = Ctx()
    nc = ctx.nc
    d_h = nc.dram_tensor("h", [TPC, D], BF16, kind="ExternalInput").ap()
    d_cnt = nc.dram_tensor("counts_g", [NCHUNK, E], F32, kind="ExternalInput").ap()
    d_S = nc.dram_tensor("S", [TPC, E], F32, kind="ExternalInput").ap()
    d_idx = nc.dram_tensor("idx6", [TPC, 8], F32, kind="ExternalInput").ap()
    d_w6 = nc.dram_tensor("w6", [TPC, 8], F32, kind="ExternalInput").ap()
    d_sgT = nc.dram_tensor("sgT", [16, 128, 1024], BF16, kind="ExternalInput").ap()
    d_suT = nc.dram_tensor("suT", [16, 128, 1024], BF16, kind="ExternalInput").ap()
    d_sdT = nc.dram_tensor("sdT", [8, 128, 2048], BF16, kind="ExternalInput").ap()
    d_sel = nc.dram_tensor("selmat", [64, NTILE], F32, kind="ExternalInput").ap()
    d_tok = nc.dram_tensor("tokcol", [TPC, 1], F32, kind="ExternalInput").ap()
    d_ss = nc.dram_tensor("sspart", [SSROWS2, 2], F32, kind="ExternalOutput").ap()
    d_ysh = nc.dram_tensor("ysh", [TPC, D], BF16, kind="ExternalOutput").ap()

    sch = Sched(ctx)
    lds = [sch.sem(f"ld{i}") for i in range(4)]
    vs = sch.sem("vs")
    ss = sch.sem("ss")
    ms = sch.sem("ms")
    gs = sch.sem("gs")
    sco = [sch.sem(f"sc{i}") for i in range(2)]
    sto = [sch.sem(f"st{i}") for i in range(2)]
    zs = sch.sem("zs")

    ident = ctx.sbuf("ident", [128, 128], F32)
    identb = ctx.sbuf("identb", [128, 128], BF16)
    T128u = ctx.sbuf("T128u", [128, 128], F32)
    T64u = ctx.sbuf("T64u", [64, 64], F32)
    onesr = ctx.sbuf("onesr", [1, 128], F32)
    iotac = ctx.sbuf("iotac", [128, 1], I32)
    iotacf = ctx.sbuf("iotacf", [128, 1], F32)
    iota32 = ctx.sbuf("iota32", [128, E], I32)
    iota32f = ctx.sbuf("iota32f", [128, E], F32)
    cnts = ctx.sbuf("cnts", [64, E], F32)
    bases = ctx.sbuf("bases", [64, E], F32)
    selm = ctx.sbuf("selm", [64, NTILE], F32)
    baser = ctx.sbuf("baser", [1, E], F32)
    Sb = [ctx.sbuf(f"Sb{i}", [128, E], F32) for i in range(2)]
    ixb = [ctx.sbuf(f"ixb{i}", [128, 8], F32) for i in range(2)]
    w6b = [ctx.sbuf(f"w6b{i}", [128, 8], F32) for i in range(2)]
    Pf = ctx.sbuf("Pf", [128, E], F32)
    mr = ctx.sbuf("mr", [128, E], F32)
    tmp = ctx.sbuf("tmp", [128, E], F32)
    posr = ctx.sbuf("posr", [128, 1], F32)
    vald = ctx.sbuf("vald", [128, 1], F32)
    acol = ctx.sbuf("acol", [128, 1], F32)
    tokc = [ctx.sbuf(f"tokc{i}", [128, 1], F32) for i in range(2)]
    pairs = [ctx.sbuf(f"pairs{i}", [128, 12], F32) for i in range(2)]
    dsts = [ctx.sbuf(f"dsts{i}", [128, 8], I32) for i in range(2)]
    dstf = ctx.sbuf("dstf", [128, 8], F32)
    zeros = ctx.sbuf("zeros", [128, 386], F32)
    ps_a = ctx.psum("ps_a", [128, 512], F32)
    ps_b = ctx.psum("ps_b", [128, 512], F32)

    hb = [ctx.sbuf(f"hb{i}", [128, D], BF16) for i in range(2)]
    hTs = ctx.sbuf("hTs", [128, 16 * 512], BF16)
    sgall = ctx.sbuf("sgall", [128, 2 * 16 * 1024], BF16)
    sdall = ctx.sbuf("sdall", [128, 8 * 2048], BF16)
    sgs = ctx.sbuf("sgs", [128, 512], F32)
    sis = ctx.sbuf("sis", [128, 512], F32)
    siT = ctx.sbuf("siT", [128, 8 * 512], BF16)
    yTt = ctx.sbuf("yTt", [128, 512], BF16)
    yshb = ctx.sbuf("yshb", [128, 4 * D], BF16)
    ps_t = [ctx.psum(f"ps_t{i}", [128, 512], F32) for i in range(2)]
    ps_g = ctx.psum("ps_g", [128, 512], F32)
    ps_u = ctx.psum("ps_u", [128, 512], F32)
    ps_d = ctx.psum("ps_d", [128, 512], F32)

    def aff(buf, n, op, base):
        def f(g):
            return g.affine_select(out=buf[:], in_=buf[:], pattern=[[1, n]],
                                   compare_op=op, fill=0.0, base=base,
                                   channel_multiplier=-1)
        return f
    for buf, n, op, base in ((ident, 128, ALU.is_equal, 0),
                             (identb, 128, ALU.is_equal, 0),
                             (T128u, 128, ALU.is_ge, -1),
                             (T64u, 64, ALU.is_ge, -1)):
        sch.add("gpsimd", (lambda b: lambda g: g.memset(b[:], 1.0))(buf), inc=gs)
        sch.add("gpsimd", aff(buf, n, op, base), waits=[sch.at(gs)], inc=gs)
    sch.add("gpsimd", lambda g: g.iota(iotac[:], [[0, 1]], base=0,
                                       channel_multiplier=1), inc=gs)
    sch.add("gpsimd", lambda g: g.iota(iota32[:], [[1, E]], base=0,
                                       channel_multiplier=0), inc=gs)
    m_gp = sch.at(gs)
    sch.add("vector", lambda v: v.tensor_copy(iotacf[:], iotac[:]),
            waits=[m_gp], inc=vs)
    sch.add("vector", lambda v: v.tensor_copy(iota32f[:], iota32[:]), inc=vs)
    sch.add("vector", lambda v: v.memset(onesr[:], 1.0), inc=vs)
    sch.add("vector", lambda v: v.memset(zeros[:], 0.0), inc=vs)
    m_const = sch.at(vs)
    sch.dma("scalar", d_ss.rearrange("(a b) c -> a (b c)", a=128), zeros[:],
            waits=[m_const], inc=zs)
    m_zdone = sch.at(zs)

    sch.dma("sync", cnts[:], d_cnt, inc=lds[3])
    sch.dma("sync", selm[:], d_sel, inc=lds[3])
    sch.add("tensor", lambda t: t.matmul(ps_a[0:64, 0:E], T64u[:], cnts[:],
                                         start=True, stop=True),
            waits=[sch.at(lds[3]), m_gp], inc=ms)
    sch.add("vector", lambda v: v.tensor_copy(bases[:], ps_a[0:64, 0:E]),
            waits=[sch.at(ms)], inc=vs)
    m_bases = sch.at(vs)

    m_pairs = [None] * NTILE
    m_scat = {0: None, 1: None}
    for t in range(NTILE):
        bb = t % 2
        sch.dma("sync", Sb[bb][:], d_S[t * 128:(t + 1) * 128, :],
                waits=[m_pairs[t - 2]] if t >= 2 else [], inc=lds[bb])
        sch.dma("sync", ixb[bb][:], d_idx[t * 128:(t + 1) * 128, :], inc=lds[bb])
        sch.dma("sync", w6b[bb][:], d_w6[t * 128:(t + 1) * 128, :], inc=lds[bb])
        sch.dma("sync", tokc[bb][:], d_tok[t * 128:(t + 1) * 128, :], inc=lds[bb])
        m_load = sch.at(lds[bb])
        sch.add("tensor", (lambda tt: lambda t_: t_.matmul(
            ps_a[0:1, 64:64 + E], selm[:, tt:tt + 1], bases[:],
            start=True, stop=True))(t), waits=[m_bases], inc=ms)
        sch.add("vector", lambda v: v.tensor_copy(baser[:], ps_a[0:1, 64:64 + E]),
                waits=[sch.at(ms)], inc=vs)
        def mk_prefix(b_):
            def f(t_):
                t_.matmul(ps_b[:, 0:E], T128u[:], Sb[b_][:], start=True,
                          stop=False)
                return t_.matmul(ps_b[:, 0:E], onesr[:], baser[:], start=False,
                                 stop=True)
            return f
        sch.add("tensor", mk_prefix(bb), waits=[m_load, sch.at(vs)], inc=ms)
        sch.add("vector", lambda v: v.tensor_copy(Pf[:], ps_b[:, 0:E]),
                waits=[sch.at(ms)], inc=vs)
        def mk_route(b_):
            def f(v):
                for r in range(K_TOP):
                    v.tensor_scalar(mr[:], iota32f[:], ixb[b_][:, r:r + 1], None,
                                    ALU.is_equal)
                    v.tensor_tensor(tmp[:], mr[:], Pf[:], ALU.mult)
                    v.reduce_sum(posr[:], tmp[:], AX)
                    v.tensor_scalar(vald[:], posr[:], float(CAP), None, ALU.is_lt)
                    v.tensor_tensor(pairs[b_][:, 2 * r + 1:2 * r + 2],
                                    w6b[b_][:, r:r + 1], vald[:], ALU.mult)
                    v.tensor_copy(pairs[b_][:, 2 * r:2 * r + 1], tokc[b_][:])
                    v.tensor_scalar(acol[:], ixb[b_][:, r:r + 1], float(CAP),
                                    None, ALU.mult)
                    v.tensor_tensor(acol[:], acol[:], posr[:], ALU.add)
                    v.scalar_tensor_tensor(acol[:], acol[:], -float(TRASH2),
                                           vald[:], ALU.add, ALU.mult)
                    v.tensor_scalar(dstf[:, r:r + 1], acol[:], float(TRASH2),
                                    None, ALU.add)
                return v.tensor_copy(dsts[b_][:, 0:K_TOP], dstf[:, 0:K_TOP])
            return f
        sch.add("vector", mk_route(bb), waits=[m_scat[bb], m_load], inc=vs)
        m_pairs[t] = sch.at(vs)
        def mk_scat(b_, r):
            def f(g):
                return g.indirect_dma_start(
                    out=d_ss, out_offset=bass.IndirectOffsetOnAxis(
                        dsts[b_][:, r:r + 1], 0),
                    in_=pairs[b_][:, 2 * r:2 * r + 2], in_offset=None)
            return f
        for r in range(K_TOP):
            sch.add("gpsimd", mk_scat(bb, r),
                    waits=[m_pairs[t], m_zdone], inc=sco[bb], is_dma=True, n=16)
        m_scat[bb] = sch.at(sco[bb])

    sch.dma("sync", sgall[:, 0:16384].rearrange("p (k c) -> p k c", k=16),
            d_sgT.rearrange("k p c -> p k c"), inc=lds[3])
    sch.dma("sync", sgall[:, 16384:32768].rearrange("p (k c) -> p k c", k=16),
            d_suT.rearrange("k p c -> p k c"), inc=lds[3])
    sch.dma("sync", sdall[:].rearrange("p (k c) -> p k c", k=8),
            d_sdT.rearrange("k p c -> p k c"), inc=lds[3])
    m_w = sch.at(lds[3])
    m_store = {0: None, 1: None}
    for b2 in range(2):
        m_hload = [None] * 4
        for tt in range(4):
            ti = b2 * 4 + tt
            sch.dma("sync", hb[tt % 2][:], d_h[ti * 128:(ti + 1) * 128, :],
                    waits=[sch.at(ms)] if tt >= 2 or b2 > 0 else [],
                    inc=lds[tt % 2])
            m_hload[tt] = sch.at(lds[tt % 2])
            for pk in range(4):
                def mk_tp(tt_, pp, b_):
                    def f(t_):
                        for k in range(4):
                            blk = pp * 4 + k
                            inst = t_.transpose(
                                ps_t[pp % 2][:].bitcast(BF16)[:, k * 128:(k + 1) * 128],
                                hb[b_][:, blk * 128:(blk + 1) * 128], identb[:])
                        return inst
                    return f
                sch.add("tensor", mk_tp(tt, pk, tt % 2),
                        waits=[m_hload[tt], m_gp, sch.at(vs)], inc=ms)
                def mk_ev(tt_, pp):
                    def f(v):
                        return v.tensor_copy(
                            hTs[:].rearrange("p (k c) -> p k c", k=16)
                            [:, pp * 4:(pp + 1) * 4, tt_ * 128:(tt_ + 1) * 128],
                            ps_t[pp % 2][:].bitcast(BF16)[:, 0:512]
                            .rearrange("p (k c) -> p k c", k=4))
                    return f
                sch.add("vector", mk_ev(tt, pk), waits=[sch.at(ms)], inc=vs)
        m_hT = sch.at(vs)
        for ic in range(8):
            def mk_gu(ic_, ps, coff):
                def f(t_):
                    for k in range(16):
                        inst = t_.matmul(
                            ps[:, 0:512],
                            sgall[:, coff + k * 1024 + ic_ * 128:
                                  coff + k * 1024 + ic_ * 128 + 128],
                            hTs[:, k * 512:(k + 1) * 512],
                            start=(k == 0), stop=(k == 15))
                    return inst
                return f
            sch.add("tensor", mk_gu(ic, ps_g, 0), waits=[m_hT, m_w, sch.at(ss),
                                                         sch.at(vs)], inc=ms)
            sch.add("tensor", mk_gu(ic, ps_u, 16384), inc=ms)
            sch.add("scalar", lambda sc: sc.activation(sgs[:], ps_g[:, 0:512],
                                                       AF.Silu),
                    waits=[sch.at(ms)], inc=ss)
            def mk_si(ic_):
                def f(v):
                    v.tensor_tensor(sis[:], sgs[:], ps_u[:, 0:512], ALU.mult)
                    return v.tensor_scalar(siT[:, ic_ * 512:(ic_ + 1) * 512],
                                           sis[:], SWIGLU_LIMIT, -SWIGLU_LIMIT,
                                           ALU.min, ALU.max)
                return f
            sch.add("vector", mk_si(ic), waits=[sch.at(ss)], inc=vs)
        m_si = sch.at(vs)
        for ob in range(16):
            def mk_dn(ob_):
                def f(t_):
                    for ib in range(8):
                        inst = t_.matmul(
                            ps_d[:, 0:512],
                            sdall[:, ib * 2048 + ob_ * 128:ib * 2048 + ob_ * 128 + 128],
                            siT[:, ib * 512:(ib + 1) * 512],
                            start=(ib == 0), stop=(ib == 7))
                    return inst
                return f
            sch.add("tensor", mk_dn(ob), waits=[m_si, m_w, sch.at(vs)], inc=ms)
            sch.add("vector", lambda v: v.tensor_copy(yTt[:], ps_d[:, 0:512]),
                    waits=[sch.at(ms)], inc=vs)
            def mk_tb(ob_):
                def f(t_):
                    for k in range(4):
                        inst = t_.transpose(
                            ps_t[0][:].bitcast(BF16)[:, k * 128:(k + 1) * 128],
                            yTt[:, k * 128:(k + 1) * 128], identb[:])
                    return inst
                return f
            sch.add("tensor", mk_tb(ob), waits=[sch.at(vs)], inc=ms)
            def mk_evy(ob_):
                def f(v):
                    return v.tensor_copy(
                        yshb[:].rearrange("p (t c) -> p t c", t=4)
                        [:, :, ob_ * 128:(ob_ + 1) * 128],
                        ps_t[0][:].bitcast(BF16)[:, 0:512]
                        .rearrange("p (k c) -> p k c", k=4))
                return f
            sch.add("vector", mk_evy(ob), waits=[sch.at(ms)], inc=vs)
        m_ysh = sch.at(vs)
        for tt in range(4):
            ti = b2 * 4 + tt
            sch.dma("scalar", d_ysh[ti * 128:(ti + 1) * 128, :],
                    yshb[:, tt * D:(tt + 1) * D],
                    waits=[m_ysh, m_store[b2]], inc=sto[b2])
        m_store[b2] = sch.at(sto[b2])

    sch.add("sync", lambda e: None,
            waits=[sch.at(sto[0]), sch.at(sto[1]), sch.at(sco[0]),
                   sch.at(sco[1]), sch.at(zs)], inc=None)
    sch.build()
    ctx.close()
    return nc


def build_BX():
    ctx = Ctx()
    nc = ctx.nc
    d_hf = nc.dram_tensor("hfull", [T + 8, D], BF16, kind="ExternalInput").ap()
    d_ssp = [nc.dram_tensor(f"ss{i}", [SSROWS2, 2], F32, kind="ExternalInput").ap()
             for i in range(NCORES)]
    d_wg = nc.dram_tensor("wgT", [16, 128, 4096], BF16, kind="ExternalInput").ap()
    d_wu = nc.dram_tensor("wuT", [16, 128, 4096], BF16, kind="ExternalInput").ap()
    d_wd = nc.dram_tensor("wdT", [8, 128, 8192], BF16, kind="ExternalInput").ap()
    d_rp = nc.dram_tensor("rp", [T + 8, D], F32, kind="ExternalOutput").ap()
    ssd = nc.dram_tensor("ssum", [SSROWS2, 2], F32, kind="Internal").ap()

    sch = Sched(ctx)
    lds = [sch.sem(f"ld{i}") for i in range(4)]
    vs = sch.sem("vs")
    ss_ = sch.sem("ss")
    ms = sch.sem("ms")
    gs = sch.sem("gs")
    sca = sch.sem("sca")
    sso = sch.sem("sso")
    gth = [sch.sem(f"gth{i}") for i in range(2)]
    identb = ctx.sbuf("identb", [128, 128], BF16)
    sspa = ctx.sbuf("sspa", [128, 386], F32)
    sspb = ctx.sbuf("sspb", [128, 386], F32)
    sl = ctx.sbuf("sl", [128, 12], F32)
    tokf = ctx.sbuf("tokf", [128, 6], F32)
    toki = ctx.sbuf("toki", [128, 6], I32)
    wsc = ctx.sbuf("wsc", [128, 6], F32)
    mcol = ctx.sbuf("mcol", [128, 6], F32)
    xb = ctx.sbuf("xb", [128, D], BF16)
    xbT = ctx.sbuf("xbT", [128, 16 * 768], BF16)
    wgb = ctx.sbuf("wgb", [128, 16 * 1024], BF16)
    wub = ctx.sbuf("wub", [128, 16 * 1024], BF16)
    wdb = ctx.sbuf("wdb", [128, 8 * 2048], BF16)
    sgsc = ctx.sbuf("sgsc", [128, 384], F32)
    sic = ctx.sbuf("sic", [128, 384], F32)
    siT = ctx.sbuf("siT", [128, 8 * 768], BF16)
    rbuf = ctx.sbuf("rbuf", [128, D], F32)
    ps_t = [ctx.psum(f"ps_t{i}", [128, 512], F32) for i in range(2)]
    ps_g = ctx.psum("ps_g", [128, 512], F32)
    ps_u = ctx.psum("ps_u", [128, 512], F32)
    ps_d = ctx.psum("ps_d", [128, 512], F32)

    sch.add("gpsimd", lambda g: g.memset(identb[:], 1.0), inc=gs)
    sch.add("gpsimd", lambda g: g.affine_select(
        out=identb[:], in_=identb[:], pattern=[[1, 128]],
        compare_op=ALU.is_equal, fill=0.0, base=0, channel_multiplier=-1),
        waits=[sch.at(gs)], inc=gs)
    m_gp = sch.at(gs)

    v_ss = d_ssp[0].rearrange("(a b) c -> a (b c)", a=128)
    sch.dma("sync", sspa[:], v_ss, inc=lds[3])
    for i in range(1, NCORES):
        sch.dma("sync", sspb[:], d_ssp[i].rearrange("(a b) c -> a (b c)", a=128),
                waits=[sch.at(vs)], inc=lds[3])
        sch.add("vector", lambda v: v.tensor_tensor(sspa[:], sspa[:], sspb[:],
                                                    ALU.add),
                waits=[sch.at(lds[3])], inc=vs)
    sch.dma("scalar", ssd.rearrange("(a b) c -> a (b c)", a=128), sspa[:],
            waits=[sch.at(vs)], inc=sso)
    m_ssum = sch.at(sso)

    m_sc_prev = None
    for el in range(EPC):
        sch.dma("sync", wgb[:].rearrange("p (k c) -> p k c", k=16),
                d_wg[:, :, 1024 * el:1024 * (el + 1)].rearrange("k p c -> p k c"),
                waits=[sch.at(ms)], inc=lds[0])
        sch.dma("sync", wub[:].rearrange("p (k c) -> p k c", k=16),
                d_wu[:, :, 1024 * el:1024 * (el + 1)].rearrange("k p c -> p k c"),
                inc=lds[1])
        sch.dma("sync", wdb[:].rearrange("p (k c) -> p k c", k=8),
                d_wd[:, :, 2048 * el:2048 * (el + 1)].rearrange("k p c -> p k c"),
                inc=lds[2])
        m_wload = [sch.at(lds[0]), sch.at(lds[1]), sch.at(lds[2])]
        e0 = CAP * el
        sch.dma("sync", sl[:].rearrange("p (k c) -> p k c", k=6),
                ssd[e0:e0 + CAP, :].rearrange("(k p) c -> p k c", k=6),
                waits=[m_ssum], inc=lds[3])
        def mk_cols(v):
            v.tensor_scalar(mcol[:], sl[:, 0:12:2], 0.0, None, ALU.is_equal)
            v.tensor_scalar(tokf[:], sl[:, 0:12:2], -1.0, None, ALU.add)
            v.scalar_tensor_tensor(tokf[:], mcol[:], float(T + 1), tokf[:],
                                   ALU.mult, ALU.add)
            v.tensor_copy(toki[:], tokf[:])
            return v.tensor_copy(wsc[:], sl[:, 1:12:2])
        sch.add("vector", mk_cols, waits=[sch.at(lds[3]), sch.at(sca)], inc=vs)
        m_cols = sch.at(vs)
        for blk in range(6):
            def mk_gath(b_):
                def f(g):
                    return g.indirect_dma_start(
                        out=xb[:], out_offset=None, in_=d_hf,
                        in_offset=bass.IndirectOffsetOnAxis(toki[:, b_:b_ + 1], 0))
                return f
            sch.add("gpsimd", mk_gath(blk), waits=[m_cols, sch.at(ms)],
                    inc=gth[blk % 2], is_dma=True, n=16)
            m_g = sch.at(gth[blk % 2])
            for pk in range(4):
                def mk_tp(pp, b_):
                    def f(t_):
                        for k in range(4):
                            kk = pp * 4 + k
                            inst = t_.transpose(
                                ps_t[pp % 2][:].bitcast(BF16)[:, k * 128:(k + 1) * 128],
                                xb[:, kk * 128:(kk + 1) * 128], identb[:])
                        return inst
                    return f
                sch.add("tensor", mk_tp(pk, blk), waits=[m_g, m_gp, sch.at(vs)],
                        inc=ms)
                def mk_ev(pp, b_):
                    def f(v):
                        return v.tensor_copy(
                            xbT[:].rearrange("p (k c) -> p k c", k=16)
                            [:, pp * 4:(pp + 1) * 4, b_ * 128:(b_ + 1) * 128],
                            ps_t[pp % 2][:].bitcast(BF16)[:, 0:512]
                            .rearrange("p (k c) -> p k c", k=4))
                    return f
                sch.add("vector", mk_ev(pk, blk), waits=[sch.at(ms)], inc=vs)
        m_xbT = sch.at(vs)
        for ic in range(8):
            for hh in range(2):
                s0 = 384 * hh
                def mk_gu(ic_, s0_, ps, wb):
                    def f(t_):
                        for k in range(16):
                            inst = t_.matmul(
                                ps[:, 0:384],
                                wb[:, k * 1024 + ic_ * 128:k * 1024 + ic_ * 128 + 128],
                                xbT[:, k * 768 + s0_:k * 768 + s0_ + 384],
                                start=(k == 0), stop=(k == 15))
                        return inst
                    return f
                sch.add("tensor", mk_gu(ic, s0, ps_g, wgb),
                        waits=[m_xbT] + m_wload + [sch.at(ss_), sch.at(vs)], inc=ms)
                sch.add("tensor", mk_gu(ic, s0, ps_u, wub), inc=ms)
                sch.add("scalar", lambda sc: sc.activation(
                    sgsc[:], ps_g[:, 0:384], AF.Silu), waits=[sch.at(ms)], inc=ss_)
                def mk_si(ic_, s0_):
                    def f(v):
                        v.tensor_tensor(sic[:], sgsc[:], ps_u[:, 0:384], ALU.mult)
                        return v.tensor_copy(
                            siT[:, ic_ * 768 + s0_:ic_ * 768 + s0_ + 384], sic[:])
                    return f
                sch.add("vector", mk_si(ic, s0), waits=[sch.at(ss_)], inc=vs)
        m_si = sch.at(vs)
        for blk in range(6):
            for oq in range(4):
                def mk_dn(b_, oq_):
                    def f(t_):
                        for ib in range(8):
                            inst = t_.matmul(
                                ps_d[:, 0:512],
                                siT[:, ib * 768 + b_ * 128:ib * 768 + (b_ + 1) * 128],
                                wdb[:, ib * 2048 + oq_ * 512:ib * 2048 + (oq_ + 1) * 512],
                                start=(ib == 0), stop=(ib == 7))
                        return inst
                    return f
                sch.add("tensor", mk_dn(blk, oq), waits=[m_si, sch.at(vs)], inc=ms)
                def mk_evr(b_, oq_):
                    def f(v):
                        return v.tensor_scalar(rbuf[:, oq_ * 512:(oq_ + 1) * 512],
                                               ps_d[:, 0:512], wsc[:, b_:b_ + 1],
                                               None, ALU.mult)
                    return f
                sch.add("vector", mk_evr(blk, oq),
                        waits=[sch.at(ms), sch.at(sca)], inc=vs)
            def mk_sca(b_):
                def f(g):
                    return g.indirect_dma_start(
                        out=d_rp, out_offset=bass.IndirectOffsetOnAxis(
                            toki[:, b_:b_ + 1], 0),
                        in_=rbuf[:], in_offset=None, compute_op=ALU.add)
                return f
            sch.add("gpsimd", mk_sca(blk), waits=[sch.at(vs), sch.at(sca)],
                    inc=sca, is_dma=True, n=16)
        m_sc_prev = sch.at(sca)
    sch.add("sync", lambda e: None, waits=[sch.at(sca), sch.at(sso)], inc=None)
    sch.build()
    ctx.close()
    return nc


def build_L4():
    ctx = Ctx()
    nc = ctx.nc
    d_x = nc.dram_tensor("x", [TPC, D], F32, kind="ExternalInput").ap()
    d_rp = [nc.dram_tensor(f"rp{i}", [TPC, D], F32, kind="ExternalInput").ap()
            for i in range(NCORES)]
    d_ysh = nc.dram_tensor("ysh", [TPC, D], BF16, kind="ExternalInput").ap()
    d_wq = nc.dram_tensor("Wq", [D, D], F32, kind="ExternalInput").ap()
    d_wk = nc.dram_tensor("Wk", [D, 512], F32, kind="ExternalInput").ap()
    d_wv = nc.dram_tensor("Wv", [D, 512], F32, kind="ExternalInput").ap()
    d_x1 = nc.dram_tensor("x1", [TPC, D], F32, kind="ExternalOutput").ap()
    d_QT = nc.dram_tensor("QT", [D, TPC], BF16, kind="ExternalOutput").ap()
    d_K = nc.dram_tensor("K", [TPC, 512], BF16, kind="ExternalOutput").ap()
    d_V = nc.dram_tensor("V", [TPC, 512], BF16, kind="ExternalOutput").ap()
    d_qn = nc.dram_tensor("qn", [1, NHEAD * TPC], F32, kind="ExternalOutput").ap()
    d_kn = nc.dram_tensor("kn", [1, NKV * TPC], F32, kind="ExternalOutput").ap()

    sch = Sched(ctx)
    lds = [sch.sem(f"ld{i}") for i in range(4)]
    vs = sch.sem("vs")
    ss = sch.sem("ss")
    ms = sch.sem("ms")
    gs = sch.sem("gs")
    sto = [sch.sem(f"st{i}") for i in range(2)]

    identb = ctx.sbuf("identb", [128, 128], BF16)
    ident32 = ctx.sbuf("ident32", [128, 128], F32)
    e2 = ctx.sbuf("e2", [128, 2], F32)
    xb = [ctx.sbuf(f"xb{i}", [128, D], F32) for i in range(2)]
    rb = ctx.sbuf("rb", [128, D], F32)
    racc = ctx.sbuf("racc", [128, D], F32)
    yshb = ctx.sbuf("yshb", [128, D], BF16)
    x1b = ctx.sbuf("x1b", [128, D], F32)
    h32 = ctx.sbuf("h32", [128, D], F32)
    hbf = ctx.sbuf("hbf", [128, D], BF16)
    scr = {"sq": ctx.sbuf("sq", [128, D], F32)}
    for n in ("cm", "cv", "ct", "cr", "cb"):
        scr[n] = ctx.sbuf(n, [128, 1], F32)
    hT = ctx.sbuf("hT", [128, 16 * 512], BF16)
    wbuf = ctx.sbuf("wbuf", [128, 16 * 2048], BF16)
    qtb = ctx.sbuf("qtb", [128, 512], BF16)
    sqq = ctx.sbuf("sqq", [128, 512], F32)
    qn2 = ctx.sbuf("qn2", [2, 512], F32)
    kvT = ctx.sbuf("kvT", [128, 512], BF16)
    kvtm = ctx.sbuf("kvtm", [128, 4 * 512], BF16)
    ps_t = [ctx.psum(f"ps_t{i}", [128, 512], F32) for i in range(2)]
    ps_p = ctx.psum("ps_p", [128, 512], F32)
    ps_n = ctx.psum("ps_n", [128, 512], F32)

    sch.add("gpsimd", lambda g: g.memset(identb[:], 1.0), inc=gs)
    sch.add("gpsimd", lambda g: g.affine_select(
        out=identb[:], in_=identb[:], pattern=[[1, 128]],
        compare_op=ALU.is_equal, fill=0.0, base=0, channel_multiplier=-1),
        waits=[sch.at(gs)], inc=gs)
    sch.add("gpsimd", lambda g: g.memset(ident32[:], 1.0), inc=gs)
    sch.add("gpsimd", lambda g: g.affine_select(
        out=ident32[:], in_=ident32[:], pattern=[[1, 128]],
        compare_op=ALU.is_equal, fill=0.0, base=0, channel_multiplier=-1),
        waits=[sch.at(gs)], inc=gs)
    sch.add("gpsimd", lambda g: g.memset(e2[:], 0.0), inc=gs)
    sch.add("gpsimd", lambda g: g.memset(e2[0:64, 0:1], 1.0), inc=gs)
    sch.add("gpsimd", lambda g: g.memset(e2[64:128, 1:2], 1.0), inc=gs)
    m_gp = sch.at(gs)

    sch.add("gpsimd", lambda g: g.dma_start(
        out=wbuf[:].rearrange("p (k c) -> p k c", k=16),
        in_=d_wq.rearrange("(k p) c -> p k c", p=128)), inc=lds[3], is_dma=True,
        n=16)
    m_wq = sch.at(lds[3])

    m_apply = [None] * NTILE
    m_hdone = [None] * NTILE
    for t in range(NTILE):
        bb = t % 2
        w0 = [m_apply[t - 2]] if t >= 2 else []
        sch.dma("sync", xb[bb][:], d_x[t * 128:(t + 1) * 128, :], waits=w0,
                inc=lds[bb])
        sch.dma("sync", yshb[:], d_ysh[t * 128:(t + 1) * 128, :],
                waits=[sch.at(vs)], inc=lds[bb])
        for i in range(NCORES):
            sch.dma("sync", rb[:], d_rp[i][t * 128:(t + 1) * 128, :],
                    waits=[sch.at(vs)], inc=lds[bb])
            m_l = sch.at(lds[bb])
            if i == 0:
                sch.add("vector", lambda v: v.tensor_copy(racc[:], rb[:]),
                        waits=[m_l], inc=vs)
            else:
                sch.add("vector", lambda v: v.tensor_tensor(racc[:], racc[:],
                                                            rb[:], ALU.add),
                        waits=[m_l], inc=vs)
        def mk_x1(b_):
            def f(v):
                v.tensor_tensor(racc[:], racc[:], yshb[:], ALU.add)
                return v.scalar_tensor_tensor(x1b[:], racc[:], 0.5, xb[b_][:],
                                              ALU.mult, ALU.add)
            return f
        sch.add("vector", mk_x1(bb), waits=[sch.at(lds[bb]), sch.at(sto[0]),
                                            sch.at(sto[1])], inc=vs)
        m_x1 = sch.at(vs)
        sch.dma("scalar", d_x1[t * 128:(t + 1) * 128, :], x1b[:],
                waits=[m_x1], inc=sto[bb])
        emit_ln(sch, vs, ss, x1b[:], h32[:], scr, waits=[m_x1],
                apply_waits=[sch.at(ms)])
        m_apply[t] = sch.at(ss)
        sch.add("scalar", lambda sc: sc.copy(hbf[:], h32[:]), inc=ss)
        m_hbf = sch.at(ss)
        for pk in range(4):
            def mk_tp(pp, tt):
                def f(t_):
                    for k in range(4):
                        kk = pp * 4 + k
                        inst = t_.transpose(
                            ps_t[pp % 2][:].bitcast(BF16)[:, k * 128:(k + 1) * 128],
                            hbf[:, kk * 128:(kk + 1) * 128], identb[:])
                    return inst
                return f
            sch.add("tensor", mk_tp(pk, t), waits=[m_hbf, m_gp, sch.at(vs)],
                    inc=ms)
            def mk_ev(pp, tt):
                def f(v):
                    return v.tensor_copy(
                        hT[:].rearrange("p (k c) -> p k c", k=16)
                        [:, pp * 4:(pp + 1) * 4, (tt % 4) * 128:(tt % 4) * 128 + 128],
                        ps_t[pp % 2][:].bitcast(BF16)[:, 0:512]
                        .rearrange("p (k c) -> p k c", k=4))
                return f
            sch.add("vector", mk_ev(pk, t), waits=[sch.at(ms)], inc=vs)
        m_hdone[t] = sch.at(vs)

        if t % 4 != 3:
            continue
        b2 = t // 4
        for oc in range(16):
            def mk_q(oc_):
                def f(t_):
                    for k in range(16):
                        inst = t_.matmul(
                            ps_p[:, 0:512],
                            wbuf[:, k * 2048 + oc_ * 128:k * 2048 + oc_ * 128 + 128],
                            hT[:, k * 512:(k + 1) * 512],
                            start=(k == 0), stop=(k == 15))
                    return inst
                return f
            sch.add("tensor", mk_q(oc), waits=[m_hdone[t], m_wq, sch.at(ss),
                                               sch.at(vs)], inc=ms)
            sch.add("scalar", lambda sc: sc.activation(qtb[:], ps_p[:, 0:512],
                                                       AF.Copy, scale=0.125),
                    waits=[sch.at(ms), sch.at(sto[0]), sch.at(sto[1])], inc=ss)
            def mk_store_q(oc_, b2_):
                def f(sc):
                    return sc.dma_start(
                        out=d_QT[oc_ * 128:(oc_ + 1) * 128,
                                 b2_ * 512:(b2_ + 1) * 512], in_=qtb[:])
                return f
            sch.add("scalar", mk_store_q(oc, b2), waits=[sch.at(ss)],
                    inc=sto[oc % 2], is_dma=True, n=16)
            sch.add("scalar", lambda sc: sc.activation(sqq[:], qtb[:], AF.Square),
                    inc=ss)
            def mk_qn(oc_, b2_):
                def f(t_):
                    return t_.matmul(ps_n[0:2, 0:512], e2[:], sqq[:],
                                     start=True, stop=True)
                return f
            sch.add("tensor", mk_qn(oc, b2), waits=[sch.at(ss), sch.at(vs)], inc=ms)
            sch.add("vector", lambda v: v.tensor_copy(qn2[:], ps_n[0:2, 0:512]),
                    waits=[sch.at(ms), sch.at(sto[0]), sch.at(sto[1])], inc=vs)
            for hh in range(2):
                hq = oc * 2 + hh
                def mk_sqn(hh_, hq_, b2_):
                    def f(sc):
                        return sc.dma_start(
                            out=d_qn[0:1, hq_ * TPC + b2_ * 512:
                                     hq_ * TPC + b2_ * 512 + 512],
                            in_=qn2[hh_:hh_ + 1, :])
                    return f
                sch.add("scalar", mk_sqn(hh, hq, b2), waits=[sch.at(vs)],
                        inc=sto[hh], is_dma=True, n=16)
        for which, dw, dout in (("k", d_wk, d_K), ("v", d_wv, d_V)):
            def mk_wload(dw_):
                def f(g):
                    return g.dma_start(
                        out=wbuf[:, 0:16 * 512].rearrange("p (k c) -> p k c", k=16),
                        in_=dw_.rearrange("(k p) c -> p k c", p=128))
                return f
            sch.add("gpsimd", mk_wload(dw), waits=[sch.at(ms)], inc=lds[3],
                    is_dma=True, n=16)
            m_wkv = sch.at(lds[3])
            for oc in range(4):
                def mk_kv(oc_):
                    def f(t_):
                        for k in range(16):
                            inst = t_.matmul(
                                ps_p[:, 0:512],
                                wbuf[:, k * 512 + oc_ * 128:k * 512 + oc_ * 128 + 128],
                                hT[:, k * 512:(k + 1) * 512],
                                start=(k == 0), stop=(k == 15))
                        return inst
                    return f
                sch.add("tensor", mk_kv(oc), waits=[m_wkv, sch.at(ss), sch.at(vs)],
                        inc=ms)
                sch.add("vector", lambda v: v.tensor_copy(kvT[:], ps_p[:, 0:512]),
                        waits=[sch.at(ms)], inc=vs)
                if which == "k":
                    sch.add("scalar", lambda sc: sc.activation(sqq[:], kvT[:],
                                                               AF.Square),
                            waits=[sch.at(vs)], inc=ss)
                    sch.add("tensor", lambda t_: t_.matmul(
                        ps_n[0:2, 0:512], e2[:], sqq[:], start=True, stop=True),
                        waits=[sch.at(ss), sch.at(vs)], inc=ms)
                    sch.add("vector", lambda v: v.tensor_copy(qn2[:],
                                                              ps_n[0:2, 0:512]),
                            waits=[sch.at(ms), sch.at(sto[0]), sch.at(sto[1])],
                            inc=vs)
                    for hh in range(2):
                        hk = oc * 2 + hh
                        def mk_skn(hh_, hk_, b2_):
                            def f(sc):
                                return sc.dma_start(
                                    out=d_kn[0:1, hk_ * TPC + b2_ * 512:
                                             hk_ * TPC + b2_ * 512 + 512],
                                    in_=qn2[hh_:hh_ + 1, :])
                            return f
                        sch.add("scalar", mk_skn(hh, hk, b2),
                                waits=[sch.at(vs)], inc=sto[hh], is_dma=True,
                                n=16)
                def mk_tb(oc_):
                    def f(t_):
                        for k in range(4):
                            inst = t_.transpose(
                                ps_t[0][:].bitcast(BF16)[:, k * 128:(k + 1) * 128],
                                kvT[:, k * 128:(k + 1) * 128], identb[:])
                        return inst
                    return f
                sch.add("tensor", mk_tb(oc), waits=[sch.at(vs)], inc=ms)
                def mk_evk(oc_):
                    def f(v):
                        return v.tensor_copy(
                            kvtm[:].rearrange("p (t c) -> p t c", t=4)
                            [:, :, oc_ * 128:(oc_ + 1) * 128],
                            ps_t[0][:].bitcast(BF16)[:, 0:512]
                            .rearrange("p (k c) -> p k c", k=4))
                    return f
                sch.add("vector", mk_evk(oc), waits=[sch.at(ms), sch.at(sto[0]),
                                                     sch.at(sto[1])], inc=vs)
            m_kv = sch.at(vs)
            for tt in range(4):
                ti = b2 * 4 + tt
                def mk_skv(dout_, ti_, tt_):
                    def f(sc):
                        return sc.dma_start(
                            out=dout_[ti_ * 128:(ti_ + 1) * 128, :],
                            in_=kvtm[:, tt_ * 512:(tt_ + 1) * 512])
                    return f
                sch.add("scalar", mk_skv(dout, ti, tt), waits=[m_kv],
                        inc=sto[tt % 2], is_dma=True, n=16)
        if b2 == 0:
            sch.add("gpsimd", lambda g: g.dma_start(
                out=wbuf[:].rearrange("p (k c) -> p k c", k=16),
                in_=d_wq.rearrange("(k p) c -> p k c", p=128)),
                waits=[sch.at(ms)], inc=lds[3], is_dma=True, n=16)
            m_wq = sch.at(lds[3])

    sch.add("sync", lambda e: None, waits=[sch.at(sto[0]), sch.at(sto[1])],
            inc=None)
    sch.build()
    ctx.close()
    return nc



BOUND = [8, 16, 24, 32]
PACKS = [bb // 4 for bb in BOUND]
PACK_OFF = [0, 2, 6, 12]
NPACKS = sum(PACKS)


def emit_attention(sch, ctx, d_QT, d_K, d_V, d_qn, d_knf, d_mask, prefix=""):
    P = prefix
    lda = [sch.sem(P + f"lda{i}") for i in range(2)]
    vs = sch.sem(P + "avs")
    ss = sch.sem(P + "ass")
    ms = sch.sem(P + "ams")

    qt_sb = ctx.sbuf(P + "qt_sb", [128, 16 * TPC], BF16)
    attn_sb = ctx.sbuf(P + "attn_sb", [128, 16 * TPC], BF16)
    _af = attn_sb[:].bitcast(F32)
    qn_sb = _af[0:NHEAD, 0:TPC]
    knf = _af[0:16, 2048:2048 + S]
    qnmaxc = ctx.sbuf(P + "qnmaxc", [NHEAD, 1], F32)
    knmaxc = ctx.sbuf(P + "knmaxc", [16, 1], F32)
    qnmax = ctx.sbuf(P + "qnmax", [1, NHEAD], F32)
    knmax = ctx.sbuf(P + "knmax", [1, 16], F32)
    identf = ctx.sbuf(P + "identf", [32, 32], F32)
    vplus = ctx.sbuf(P + "vplus", [128, 32 * 520], BF16)
    ktb = ctx.sbuf(P + "ktb", [128, S], BF16)
    masks = ctx.sbuf(P + "masks", [128, NPACKS * 512], BF16)
    vstage = ctx.sbuf(P + "vstage", [128, 512], BF16)
    mcol = ctx.sbuf(P + "mcol", [1, 4], F32)
    negM = ctx.sbuf(P + "negM", [128, 1], F32)
    onesr1 = ctx.sbuf(P + "onesr1", [1, 128], F32)
    ones64r = ctx.sbuf(P + "ones64r", [1, 64], F32)
    pbuf = [ctx.sbuf(P + f"pbuf{i}", [128, 512], BF16) for i in range(2)]
    rcol = ctx.sbuf(P + "rcol", [1, 128], F32)
    obu = ctx.sbuf(P + "obu", [64, 128], F32)
    ps_s = [ctx.psum(P + f"ps_s{i}", [128, 512], F32) for i in range(2)]
    ps_o = [ctx.psum(P + f"ps_o{i}", [128, 512], F32) for i in range(2)]
    ps_r = ctx.psum(P + "ps_r", [128, 512], F32)

    gsA = sch.sem(P + "gsA")
    sch.add("gpsimd", lambda g: g.memset(identf[:], 1.0), inc=gsA)
    sch.add("gpsimd", lambda g: g.affine_select(
        out=identf[:], in_=identf[:], pattern=[[1, 32]],
        compare_op=ALU.is_equal, fill=0.0, base=0, channel_multiplier=-1),
        inc=gsA)
    m_idf = sch.at(gsA)
    sch.add("vector", lambda v: v.memset(onesr1[:], 1.0), inc=vs)
    sch.add("vector", lambda v: v.memset(ones64r[:], 1.0), inc=vs)

    def mk_vones(v):
        inst = None
        for ch in range(32):
            for h in range(8):
                inst = v.memset(vplus[:, ch * 520 + h * 65 + 64:
                                      ch * 520 + h * 65 + 65], 1.0)
        return inst
    sch.add("vector", mk_vones, inc=vs)
    for h in range(NKV):
        for g in range(G):
            cb = 4 * (h // 2) + g
            rb = 64 * (h % 2)
            f0 = 256 * h + 64 * g
            sch.dma("sync", qt_sb[rb:rb + 64, cb * TPC:(cb + 1) * TPC],
                    d_QT[f0:f0 + 64, :], inc=lda[0])
    sch.dma("sync", qn_sb, d_qn.rearrange("a (h q) -> (a h) q", q=TPC),
            inc=lda[0])
    sch.dma("sync", knf, d_knf.rearrange("a (r q) -> (a r) q", q=S),
            inc=lda[0])
    sch.dma("sync", masks[:], d_mask, inc=lda[0])
    m_qt = sch.at(lda[0])
    sch.add("vector", lambda v: v.reduce_max(qnmaxc[:], qn_sb, AX),
            waits=[m_qt], inc=vs)
    sch.add("vector", lambda v: v.reduce_max(knmaxc[:], knf, AX), inc=vs)
    sch.add("tensor", lambda t_: t_.transpose(ps_r[0:1, 0:NHEAD], qnmaxc[:],
                                              identf[:]),
            waits=[sch.at(vs), m_idf], inc=ms)
    sch.add("vector", lambda v: v.tensor_copy(qnmax[:], ps_r[0:1, 0:NHEAD]),
            waits=[sch.at(ms)], inc=vs)
    sch.add("tensor", lambda t_: t_.transpose(ps_r[0:1, 64:64 + 16], knmaxc[:],
                                              identf[0:16, 0:16]),
            waits=[sch.at(vs)], inc=ms)
    sch.add("vector", lambda v: v.tensor_copy(knmax[:], ps_r[0:1, 64:64 + 16]),
            waits=[sch.at(ms)], inc=vs)
    m_maxes = sch.at(vs)

    otile = 0
    for b in range(B):
        for ch in range(32):
            def mk_vl(b_, ch_):
                def f(sy):
                    return sy.dma_start(
                        out=vstage[:],
                        in_=d_V[b_ * S + ch_ * 128:b_ * S + (ch_ + 1) * 128, :])
                return f
            sch.add("sync", mk_vl(b, ch), waits=[sch.at(vs), sch.at(ms)],
                    inc=lda[1], is_dma=True, n=16)
            def mk_vc(ch_):
                def f(v):
                    inst = None
                    for h in range(8):
                        inst = v.tensor_copy(
                            vplus[:, ch_ * 520 + h * 65:ch_ * 520 + h * 65 + 64],
                            vstage[:, h * 64:(h + 1) * 64])
                    return inst
                return f
            sch.add("vector", mk_vc(ch), waits=[sch.at(lda[1])], inc=vs)
        m_vplus = sch.at(vs)
        for hp in range(4):
            def mk_kt(b_, hp_):
                def f(sy):
                    return sy.dma_start_transpose(
                        ktb[:], d_K[b_ * S:(b_ + 1) * S,
                                    hp_ * 128:(hp_ + 1) * 128])
                return f
            sch.add("sync", mk_kt(b, hp), waits=[sch.at(ms)], inc=lda[0],
                    is_dma=True, n=16)
            m_kt = sch.at(lda[0])
            for h2 in range(2):
                h = 2 * hp + h2
                krow = 64 * h2
                def mk_maxk(b_, h_):
                    def f(v):
                        return v.tensor_copy(
                            mcol[0:1, 0:1],
                            knmax[0:1, h_ * 2 + b_:h_ * 2 + b_ + 1])
                    return f
                sch.add("vector", mk_maxk(b, h), waits=[m_maxes], inc=vs)
                for g in range(4):
                    hq = 4 * h + g
                    prow = 64 * (h % 2)
                    qdb = 4 * (h // 2) + g
                    def mk_m(hq_):
                        def f(v):
                            v.tensor_copy(mcol[0:1, 1:2],
                                          qnmax[0:1, hq_:hq_ + 1])
                            return v.tensor_tensor(mcol[0:1, 2:3],
                                                   mcol[0:1, 0:1],
                                                   mcol[0:1, 1:2], ALU.mult)
                        return f
                    sch.add("vector", mk_m(hq), inc=vs)
                    sch.add("scalar", lambda sc: sc.activation(
                        mcol[0:1, 3:4], mcol[0:1, 2:3], AF.Sqrt),
                        waits=[sch.at(vs)], inc=ss)
                    sch.add("vector", lambda v: v.tensor_scalar(
                        mcol[0:1, 3:4], mcol[0:1, 3:4], -1.0001, None,
                        ALU.mult), waits=[sch.at(ss)], inc=vs)
                    sch.add("tensor", lambda t_: t_.matmul(
                        ps_r[0:128, 0:1], onesr1[:], mcol[0:1, 3:4],
                        start=True, stop=True), waits=[sch.at(vs)], inc=ms)
                    sch.add("vector", lambda v: v.tensor_copy(
                        negM[:], ps_r[0:128, 0:1]), waits=[sch.at(ms)], inc=vs)
                    m_negm = sch.at(vs)
                    for s in range(4):
                        qc0 = (4 * b + s) * 128
                        po = otile % 2
                        otile += 1
                        npk = PACKS[s]
                        for pg in range(npk):
                            pp = pg % 2
                            def mk_sc(pp_, pg_, krow_, prow_, qdb_, qc0_):
                                def f(t_):
                                    inst = None
                                    for kk in range(4):
                                        ch = pg_ * 4 + kk
                                        inst = t_.matmul(
                                            ps_s[pp_][:, kk * 128:(kk + 1) * 128],
                                            ktb[krow_:krow_ + 64,
                                                ch * 128:(ch + 1) * 128],
                                            qt_sb[prow_:prow_ + 64,
                                                  qdb_ * TPC + qc0_:
                                                  qdb_ * TPC + qc0_ + 128],
                                            start=True, stop=True)
                                    return inst
                                return f
                            sch.add("tensor",
                                    mk_sc(pp, pg, krow, prow, qdb, qc0),
                                    waits=[m_kt, m_negm, m_vplus, sch.at(ss)],
                                    inc=ms)
                            def mk_exp(pp_):
                                def f(sc):
                                    return sc.activation(
                                        pbuf[pp_][:], ps_s[pp_][:], AF.Exp,
                                        bias=negM[:])
                                return f
                            sch.add("scalar", mk_exp(pp),
                                    waits=[sch.at(ms), sch.at(vs)], inc=ss)
                            def mk_msk(pp_, s_, pg_):
                                def f(v):
                                    return v.tensor_tensor(
                                        pbuf[pp_][:], pbuf[pp_][:],
                                        masks[:, (PACK_OFF[s_] + pg_) * 512:
                                              (PACK_OFF[s_] + pg_ + 1) * 512],
                                        ALU.mult)
                                return f
                            sch.add("vector", mk_msk(pp, s, pg),
                                    waits=[sch.at(ss)], inc=vs)
                            def mk_pv(po_, pp_, pg_, h_, npk_):
                                def f(t_):
                                    inst = None
                                    for kk in range(4):
                                        ch = pg_ * 4 + kk
                                        inst = t_.matmul(
                                            ps_o[po_][0:65, 0:128],
                                            vplus[:, ch * 520 + h_ * 65:
                                                  ch * 520 + (h_ + 1) * 65],
                                            pbuf[pp_][:, kk * 128:(kk + 1) * 128],
                                            start=(pg_ == 0 and kk == 0),
                                            stop=(pg_ == npk_ - 1 and kk == 3))
                                    return inst
                                return f
                            sch.add("tensor", mk_pv(po, pp, pg, h, npk),
                                    waits=[sch.at(vs)], inc=ms)
                        def mk_rc(po_):
                            def f(v):
                                return v.reciprocal(rcol[:],
                                                    ps_o[po_][64:65, 0:128])
                            return f
                        sch.add("vector", mk_rc(po), waits=[sch.at(ms)], inc=vs)
                        sch.add("tensor", lambda t_: t_.matmul(
                            ps_r[0:64, 0:128], ones64r[:], rcol[:],
                            start=True, stop=True), waits=[sch.at(vs)], inc=ms)
                        def mk_ev(po_):
                            def f(v):
                                return v.tensor_copy(obu[:],
                                                     ps_o[po_][0:64, 0:128])
                            return f
                        sch.add("vector", mk_ev(po), waits=[sch.at(ms)], inc=vs)
                        def mk_nrm(prow_, qdb_, qc0_):
                            def f(v):
                                return v.tensor_tensor(
                                    attn_sb[prow_:prow_ + 64,
                                            qdb_ * TPC + qc0_:
                                            qdb_ * TPC + qc0_ + 128],
                                    obu[:], ps_r[0:64, 0:128], ALU.mult)
                            return f
                        sch.add("vector", mk_nrm(prow, qdb, qc0),
                                waits=[sch.at(ms)], inc=vs)
    bufs = {"qt_sb": qt_sb, "vplus": vplus, "ktb": ktb, "masks": masks,
            "ps_s": ps_s, "ps_o": ps_o, "ps_r": ps_r, "identf": identf}
    return attn_sb, sch.at(vs), (vs, ss, ms), bufs


def build_L5():
    ctx = Ctx()
    nc = ctx.nc
    d_x1 = nc.dram_tensor("x1", [TPC, D], F32, kind="ExternalInput").ap()
    d_QT = nc.dram_tensor("QT", [D, TPC], BF16, kind="ExternalInput").ap()
    d_qn = nc.dram_tensor("qn", [1, NHEAD * TPC], F32, kind="ExternalInput").ap()
    d_K = nc.dram_tensor("K", [T, 512], BF16, kind="ExternalInput").ap()
    d_V = nc.dram_tensor("V", [T, 512], BF16, kind="ExternalInput").ap()
    d_knf = nc.dram_tensor("knf", [1, NKV * T], F32, kind="ExternalInput").ap()
    d_msk = nc.dram_tensor("amask", [128, NPACKS * 512], BF16,
                           kind="ExternalInput").ap()
    d_wo = nc.dram_tensor("Wo", [D, D], F32, kind="ExternalInput").ap()
    d_x2 = nc.dram_tensor("x2", [TPC, D], F32, kind="ExternalOutput").ap()

    sch = Sched(ctx)
    attn_sb, m_attn, (avs, ass, ams), _ab = emit_attention(
        sch, ctx, d_QT, d_K, d_V, d_qn, d_knf, d_msk)

    ldo = [sch.sem(f"ldo{i}") for i in range(2)]
    vso = sch.sem("vso")
    sso = sch.sem("sso")
    mso = sch.sem("mso")
    sto = [sch.sem(f"sto{i}") for i in range(2)]
    identb = ctx.sbuf("identbO", [128, 128], BF16)
    wbuf = ctx.sbuf("wbufO", [128, 16 * 1024], BF16)
    x1b = ctx.sbuf("x1bO", [128, D], F32)
    oT = ctx.sbuf("oTO", [128, 512], BF16)
    otm = ctx.sbuf("otmO", [128, 4 * D], BF16)
    otf = ctx.sbuf("otfO", [128, D], F32)
    ps_p = ctx.psum("ps_pO", [128, 512], F32)
    ps_t2 = ctx.psum("ps_t2O", [128, 512], F32)

    gso = sch.sem("gso")
    sch.add("gpsimd", lambda g: g.memset(identb[:], 1.0), inc=gso)
    sch.add("gpsimd", lambda g: g.affine_select(
        out=identb[:], in_=identb[:], pattern=[[1, 128]],
        compare_op=ALU.is_equal, fill=0.0, base=0, channel_multiplier=-1),
        inc=gso)
    m_id = sch.at(gso)
    for b2 in range(2):
        for oh in range(2):
            for cb in range(16):
                a_, g_ = cb // 4, cb % 4
                for h2 in range(2):
                    f0 = 512 * a_ + 256 * h2 + 64 * g_
                    def mk_wl(cb_, h2_, f0_, oh_):
                        def f(g):
                            return g.dma_start(
                                out=wbuf[64 * h2_:64 * h2_ + 64,
                                         cb_ * 1024:(cb_ + 1) * 1024],
                                in_=d_wo[f0_:f0_ + 64,
                                         oh_ * 1024:(oh_ + 1) * 1024])
                        return f
                    sch.add("gpsimd", mk_wl(cb, h2, f0, oh),
                            waits=[sch.at(mso)], inc=ldo[0], is_dma=True, n=16)
            m_wo = sch.at(ldo[0])
            for oc8 in range(8):
                oc = oh * 8 + oc8
                def mk_o(oc8_, b2_):
                    def f(t_):
                        inst = None
                        for k in range(16):
                            inst = t_.matmul(
                                ps_p[:, 0:512],
                                wbuf[:, k * 1024 + oc8_ * 128:
                                     k * 1024 + oc8_ * 128 + 128],
                                attn_sb[:, k * TPC + b2_ * 512:
                                        k * TPC + b2_ * 512 + 512],
                                start=(k == 0), stop=(k == 15))
                        return inst
                    return f
                sch.add("tensor", mk_o(oc8, b2), waits=[m_attn, m_wo, m_id,
                                                        sch.at(vso)], inc=mso)
                sch.add("vector", lambda v: v.tensor_copy(oT[:], ps_p[:, 0:512]),
                        waits=[sch.at(mso)], inc=vso)
                def mk_tb(oc_):
                    def f(t_):
                        inst = None
                        for k in range(4):
                            inst = t_.transpose(
                                ps_t2[:].bitcast(BF16)[:, k * 128:(k + 1) * 128],
                                oT[:, k * 128:(k + 1) * 128], identb[:])
                        return inst
                    return f
                sch.add("tensor", mk_tb(oc), waits=[sch.at(vso)], inc=mso)
                def mk_evo(oc_):
                    def f(v):
                        return v.tensor_copy(
                            otm[:].rearrange("p (t c) -> p t c", t=4)
                            [:, :, oc_ * 128:(oc_ + 1) * 128],
                            ps_t2[:].bitcast(BF16)[:, 0:512]
                            .rearrange("p (k c) -> p k c", k=4))
                    return f
                sch.add("vector", mk_evo(oc), waits=[sch.at(mso), sch.at(sto[0]),
                                                     sch.at(sto[1])], inc=vso)
        m_otm = sch.at(vso)
        for tt in range(4):
            ti = b2 * 4 + tt
            def mk_x2(tt_):
                def f(v):
                    v.tensor_copy(otf[:], otm[:, tt_ * D:(tt_ + 1) * D])
                    return v.tensor_tensor(otf[:], otf[:], x1b[:], ALU.add)
                return f
            sch.dma("sync", x1b[:], d_x1[ti * 128:(ti + 1) * 128, :],
                    waits=[m_otm, sch.at(vso)], inc=ldo[1])
            sch.add("vector", mk_x2(tt), waits=[sch.at(ldo[1]),
                                                sch.at(sto[0]), sch.at(sto[1])],
                    inc=vso)
            def mk_sx2(ti_):
                def f(sc):
                    return sc.dma_start(out=d_x2[ti_ * 128:(ti_ + 1) * 128, :],
                                        in_=otf[:])
                return f
            sch.add("scalar", mk_sx2(ti), waits=[sch.at(vso)],
                    inc=sto[tt % 2], is_dma=True, n=16)
    sch.add("sync", lambda e: None, waits=[sch.at(sto[0]), sch.at(sto[1])],
            inc=None)
    sch.build()
    ctx.close()
    return nc


def mk_ffn1_bufs(ctx, sch, prefix):
    P = prefix
    b = {}
    b["lds"] = [sch.sem(P + f"fld{i}") for i in range(2)]
    b["vs"] = sch.sem(P + "fvs")
    b["ss"] = sch.sem(P + "fss")
    b["ms"] = sch.sem(P + "fms")
    b["wslab"] = [ctx.sbuf(P + f"wslab{i}", [128, 2 * 16 * 128], BF16)
                  for i in range(2)]
    b["dslab"] = [ctx.sbuf(P + f"dslab{i}", [128, 44 * 128], BF16)
                  for i in range(2)]
    interT = ctx.sbuf(P + "interT", [128, 44 * 512], BF16)
    b["ic"] = lambda c: interT[:, c * 512:(c + 1) * 512]
    b["sgs"] = ctx.sbuf(P + "fsgs", [128, 512], F32)
    b["sis"] = ctx.sbuf(P + "fsis", [128, 512], F32)
    b["ybf"] = ctx.sbuf(P + "fybf", [128, 16 * 512], BF16)
    b["sq"] = ctx.sbuf(P + "fsq", [128, 512], F32)
    b["ones1"] = ctx.sbuf(P + "fones", [128, 1], F32)
    b["onesr"] = ctx.sbuf(P + "fonesr", [1, 128], F32)
    b["rrow"] = ctx.sbuf(P + "frrow", [1, 512], F32)
    b["ysc"] = ctx.sbuf(P + "fysc", [128, 512], BF16)
    b["ytm"] = ctx.sbuf(P + "fytm", [128, 4 * D], BF16)
    b["ps_g"] = ctx.psum(P + "fps_g", [128, 512], F32)
    b["ps_u"] = ctx.psum(P + "fps_u", [128, 512], F32)
    b["ps_d"] = ctx.psum(P + "fps_d", [128, 512], F32)
    b["ps_r"] = ctx.psum(P + "fps_r", [128, 512], F32)
    b["ps_t"] = ctx.psum(P + "fps_t", [128, 512], F32)
    sch.add("vector", lambda v: v.memset(b["ones1"][:], 1.0), inc=b["vs"])
    sch.add("vector", lambda v: v.memset(b["onesr"][:], 1.0), inc=b["vs"])
    return b


def emit_ffn1_block(sch, b, identb, m_ident, d_wgu, d_wdn, m_hT, hT,
                    m_prev=None, m_start=None):
    vs, ss, ms, lds = b["vs"], b["ss"], b["ms"], b["lds"]
    NOC = I_FFN1 // 128
    for ocp in range(NOC):
        sl = b["wslab"][ocp % 2]
        w0 = [sch.at(ms)]
        if ocp < 2 and m_start is not None:
            w0.extend(m_start if isinstance(m_start, list) else [m_start])
        sch.dma("gpsimd", sl[:, 0:2048].rearrange("p (k c) -> p k c", k=16),
                d_wgu[:, :, 128 * ocp:128 * (ocp + 1)].rearrange("k p c -> p k c"),
                waits=w0, inc=lds[ocp % 2])
        sch.dma("gpsimd", sl[:, 2048:4096].rearrange("p (k c) -> p k c", k=16),
                d_wgu[:, :, I_FFN1 + 128 * ocp:I_FFN1 + 128 * (ocp + 1)]
                .rearrange("k p c -> p k c"), inc=lds[ocp % 2])
        m_w = sch.at(lds[ocp % 2])
        for which, ps, coff in (("g", b["ps_g"], 0), ("u", b["ps_u"], 2048)):
            def mk_mm(sl_, ps_, coff_):
                def f(t_):
                    inst = None
                    for k in range(16):
                        inst = t_.matmul(
                            ps_[:, 0:512],
                            sl_[:, coff_ + k * 128:coff_ + (k + 1) * 128],
                            hT[:, k * 512:(k + 1) * 512],
                            start=(k == 0), stop=(k == 15))
                    return inst
                return f
            sch.add("tensor", mk_mm(sl, ps, coff),
                    waits=[m_w, m_hT, sch.at(ss), sch.at(vs)], inc=ms)
        sch.add("scalar", lambda sc: sc.activation(b["sgs"][:], b["ps_g"][:, 0:512],
                                                   AF.Silu),
                waits=[sch.at(ms)], inc=ss)
        def mk_si(ocp_):
            def f(v):
                v.tensor_tensor(b["sis"][:], b["sgs"][:], b["ps_u"][:, 0:512],
                                ALU.mult)
                return v.tensor_copy(b["ic"](ocp_), b["sis"][:])
            return f
        sch.add("vector", mk_si(ocp), waits=[sch.at(ss)], inc=vs)
    m_int = sch.at(vs)
    for oc in range(16):
        dsl = b["dslab"][oc % 2]
        def mk_dl(oc_, dsl_):
            def f(g):
                return g.dma_start(
                    out=dsl_[:].rearrange("p (k c) -> p k c", k=44),
                    in_=d_wdn[:, :, 128 * oc_:128 * (oc_ + 1)]
                    .rearrange("k p c -> p k c"))
            return f
        sch.add("gpsimd", mk_dl(oc, dsl), waits=[sch.at(ms)],
                inc=lds[oc % 2], is_dma=True, n=16)
        m_w = sch.at(lds[oc % 2])
        def mk_dn(dsl_):
            def f(t_):
                inst = None
                for ib in range(44):
                    inst = t_.matmul(b["ps_d"][:, 0:512],
                                     dsl_[:, ib * 128:(ib + 1) * 128],
                                     b["ic"](ib),
                                     start=(ib == 0), stop=(ib == 43))
                return inst
            return f
        sch.add("tensor", mk_dn(dsl), waits=[m_w, m_int, sch.at(vs),
                                             sch.at(ss)], inc=ms)
        def mk_ev(oc_):
            def f(v):
                return v.tensor_copy(b["ybf"][:, oc_ * 512:(oc_ + 1) * 512],
                                     b["ps_d"][:, 0:512])
            return f
        sch.add("vector", mk_ev(oc), waits=[sch.at(ms)], inc=vs)
        sch.add("scalar", (lambda oc_: lambda sc: sc.activation(
            b["sq"][:], b["ybf"][:, oc_ * 512:(oc_ + 1) * 512], AF.Square))(oc),
            waits=[sch.at(vs)], inc=ss)
        def mk_rs(oc_):
            def f(t_):
                return t_.matmul(b["ps_r"][0:1, 0:512], b["ones1"][:],
                                 b["sq"][:], start=(oc_ == 0), stop=(oc_ == 15))
            return f
        sch.add("tensor", mk_rs(oc), waits=[sch.at(ss), sch.at(vs)], inc=ms)
    def mk_rms(v):
        v.tensor_scalar(b["rrow"][:], b["ps_r"][0:1, 0:512], 1.0 / D, EPS_RMS,
                        ALU.mult, ALU.add)
        return v.reciprocal(b["rrow"][:], b["rrow"][:])
    sch.add("vector", mk_rms, waits=[sch.at(ms)], inc=vs)
    sch.add("scalar", lambda sc: sc.activation(b["sq"][:, 0:512][0:1, 0:512],
                                               b["rrow"][:], AF.Sqrt),
            waits=[sch.at(vs)], inc=ss)
    sch.add("tensor", lambda t_: t_.matmul(
        b["ps_r"][0:128, 0:512], b["onesr"][:], b["sq"][0:1, 0:512],
        start=True, stop=True), waits=[sch.at(ss)], inc=ms)
    m_rbc = sch.at(ms)
    for oc in range(16):
        def mk_sc2(oc_):
            def f(v):
                return v.tensor_tensor(b["ysc"][:],
                                       b["ybf"][:, oc_ * 512:(oc_ + 1) * 512],
                                       b["ps_r"][0:128, 0:512], ALU.mult)
            return f
        sch.add("vector", mk_sc2(oc), waits=[m_rbc, sch.at(ms)], inc=vs)
        def mk_tb(oc_):
            def f(t_):
                inst = None
                for k in range(4):
                    inst = t_.transpose(
                        b["ps_t"][:].bitcast(BF16)[:, k * 128:(k + 1) * 128],
                        b["ysc"][:, k * 128:(k + 1) * 128], identb[:])
                return inst
            return f
        sch.add("tensor", mk_tb(oc), waits=[sch.at(vs), m_ident], inc=ms)
        def mk_evy(oc_):
            def f(v):
                return v.tensor_copy(
                    b["ytm"][:].rearrange("p (t c) -> p t c", t=4)
                    [:, :, oc_ * 128:(oc_ + 1) * 128],
                    b["ps_t"][:].bitcast(BF16)[:, 0:512]
                    .rearrange("p (k c) -> p k c", k=4))
            return f
        sch.add("vector", mk_evy(oc), waits=[sch.at(ms), m_prev], inc=vs)
    return sch.at(vs)



TWO_PI = 6.283185307179586
CW1 = float(np.float32(6.28125))
CW2 = float(np.float32(TWO_PI - 6.28125))
CW3 = TWO_PI - CW1 - float(np.float32(TWO_PI - 6.28125))
THETA_SCALE = -math.log(10000.0) / 32.0


def build_L8():
    ctx = Ctx()
    nc = ctx.nc
    d_x = nc.dram_tensor("x", [TPC, D], F32, kind="ExternalInput").ap()
    d_rp = [nc.dram_tensor(f"rp{i}", [TPC, D], F32, kind="ExternalInput").ap()
            for i in range(NCORES)]
    d_ysh = nc.dram_tensor("ysh", [TPC, D], BF16, kind="ExternalInput").ap()
    d_wgu = nc.dram_tensor("wguT", [16, 128, 2 * I_FFN1], BF16,
                           kind="ExternalInput").ap()
    d_wdn = nc.dram_tensor("wdnT", [44, 128, D], BF16, kind="ExternalInput").ap()
    d_wq = nc.dram_tensor("wqT", [16, 128, D], BF16, kind="ExternalInput").ap()
    d_wk = nc.dram_tensor("wkT", [16, 128, 512], BF16, kind="ExternalInput").ap()
    d_wv = nc.dram_tensor("wvT", [16, 128, 512], BF16, kind="ExternalInput").ap()
    d_bq = nc.dram_tensor("bq", [D, 1], F32, kind="ExternalInput").ap()
    d_bk = nc.dram_tensor("bk", [512, 1], F32, kind="ExternalInput").ap()
    d_bv = nc.dram_tensor("bv", [512, 1], F32, kind="ExternalInput").ap()
    d_pos = nc.dram_tensor("poscol", [TPC, 1], F32, kind="ExternalInput").ap()
    d_x4 = nc.dram_tensor("x4", [TPC, D], F32, kind="ExternalOutput").ap()
    d_QT = nc.dram_tensor("QT", [D, TPC], BF16, kind="ExternalOutput").ap()
    d_K = nc.dram_tensor("K", [TPC, 512], BF16, kind="ExternalOutput").ap()
    d_V = nc.dram_tensor("V", [TPC, 512], BF16, kind="ExternalOutput").ap()
    d_qn = nc.dram_tensor("qn", [1, NHEAD * TPC], F32, kind="ExternalOutput").ap()
    d_kn = nc.dram_tensor("kn", [1, NKV * TPC], F32, kind="ExternalOutput").ap()
    ds_x3 = nc.dram_tensor("x3scr", [TPC, D], F32, kind="Internal").ap()

    sch = Sched(ctx)
    lds = [sch.sem(f"ld{i}") for i in range(4)]
    vs = sch.sem("vs")
    ss = sch.sem("ss")
    ms = sch.sem("ms")
    gs = sch.sem("gs")
    sto = [sch.sem(f"st{i}") for i in range(2)]
    gld = [sch.sem(f"gld{i}") for i in range(2)]

    identb = ctx.sbuf("identb", [128, 128], BF16)
    e2 = ctx.sbuf("e2", [128, 2], F32)
    xb = ctx.sbuf("xb", [128, D], F32)
    rb = ctx.sbuf("rb", [128, D], F32)
    racc = ctx.sbuf("racc", [128, D], F32)
    h32 = ctx.sbuf("h32", [128, D], F32)
    hbf = ctx.sbuf("hbf", [128, D], BF16)
    scr = {"sq": ctx.sbuf("sq", [128, D], F32)}
    for n in ("cm", "cv", "ct", "cr", "cb"):
        scr[n] = ctx.sbuf(n, [128, 1], F32)
    hT = ctx.sbuf("hT", [128, 16 * 512], BF16)
    fb = mk_ffn1_bufs(ctx, sch, "f_")
    ytm = fb["ytm"]
    qslab = [fb["wslab"][i][:, 0:2048] for i in range(2)]
    qT_sb = ctx.sbuf("qT_sb", [128, 16 * 512], BF16)
    biasc = ctx.sbuf("biasc", [128, 16], F32)
    bql = ctx.sbuf("bql", [16, 128], F32)
    identf = ctx.sbuf("identf", [128, 128], F32)
    posc = ctx.sbuf("posc", [128, 1], F32)
    thetaM = ctx.sbuf("thetaM", [128, 32], F32)
    angs = ctx.sbuf("angs", [128, 32], F32)
    kwb = ctx.sbuf("kwb", [128, 32], F32)
    cosM = ctx.sbuf("cosM", [128, 32], F32)
    sinM = ctx.sbuf("sinM", [128, 32], F32)
    rtmp = h32[:, 0:1024]
    rtm2 = h32[:, 1024:2048]
    kvtm = ctx.sbuf("kvtm", [128, 2 * 512], BF16)
    qn2 = ctx.sbuf("qn2", [2, 512], F32)
    sqq = scr["sq"][:, 0:512]
    qfm = ctx.sbuf("qfm", [128, 512], BF16)
    ps_t = [ctx.psum(f"ps_t{i}", [128, 512], F32) for i in range(2)]
    ps_p = fb["ps_u"]
    ps_n = ctx.psum("ps_n", [128, 512], F32)

    sch.add("gpsimd", lambda g: g.memset(identb[:], 1.0), inc=gs)
    sch.add("gpsimd", lambda g: g.affine_select(
        out=identb[:], in_=identb[:], pattern=[[1, 128]],
        compare_op=ALU.is_equal, fill=0.0, base=0, channel_multiplier=-1),
        waits=[sch.at(gs)], inc=gs)
    sch.add("gpsimd", lambda g: g.memset(e2[:], 0.0), inc=gs)
    sch.add("gpsimd", lambda g: g.memset(e2[0:64, 0:1], 1.0), inc=gs)
    sch.add("gpsimd", lambda g: g.memset(e2[64:128, 1:2], 1.0), inc=gs)
    sch.add("gpsimd", lambda g: g.iota(thetaM[:].bitcast(I32), [[1, 32]],
                                       base=0, channel_multiplier=0), inc=gs)
    sch.add("gpsimd", lambda g: g.memset(identf[:], 1.0), inc=gs)
    sch.add("gpsimd", lambda g: g.affine_select(
        out=identf[:], in_=identf[:], pattern=[[1, 128]],
        compare_op=ALU.is_equal, fill=0.0, base=0, channel_multiplier=-1),
        waits=[sch.at(gs)], inc=gs)
    m_gp = sch.at(gs)
    sch.add("vector", lambda v: v.tensor_copy(thetaM[:], thetaM[:].bitcast(I32)),
            waits=[m_gp], inc=vs)
    sch.add("scalar", lambda sc: sc.activation(thetaM[:], thetaM[:], AF.Exp,
                                               scale=THETA_SCALE),
            waits=[sch.at(vs)], inc=ss)
    m_theta = sch.at(ss)

    def rope_tables(tile):
        sch.dma("sync", posc[:], d_pos[tile * 128:(tile + 1) * 128, :],
                waits=[sch.at(vs)], inc=lds[3])
        def red(v):
            v.tensor_scalar(angs[:], thetaM[:], posc[:], None, ALU.mult)
            v.tensor_scalar(kwb[:], angs[:], 1.0 / TWO_PI, None, ALU.mult)
            v.tensor_copy(kwb[:].bitcast(I32), kwb[:])
            v.tensor_copy(kwb[:], kwb[:].bitcast(I32))
            v.scalar_tensor_tensor(angs[:], kwb[:], -CW1, angs[:], ALU.mult,
                                   ALU.add)
            v.scalar_tensor_tensor(angs[:], kwb[:], -CW2, angs[:], ALU.mult,
                                   ALU.add)
            v.scalar_tensor_tensor(angs[:], kwb[:], -CW3, angs[:], ALU.mult,
                                   ALU.add)
            v.tensor_scalar(kwb[:], angs[:], math.pi, None, ALU.is_gt)
            return v.scalar_tensor_tensor(angs[:], kwb[:], -TWO_PI, angs[:],
                                          ALU.mult, ALU.add)
        sch.add("vector", red, waits=[sch.at(lds[3]), m_theta], inc=vs)
        sch.add("scalar", lambda sc: sc.activation(sinM[:], angs[:], AF.Sin),
                waits=[sch.at(vs)], inc=ss)
        def c2(v):
            v.tensor_scalar(angs[:], angs[:], math.pi / 2, None, ALU.add)
            v.tensor_scalar(kwb[:], angs[:], math.pi, None, ALU.is_gt)
            return v.scalar_tensor_tensor(angs[:], kwb[:], -TWO_PI, angs[:],
                                          ALU.mult, ALU.add)
        sch.add("vector", c2, waits=[sch.at(ss)], inc=vs)
        sch.add("scalar", lambda sc: sc.activation(cosM[:], angs[:], AF.Sin),
                waits=[sch.at(vs)], inc=ss)
        return sch.at(ss)

    def rope_apply(buf_ap, ngrp, m_ready):
        lo = buf_ap.rearrange("p (g d) -> p g d", d=64)[:, :, 0:32]
        hi = buf_ap.rearrange("p (g d) -> p g d", d=64)[:, :, 32:64]
        t1 = rtmp[:, 0:ngrp * 32].rearrange("p (g d) -> p g d", d=32)
        t2 = rtm2[:, 0:ngrp * 32].rearrange("p (g d) -> p g d", d=32)
        def f(v):
            cB = cosM[:].rearrange("p (g d) -> p g d", g=1).to_broadcast(
                [128, ngrp, 32])
            sB = sinM[:].rearrange("p (g d) -> p g d", g=1).to_broadcast(
                [128, ngrp, 32])
            v.tensor_tensor(t1, lo, cB, ALU.mult)
            v.tensor_tensor(t2, hi, sB, ALU.mult)
            v.tensor_tensor(t1, t1, t2, ALU.subtract)
            v.tensor_tensor(t2, hi, cB, ALU.mult)
            v.tensor_tensor(hi, lo, sB, ALU.mult)
            v.tensor_tensor(hi, t2, hi, ALU.add)
            return v.tensor_copy(lo, t1)
        sch.add("vector", f, waits=[m_ready], inc=vs)
        return sch.at(vs)

    def emit_tp_block(m_hbf, tt):
        for pk in range(4):
            def mk_tp(pp):
                def f(t_):
                    inst = None
                    for k in range(4):
                        kk = pp * 4 + k
                        inst = t_.transpose(
                            ps_t[pp % 2][:].bitcast(BF16)[:, k * 128:(k + 1) * 128],
                            hbf[:, kk * 128:(kk + 1) * 128], identb[:])
                    return inst
                return f
            sch.add("tensor", mk_tp(pk), waits=[m_hbf, m_gp, sch.at(vs)],
                    inc=ms)
            def mk_ev(pp, tt_):
                def f(v):
                    return v.tensor_copy(
                        hT[:].rearrange("p (k c) -> p k c", k=16)
                        [:, pp * 4:(pp + 1) * 4, tt_ * 128:(tt_ + 1) * 128],
                        ps_t[pp % 2][:].bitcast(BF16)[:, 0:512]
                        .rearrange("p (k c) -> p k c", k=4))
                return f
            sch.add("vector", mk_ev(pk, tt), waits=[sch.at(ms)], inc=vs)

    m_prev_ytm = None
    for b2 in range(2):
        for tt in range(4):
            t = b2 * 4 + tt
            sch.dma("sync", xb[:], d_x[t * 128:(t + 1) * 128, :],
                    waits=[sch.at(vs)], inc=lds[0])
            for i in range(NCORES):
                sch.dma("sync", rb[:], d_rp[i][t * 128:(t + 1) * 128, :],
                        waits=[sch.at(vs)], inc=lds[0])
                m_l = sch.at(lds[0])
                if i == 0:
                    sch.add("vector", lambda v: v.tensor_copy(racc[:], rb[:]),
                            waits=[m_l], inc=vs)
                else:
                    sch.add("vector", lambda v: v.tensor_tensor(
                        racc[:], racc[:], rb[:], ALU.add), waits=[m_l], inc=vs)
            sch.dma("sync", rb[:].bitcast(BF16)[:, 0:D],
                    d_ysh[t * 128:(t + 1) * 128, :], waits=[sch.at(vs)],
                    inc=lds[0])
            def mk_x3(v):
                v.tensor_tensor(racc[:], racc[:], rb[:].bitcast(BF16)[:, 0:D],
                                ALU.add)
                return v.scalar_tensor_tensor(racc[:], racc[:], 0.5, xb[:],
                                              ALU.mult, ALU.add)
            sch.add("vector", mk_x3, waits=[sch.at(lds[0])], inc=vs)
            m_x3 = sch.at(vs)
            sch.dma("scalar", ds_x3[t * 128:(t + 1) * 128, :], racc[:],
                    waits=[m_x3, sch.at(ss)], inc=sto[0])
            emit_ln(sch, vs, ss, racc[:], h32[:], scr, waits=[m_x3],
                    apply_waits=[sch.at(ms)])
            sch.add("scalar", lambda sc: sc.copy(hbf[:], h32[:]), inc=ss)
            emit_tp_block(sch.at(ss), tt)
        m_hT = sch.at(vs)
        m_ytm = emit_ffn1_block(sch, fb, identb, m_gp, d_wgu, d_wdn, m_hT, hT,
                                m_prev=m_prev_ytm,
                                m_start=(sch.at(ms) if b2 else None))
        for tt in range(4):
            t = b2 * 4 + tt
            sch.dma("sync", xb[:], ds_x3[t * 128:(t + 1) * 128, :],
                    waits=[m_ytm, sch.at(sto[0]), sch.at(vs)], inc=lds[0])
            def mk_x4(tt_):
                def f(v):
                    v.tensor_copy(racc[:], ytm[:, tt_ * D:(tt_ + 1) * D])
                    return v.scalar_tensor_tensor(racc[:], racc[:], 0.5, xb[:],
                                                  ALU.mult, ALU.add)
                return f
            sch.add("vector", mk_x4(tt), waits=[sch.at(lds[0])], inc=vs)
            m_x4 = sch.at(vs)
            sch.dma("scalar", d_x4[t * 128:(t + 1) * 128, :], racc[:],
                    waits=[m_x4, sch.at(ss)], inc=sto[1])
            emit_ln(sch, vs, ss, racc[:], h32[:], scr, waits=[m_x4],
                    apply_waits=[sch.at(ms)])
            sch.add("scalar", lambda sc: sc.copy(hbf[:], h32[:]), inc=ss)
            emit_tp_block(sch.at(ss), tt)
        m_h2T = sch.at(vs)
        sch.dma("sync", bql[0:16, :], d_bq.rearrange("(k c) a -> k (c a)", c=128),
                waits=[sch.at(ss)], inc=lds[3])
        sch.add("tensor", lambda t_: t_.transpose(ps_n[:, 0:16], bql[0:16, :],
                                                  identf[0:16, 0:16]),
                waits=[sch.at(lds[3]), m_gp, sch.at(vs)], inc=ms)
        sch.add("vector", lambda v: v.tensor_copy(biasc[:], ps_n[:, 0:16]),
                waits=[sch.at(ms)], inc=vs)
        m_bias = sch.at(vs)
        for oc in range(16):
            sl = qslab[oc % 2]
            def mk_ql(oc_, sl_):
                def f(g):
                    return g.dma_start(
                        out=sl_.rearrange("p (k c) -> p k c", k=16),
                        in_=d_wq[:, :, 128 * oc_:128 * (oc_ + 1)]
                        .rearrange("k p c -> p k c"))
                return f
            sch.add("gpsimd", mk_ql(oc, sl),
                    waits=[sch.at(ms), sch.at(fb["ms"])],
                    inc=gld[oc % 2], is_dma=True, n=16)
            m_w = sch.at(gld[oc % 2])
            def mk_qm(sl_):
                def f(t_):
                    inst = None
                    for k in range(16):
                        inst = t_.matmul(ps_p[:, 0:512],
                                         sl_[:, k * 128:(k + 1) * 128],
                                         hT[:, k * 512:(k + 1) * 512],
                                         start=(k == 0), stop=(k == 15))
                    return inst
                return f
            sch.add("tensor", mk_qm(sl), waits=[m_w, m_h2T, m_bias,
                                                sch.at(ss), sch.at(vs),
                                                sch.at(fb["vs"])], inc=ms)
            def mk_qe(oc_):
                def f(sc):
                    return sc.activation(qT_sb[:, oc_ * 512:(oc_ + 1) * 512],
                                         ps_p[:, 0:512], AF.Identity,
                                         scale=1.0, bias=biasc[:, oc_:oc_ + 1])
                return f
            sch.add("scalar", mk_qe(oc), waits=[sch.at(ms)], inc=ss)
        m_qproj = sch.at(ss)
        for tt in range(4):
            t = b2 * 4 + tt
            m_tab = rope_tables(t)
            for oc in range(16):
                def mk_tq(oc_, tt_):
                    def f(t_):
                        return t_.transpose(
                            ps_t[0][:].bitcast(BF16)[:, (oc_ % 4) * 128:
                                                     (oc_ % 4) * 128 + 128],
                            qT_sb[:, oc_ * 512 + tt_ * 128:
                                  oc_ * 512 + tt_ * 128 + 128], identb[:])
                    return f
                sch.add("tensor", mk_tq(oc, tt), waits=[m_qproj, sch.at(vs)],
                        inc=ms)
                if oc % 4 == 3:
                    def mk_evq(oc_, tt_):
                        def f(v):
                            return v.tensor_copy(
                                ytm[:, tt_ * D + (oc_ - 3) * 128:
                                    tt_ * D + (oc_ + 1) * 128],
                                ps_t[0][:].bitcast(BF16)[:, 0:512])
                        return f
                    sch.add("vector", mk_evq(oc, tt), waits=[sch.at(ms)],
                            inc=vs)
            m_rope = rope_apply(ytm[:, tt * D:(tt + 1) * D], 32, m_tab)
            for oc in range(16):
                def mk_tb(oc_, tt_):
                    def f(t_):
                        return t_.transpose(
                            ps_t[1][:].bitcast(BF16)[:, (oc_ % 4) * 128:
                                                     (oc_ % 4) * 128 + 128],
                            ytm[:, tt_ * D + oc_ * 128:tt_ * D + (oc_ + 1) * 128],
                            identb[:])
                    return f
                sch.add("tensor", mk_tb(oc, tt), waits=[m_rope, sch.at(ss)],
                        inc=ms)
                if oc % 4 == 3:
                    def mk_eq(sc):
                        return sc.activation(qfm[:],
                                             ps_t[1][:].bitcast(BF16)[:, 0:512],
                                             AF.Copy, scale=0.125)
                    sch.add("scalar", mk_eq, waits=[sch.at(ms), sch.at(sto[0])],
                            inc=ss)
                    def mk_sq2(oc_, t_glob):
                        def f(sc):
                            return sc.dma_start(
                                out=d_QT[(oc_ - 3) * 128:(oc_ + 1) * 128,
                                         t_glob * 128:(t_glob + 1) * 128]
                                .rearrange("(k p) c -> p k c", p=128),
                                in_=qfm[:].rearrange("p (k c) -> p k c", k=4))
                        return f
                    sch.add("scalar", mk_sq2(oc, t), waits=[sch.at(ss)],
                            inc=sto[0], is_dma=True, n=16)
                    sch.add("scalar", lambda sc: sc.activation(sqq, qfm[:],
                                                               AF.Square),
                            inc=ss)
                    def mk_qn(t_):
                        return t_.matmul(ps_n[0:2, 0:512], e2[:], sqq,
                                         start=True, stop=True)
                    sch.add("tensor", mk_qn, waits=[sch.at(ss), sch.at(vs)],
                            inc=ms)
                    def mk_qnev(v):
                        return v.tensor_copy(qn2[:], ps_n[0:2, 0:512])
                    sch.add("vector", mk_qnev, waits=[sch.at(ms),
                                                      sch.at(sto[1])], inc=vs)
                    def mk_sqn(oc_, t_glob):
                        def f(sc):
                            return sc.dma_start(
                                out=d_qn.rearrange("a (f h t c) -> h (a f) t c",
                                                   f=16, h=2, t=8)
                                [:, oc_ - 3:oc_ + 1, t_glob, :],
                                in_=qn2[:].rearrange("h (k c) -> h k c", k=4))
                        return f
                    sch.add("scalar", mk_sqn(oc, t), waits=[sch.at(vs)],
                            inc=sto[1], is_dma=True, n=16)
        for which, dw, dbias, dout in (("k", d_wk, d_bk, d_K),
                                       ("v", d_wv, d_bv, d_V)):
            sch.dma("sync", bql[0:4, :],
                    dbias.rearrange("(k c) a -> k (c a)", c=128),
                    waits=[sch.at(ss)], inc=lds[3])
            sch.add("tensor", (lambda t_: t_.transpose(
                ps_n[:, 0:4], bql[0:4, :], identf[0:4, 0:4])),
                waits=[sch.at(lds[3]), sch.at(vs)], inc=ms)
            sch.add("vector", lambda v: v.tensor_copy(biasc[:, 0:4],
                                                      ps_n[:, 0:4]),
                    waits=[sch.at(ms), sch.at(sto[1])], inc=vs)
            m_bias = sch.at(vs)
            for oc in range(4):
                sl = qslab[oc % 2]
                def mk_kl(oc_, sl_, dw_):
                    def f(g):
                        return g.dma_start(
                            out=sl_.rearrange("p (k c) -> p k c", k=16),
                            in_=dw_[:, :, 128 * oc_:128 * (oc_ + 1)]
                            .rearrange("k p c -> p k c"))
                    return f
                sch.add("gpsimd", mk_kl(oc, sl, dw), waits=[sch.at(ms)],
                        inc=gld[oc % 2], is_dma=True, n=16)
                m_w = sch.at(gld[oc % 2])
                def mk_km(sl_):
                    def f(t_):
                        inst = None
                        for k in range(16):
                            inst = t_.matmul(ps_p[:, 0:512],
                                             sl_[:, k * 128:(k + 1) * 128],
                                             hT[:, k * 512:(k + 1) * 512],
                                             start=(k == 0), stop=(k == 15))
                        return inst
                    return f
                sch.add("tensor", mk_km(sl), waits=[m_w, m_h2T, m_bias,
                                                    sch.at(ss), sch.at(vs)],
                        inc=ms)
                def mk_ke(oc_):
                    def f(sc):
                        return sc.activation(qT_sb[:, oc_ * 512:(oc_ + 1) * 512],
                                             ps_p[:, 0:512], AF.Identity,
                                             scale=1.0,
                                             bias=biasc[:, oc_:oc_ + 1])
                    return f
                sch.add("scalar", mk_ke(oc), waits=[sch.at(ms)], inc=ss)
                if which == "k":
                    sch.add("scalar", (lambda oc_: lambda sc: sc.activation(
                        sqq, qT_sb[:, oc_ * 512:(oc_ + 1) * 512], AF.Square)
                        )(oc), inc=ss)
                    sch.add("tensor", lambda t_: t_.matmul(
                        ps_n[0:2, 0:512], e2[:], sqq, start=True, stop=True),
                        waits=[sch.at(ss), sch.at(vs)], inc=ms)
                    sch.add("vector", lambda v: v.tensor_copy(
                        qn2[:], ps_n[0:2, 0:512]),
                        waits=[sch.at(ms), sch.at(sto[1])], inc=vs)
                    def mk_skn(oc_, b2_):
                        def f(sc):
                            return sc.dma_start(
                                out=d_kn.rearrange("a (o h b c) -> h (a o) b c",
                                                   o=4, h=2, b=2)
                                [:, oc_, b2_, :], in_=qn2[:])
                        return f
                    sch.add("scalar", mk_skn(oc, b2), waits=[sch.at(vs)],
                            inc=sto[1], is_dma=True, n=16)
            m_kproj = sch.at(ss)
            for tt in range(4):
                t = b2 * 4 + tt
                for oc in range(4):
                    def mk_tk(oc_, tt_):
                        def f(t_):
                            return t_.transpose(
                                ps_t[0][:].bitcast(BF16)[:, oc_ * 128:
                                                         (oc_ + 1) * 128],
                                qT_sb[:, oc_ * 512 + tt_ * 128:
                                      oc_ * 512 + tt_ * 128 + 128], identb[:])
                        return f
                    sch.add("tensor", mk_tk(oc, tt), waits=[m_kproj,
                                                            sch.at(vs)], inc=ms)
                def mk_evk(tt_):
                    def f(v):
                        return v.tensor_copy(
                            kvtm[:, (tt_ % 2) * 512:(tt_ % 2) * 512 + 512],
                            ps_t[0][:].bitcast(BF16)[:, 0:512])
                    return f
                sch.add("vector", mk_evk(tt), waits=[sch.at(ms),
                                                     sch.at(sto[tt % 2])],
                        inc=vs)
                if which == "k":
                    m_tab = rope_tables(t)
                    rope_apply(kvtm[:, (tt % 2) * 512:(tt % 2) * 512 + 512],
                               8, m_tab)
                m_kv = sch.at(vs)
                def mk_skv(dout_, t_glob, tt_):
                    def f(sc):
                        return sc.dma_start(
                            out=dout_[t_glob * 128:(t_glob + 1) * 128, :],
                            in_=kvtm[:, (tt_ % 2) * 512:(tt_ % 2) * 512 + 512])
                    return f
                sch.add("scalar", mk_skv(dout, t, tt),
                        waits=[m_kv, sch.at(ss)], inc=sto[tt % 2],
                        is_dma=True, n=16)
        m_prev_ytm = sch.at(vs)
    sch.add("vector", lambda v: v.memset(posc[:], 0.0),
            waits=[sch.at(sto[0]), sch.at(sto[1]), sch.at(lds[0]),
                   sch.at(lds[1])], inc=vs)
    sch.build()
    ctx.close()
    return nc



def build_L9():
    ctx = Ctx()
    nc = ctx.nc
    d_x4 = nc.dram_tensor("x4", [TPC, D], F32, kind="ExternalInput").ap()
    d_QT = nc.dram_tensor("QT", [D, TPC], BF16, kind="ExternalInput").ap()
    d_qn = nc.dram_tensor("qn", [1, NHEAD * TPC], F32, kind="ExternalInput").ap()
    d_K = nc.dram_tensor("K", [T, 512], BF16, kind="ExternalInput").ap()
    d_V = nc.dram_tensor("V", [T, 512], BF16, kind="ExternalInput").ap()
    d_knf = nc.dram_tensor("knf", [1, NKV * T], F32, kind="ExternalInput").ap()
    d_msk = nc.dram_tensor("amask", [128, NPACKS * 512], BF16,
                           kind="ExternalInput").ap()
    d_wo = nc.dram_tensor("woP", [16, 128, D], BF16, kind="ExternalInput").ap()
    d_bo = nc.dram_tensor("bo", [D, 1], F32, kind="ExternalInput").ap()
    d_wgu = nc.dram_tensor("wguT", [16, 128, 2 * I_FFN1], BF16,
                           kind="ExternalInput").ap()
    d_wdn = nc.dram_tensor("wdnT", [44, 128, D], BF16, kind="ExternalInput").ap()
    d_xout = nc.dram_tensor("xout", [TPC, D], F32, kind="ExternalOutput").ap()
    ds_x5 = nc.dram_tensor("x5scr", [TPC, D], F32, kind="Internal").ap()

    sch = Sched(ctx)
    attn_sb, m_attn, (avs, ass, ams), ab = emit_attention(
        sch, ctx, d_QT, d_K, d_V, d_qn, d_knf, d_msk)
    qt_sb, vplus, ktb, masks = (ab["qt_sb"], ab["vplus"], ab["ktb"],
                                ab["masks"])

    lds = [sch.sem(f"ld9{i}") for i in range(2)]
    gld = [sch.sem(f"gld9{i}") for i in range(2)]
    vs = sch.sem("vs9")
    ss = sch.sem("ss9")
    ms = sch.sem("ms9")
    gs = sch.sem("gs9")
    sto = [sch.sem(f"st9{i}") for i in range(2)]

    identb = ctx.sbuf("identb9", [128, 128], BF16)
    xb = ctx.sbuf("xb9", [128, D], F32)
    h32 = ctx.sbuf("h329", [128, D], F32)
    hbf = ctx.sbuf("hbf9", [128, D], BF16)
    scr = {"sq": ctx.sbuf("sq9", [128, D], F32)}
    for n in ("cm", "cv", "ct", "cr", "cb"):
        scr[n] = ctx.sbuf(n + "9", [128, 1], F32)
    hT = ctx.sbuf("hT9", [128, 16 * 512], BF16)
    oT = ctx.sbuf("oT9", [128, 512], BF16)
    otf = ctx.sbuf("otf9", [128, D], F32)
    biasc = ctx.sbuf("biasc9", [128, 16], F32)
    bql = ctx.sbuf("bql9", [16, 128], F32)
    f_ones1 = ctx.sbuf("f_ones19", [128, 1], F32)
    f_onesr = ctx.sbuf("f_onesr9", [1, 128], F32)
    f_rrow = ctx.sbuf("f_rrow9", [1, 512], F32)
    ps_f = [ctx.psum(f"ps_f9{i}", [128, 512], F32) for i in range(3)]

    qf32 = qt_sb[:].bitcast(F32)
    def ic(c):
        if c < 16:
            return vplus[:, 8192 + c * 512:8192 + (c + 1) * 512]
        if c < 36:
            return masks[:, (c - 16) * 512:(c - 15) * 512]
        return ktb[:, (c - 36) * 512:(c - 35) * 512]
    fb = {
        "lds": [sch.sem("f9ld0"), sch.sem("f9ld1")],
        "vs": sch.sem("f9vs"), "ss": sch.sem("f9ss"), "ms": sch.sem("f9ms"),
        "wslab": [vplus[:, 0:4096], vplus[:, 4096:8192]],
        "dslab": [qt_sb[:, 0:5632], qt_sb[:, 5632:11264]],
        "ic": ic,
        "sgs": qf32[:, 5632:6144], "sis": qf32[:, 6144:6656],
        "sq": qf32[:, 6656:7168], "ysc": qt_sb[:, 14336:14848],
        "ones1": f_ones1, "onesr": f_onesr, "rrow": f_rrow,
        "ybf": attn_sb[:, 0:8192], "ytm": attn_sb[:, 8192:16384],
        "ps_g": ab["ps_s"][0], "ps_u": ab["ps_s"][1],
        "ps_d": ps_f[0], "ps_r": ps_f[1], "ps_t": ps_f[2],
    }
    otm = masks[:, 0:8192]

    sch.add("gpsimd", lambda g: g.memset(identb[:], 1.0), inc=gs)
    sch.add("gpsimd", lambda g: g.affine_select(
        out=identb[:], in_=identb[:], pattern=[[1, 128]],
        compare_op=ALU.is_equal, fill=0.0, base=0, channel_multiplier=-1),
        waits=[sch.at(gs)], inc=gs)
    m_gp = sch.at(gs)
    sch.add("vector", lambda v: v.memset(f_ones1[:], 1.0), inc=vs)
    sch.add("vector", lambda v: v.memset(f_onesr[:], 1.0), inc=vs)
    sch.dma("sync", bql[0:16, :], d_bo.rearrange("(k c) a -> k (c a)", c=128),
            inc=lds[1])
    sch.add("tensor", lambda t_: t_.transpose(ps_f[0][:, 0:16], bql[0:16, :],
                                              ab["identf"][0:16, 0:16]),
            waits=[sch.at(lds[1]), m_attn], inc=ms)
    sch.add("vector", lambda v: v.tensor_copy(biasc[:], ps_f[0][:, 0:16]),
            waits=[sch.at(ms)], inc=vs)
    m_bias = sch.at(vs)

    ps_p = ab["ps_s"][0]
    ps_t2 = ab["ps_s"][1]
    for b2 in range(2):
        for oc in range(16):
            sl = fb["wslab"][oc % 2]
            def mk_wl(oc_, sl_):
                def f(g):
                    return g.dma_start(
                        out=sl_[:, 0:2048].rearrange("p (k c) -> p k c", k=16),
                        in_=d_wo[:, :, 128 * oc_:128 * (oc_ + 1)]
                        .rearrange("k p c -> p k c"))
                return f
            w0 = [sch.at(ms)]
            if b2 == 0 and oc < 2:
                w0.append(m_attn)
            sch.add("gpsimd", mk_wl(oc, sl), waits=w0, inc=gld[oc % 2],
                    is_dma=True, n=16)
            m_w = sch.at(gld[oc % 2])
            def mk_o(sl_, b2_):
                def f(t_):
                    inst = None
                    for k in range(16):
                        inst = t_.matmul(
                            ps_p[:, 0:512], sl_[:, k * 128:(k + 1) * 128],
                            attn_sb[:, k * TPC + b2_ * 512:
                                    k * TPC + b2_ * 512 + 512],
                            start=(k == 0), stop=(k == 15))
                    return inst
                return f
            sch.add("tensor", mk_o(sl, b2), waits=[m_w, m_attn, m_bias,
                                                   sch.at(ss), sch.at(vs)],
                    inc=ms)
            def mk_oe(oc_):
                def f(sc):
                    return sc.activation(oT[:], ps_p[:, 0:512], AF.Identity,
                                         scale=1.0, bias=biasc[:, oc_:oc_ + 1])
                return f
            sch.add("scalar", mk_oe(oc), waits=[sch.at(ms)], inc=ss)
            def mk_tb(t_):
                inst = None
                for k in range(4):
                    inst = t_.transpose(
                        ps_t2[:].bitcast(BF16)[:, k * 128:(k + 1) * 128],
                        oT[:, k * 128:(k + 1) * 128], identb[:])
                return inst
            sch.add("tensor", mk_tb, waits=[sch.at(ss), m_gp], inc=ms)
            def mk_evo(oc_):
                def f(v):
                    return v.tensor_copy(
                        otm.rearrange("p (t c) -> p t c", t=4)
                        [:, :, oc_ * 128:(oc_ + 1) * 128],
                        ps_t2[:].bitcast(BF16)[:, 0:512]
                        .rearrange("p (k c) -> p k c", k=4))
                return f
            sch.add("vector", mk_evo(oc), waits=[sch.at(ms)], inc=vs)
        m_otm = sch.at(vs)
        for tt in range(4):
            ti = b2 * 4 + tt
            sch.dma("sync", xb[:], d_x4[ti * 128:(ti + 1) * 128, :],
                    waits=[m_otm, sch.at(vs)], inc=lds[0])
            def mk_x5(tt_):
                def f(v):
                    v.tensor_copy(otf[:], otm[:, tt_ * D:(tt_ + 1) * D])
                    return v.tensor_tensor(otf[:], otf[:], xb[:], ALU.add)
                return f
            sch.add("vector", mk_x5(tt), waits=[sch.at(lds[0]), sch.at(sto[0])],
                    inc=vs)
            def mk_sx5(ti_):
                def f(sc):
                    return sc.dma_start(out=ds_x5[ti_ * 128:(ti_ + 1) * 128, :],
                                        in_=otf[:])
                return f
            sch.add("scalar", mk_sx5(ti), waits=[sch.at(vs), sch.at(ss)],
                    inc=sto[0], is_dma=True, n=16)
    m_x5done = sch.at(sto[0])

    def emit_tp9(m_hbf, tt):
        for pk in range(4):
            def mk_tp(pp):
                def f(t_):
                    inst = None
                    for k in range(4):
                        kk = pp * 4 + k
                        inst = t_.transpose(
                            ab["ps_o"][pp % 2][:].bitcast(BF16)
                            [:, k * 128:(k + 1) * 128],
                            hbf[:, kk * 128:(kk + 1) * 128], identb[:])
                    return inst
                return f
            sch.add("tensor", mk_tp(pk), waits=[m_hbf, m_gp, sch.at(vs)],
                    inc=ms)
            def mk_ev(pp, tt_):
                def f(v):
                    return v.tensor_copy(
                        hT[:].rearrange("p (k c) -> p k c", k=16)
                        [:, pp * 4:(pp + 1) * 4, tt_ * 128:(tt_ + 1) * 128],
                        ab["ps_o"][pp % 2][:].bitcast(BF16)[:, 0:512]
                        .rearrange("p (k c) -> p k c", k=4))
                return f
            sch.add("vector", mk_ev(pk, tt), waits=[sch.at(ms)], inc=vs)

    m_prev_ytm = None
    for b2 in range(2):
        for tt in range(4):
            ti = b2 * 4 + tt
            w0 = [m_x5done, sch.at(vs)]
            if m_prev_ytm is not None:
                w0.append(m_prev_ytm)
            sch.dma("sync", xb[:], ds_x5[ti * 128:(ti + 1) * 128, :],
                    waits=w0, inc=lds[0])
            emit_ln(sch, vs, ss, xb[:], h32[:], scr,
                    waits=[sch.at(lds[0])], apply_waits=[sch.at(ms)])
            sch.add("scalar", lambda sc: sc.copy(hbf[:], h32[:]), inc=ss)
            emit_tp9(sch.at(ss), tt)
        m_hT = sch.at(vs)
        m_ytm = emit_ffn1_block(sch, fb, identb, m_gp, d_wgu, d_wdn, m_hT, hT,
                                m_prev=m_prev_ytm,
                                m_start=([m_attn] if b2 == 0 else None))
        for tt in range(4):
            ti = b2 * 4 + tt
            sch.dma("sync", xb[:], ds_x5[ti * 128:(ti + 1) * 128, :],
                    waits=[m_ytm, sch.at(vs)], inc=lds[0])
            def mk_x6(tt_):
                def f(v):
                    v.tensor_copy(otf[:], fb["ytm"][:, tt_ * D:(tt_ + 1) * D])
                    return v.scalar_tensor_tensor(otf[:], otf[:], 0.5, xb[:],
                                                  ALU.mult, ALU.add)
                return f
            sch.add("vector", mk_x6(tt), waits=[sch.at(lds[0]), sch.at(sto[1])],
                    inc=vs)
            def mk_sx6(ti_):
                def f(sc):
                    return sc.dma_start(out=d_xout[ti_ * 128:(ti_ + 1) * 128, :],
                                        in_=otf[:])
                return f
            sch.add("scalar", mk_sx6(ti), waits=[sch.at(vs), sch.at(ss)],
                    inc=sto[1], is_dma=True, n=16)
        m_prev_ytm = sch.at(vs)
    sch.add("vector", lambda v: v.memset(f_ones1[:], 1.0),
            waits=[sch.at(sto[0]), sch.at(sto[1])], inc=vs)
    sch.build()
    ctx.close()
    return nc

import ml_dtypes

BFNP = ml_dtypes.bfloat16


def attn_masks(core):
    out = np.zeros((128, NPACKS * 512), np.float32)
    per_batch = sorted({core, 15 - core, 16 + core, 31 - core})
    tri = (np.arange(128)[None, :] >= np.arange(128)[:, None]).astype(np.float32)
    for s4 in range(4):
        jq = per_batch[s4]
        for pg in range(PACKS[s4]):
            for kk in range(4):
                ch = 4 * pg + kk
                col0 = (PACK_OFF[s4] + pg) * 512 + kk * 128
                if ch < jq:
                    out[:, col0:col0 + 128] = 1.0
                elif ch == jq:
                    out[:, col0:col0 + 128] = tri
    return out.astype(BFNP)


def _f32(a):
    return np.ascontiguousarray(np.asarray(a), dtype=np.float32)


def _p0_maps(inp):
    Wq, Wo = inp["L1_attn_Wq"], inp["L1_attn_Wo"]
    Wk, Wv = inp["L1_attn_Wk"], inp["L1_attn_Wv"]
    sg, su, sd = inp["L0_ffn_sg"], inp["L0_ffn_su"], inp["L0_ffn_sd"]
    wg, wu, wd = inp["L0_ffn_w_gate"], inp["L0_ffn_w_up"], inp["L0_ffn_w_down"]
    gup = np.ascontiguousarray(np.asarray(inp["L1_ffn_gate_up_packed"]),
                               dtype=np.int32)
    gus = np.ascontiguousarray(np.asarray(inp["L1_ffn_gate_up_scales"]),
                               dtype=np.float16)
    dnp = np.ascontiguousarray(np.asarray(inp["L1_ffn_down_packed"]),
                               dtype=np.int32)
    dns = np.ascontiguousarray(np.asarray(inp["L1_ffn_down_scales"]),
                               dtype=np.float16)
    maps = []
    for c in range(NCORES):
        rows2 = lambda W: np.stack([_f32(W[128 * j:128 * (j + 1)])
                                    for j in (c, c + 8)])
        maps.append({
            "q_rows": rows2(Wq), "o_rows": rows2(Wo),
            "kv_rows": _f32(Wk[128 * c:128 * (c + 1)])[None] if c < 4
                       else _f32(Wv[128 * (c - 4):128 * (c - 3)])[None],
            "sg_rows": _f32(sg[128 * c:128 * (c + 1)])[None],
            "su_rows": _f32(su[128 * c:128 * (c + 1)])[None],
            "sd_rows": rows2(sd),
            "wg_rows": _f32(wg[4 * c:4 * c + 4]).reshape(32, 128, 2048),
            "wu_rows": _f32(wu[4 * c:4 * c + 4]).reshape(32, 128, 2048),
            "wd_rows": _f32(wd[4 * c:4 * c + 4]).reshape(64, 128, 1024),
            "gup": gup[GUS * c:GUS * (c + 1)],
            "gus": np.ascontiguousarray(gus[:, GUS * c:GUS * (c + 1)]),
            "dnp": dnp[DNS * c:DNS * (c + 1)],
            "dns": np.ascontiguousarray(dns[:, DNS * c:DNS * (c + 1)]),
        })
    return maps


def _asm_2blk(resP, key, w):
    out = np.empty((w // 128, 128, 2048), BFNP)
    for c in range(NCORES):
        t = resP[c]["T_" + key]
        for jj, J in enumerate((c, c + 8)):
            out[:, :, 128 * J:128 * (J + 1)] = t[:, :, 128 * jj:128 * (jj + 1)]
    return out


def _moe_pass(x_shards, gw, mb, ap_consts, bx_consts):
    mapsA = [{"x": x_shards[c], "gate_w": gw, "mbias": mb}
             for c in range(NCORES)]
    resA = run_bass_kernel_spmd(build_A(), mapsA,
                                core_ids=list(range(NCORES))).results
    counts_g = np.zeros((NCHUNK, E), np.float32)
    for c in range(NCORES):
        cc = resA[c]["counts"].reshape(NTILE, E)
        for t, j in enumerate(OWN[c]):
            counts_g[j] = cc[t]
    mapsAP = []
    for c in range(NCORES):
        m = {"h": resA[c]["h"], "counts_g": counts_g, "S": resA[c]["S"],
             "idx6": resA[c]["idx6"], "w6": resA[c]["w6"]}
        m.update(ap_consts[c])
        mapsAP.append(m)
    resAP = run_bass_kernel_spmd(build_AP(), mapsAP,
                                 core_ids=list(range(NCORES))).results
    hfull = np.zeros((T + 8, D), BFNP)
    for c in range(NCORES):
        hfull[OWN_ROWS[c]] = resA[c]["h"]
    mapsBX = []
    nsl = EPC * CAP
    for c in range(NCORES):
        m = {"hfull": hfull}
        for i in range(NCORES):
            buf = np.zeros((SSROWS2, 2), np.float32)
            buf[0:nsl] = resAP[i]["sspart"][nsl * c:nsl * (c + 1)]
            m[f"ss{i}"] = buf
        m.update(bx_consts[c])
        mapsBX.append(m)
    resBX = run_bass_kernel_spmd(build_BX(), mapsBX,
                                 core_ids=list(range(NCORES))).results
    rp_slices = [[np.ascontiguousarray(resBX[i]["rp"][OWN_ROWS[c]])
                  for i in range(NCORES)] for c in range(NCORES)]
    ysh = [resAP[c]["ysh"] for c in range(NCORES)]
    return rp_slices, ysh


def kernel(**inputs):
    cores = list(range(NCORES))
    X = _f32(inputs["L0_attn_x"]).reshape(T, D)
    xs = shard_rows(X)
    gw = _f32(inputs["L0_ffn_gate_w"])
    mb = _f32(inputs["L0_ffn_bias"]).reshape(1, E)
    Wq0 = _f32(inputs["L0_attn_Wq"])
    Wk0 = _f32(inputs["L0_attn_Wk"])
    Wv0 = _f32(inputs["L0_attn_Wv"])
    Wo0 = _f32(inputs["L0_attn_Wo"])
    bq1 = _f32(inputs["L1_attn_bq"]).reshape(D, 1)
    bk1 = _f32(inputs["L1_attn_bk"]).reshape(512, 1)
    bv1 = _f32(inputs["L1_attn_bv"]).reshape(512, 1)
    bo1 = _f32(inputs["L1_attn_bo"]).reshape(D, 1)

    resP = run_bass_kernel_spmd(build_P0(), _p0_maps(inputs),
                                core_ids=cores).results
    wqT1 = _asm_2blk(resP, "q_rows", D)
    woT1 = _asm_2blk(resP, "o_rows", D)
    sdT = _asm_2blk(resP, "sd_rows", I_MOE)
    wkT1 = np.empty((16, 128, 512), BFNP)
    wvT1 = np.empty((16, 128, 512), BFNP)
    sgT = np.empty((16, 128, 1024), BFNP)
    suT = np.empty((16, 128, 1024), BFNP)
    for c in range(NCORES):
        if c < 4:
            wkT1[:, :, 128 * c:128 * (c + 1)] = resP[c]["T_kv_rows"]
        else:
            wvT1[:, :, 128 * (c - 4):128 * (c - 3)] = resP[c]["T_kv_rows"]
        sgT[:, :, 128 * c:128 * (c + 1)] = resP[c]["T_sg_rows"]
        suT[:, :, 128 * c:128 * (c + 1)] = resP[c]["T_su_rows"]
    wguT = np.empty((16, 128, 2 * I_FFN1), BFNP)
    wdnT = np.empty((44, 128, D), BFNP)
    for c in range(NCORES):
        wguT[:, :, GUS * c:GUS * (c + 1)] = resP[c]["wguT"]
        wdnT[:, :, DNS * c:DNS * (c + 1)] = resP[c]["wdnT"]
    woP = np.empty((16, 128, D), BFNP)
    for cb in range(16):
        a_, g_ = cb // 4, cb % 4
        for h2 in range(2):
            f0 = 512 * a_ + 256 * h2 + 64 * g_
            for ff in range(64):
                f = f0 + ff
                woP[cb, 64 * h2 + ff, :] = woT1[f // 128, f % 128, :]

    ap_consts, bx_consts = [], []
    masks = [attn_masks(c) for c in range(NCORES)]
    poscols = [((OWN_ROWS[c] % S) + 1).astype(np.float32).reshape(TPC, 1)
               for c in range(NCORES)]
    for c in range(NCORES):
        sel = np.zeros((NCHUNK, NTILE), np.float32)
        for t, j in enumerate(OWN[c]):
            sel[j, t] = 1.0
        tokcol = (OWN_ROWS[c] + 1).astype(np.float32).reshape(TPC, 1)
        ap_consts.append({"sgT": sgT, "suT": suT, "sdT": sdT, "selmat": sel,
                          "tokcol": tokcol})
        bx_consts.append({"wgT": resP[c]["T_wg_rows"],
                          "wuT": resP[c]["T_wu_rows"],
                          "wdT": resP[c]["T_wd_rows"]})

    rp1, ysh1 = _moe_pass(xs, gw, mb, ap_consts, bx_consts)
    mapsL4 = []
    for c in range(NCORES):
        m = {"x": xs[c], "ysh": ysh1[c], "Wq": Wq0, "Wk": Wk0, "Wv": Wv0}
        for i in range(NCORES):
            m[f"rp{i}"] = rp1[c][i]
        mapsL4.append(m)
    resL4 = run_bass_kernel_spmd(build_L4(), mapsL4, core_ids=cores).results
    K0 = np.empty((T, 512), BFNP)
    V0 = np.empty((T, 512), BFNP)
    kn0 = np.empty((NKV, T), np.float32)
    for c in range(NCORES):
        K0[OWN_ROWS[c]] = resL4[c]["K"]
        V0[OWN_ROWS[c]] = resL4[c]["V"]
        kn0[:, OWN_ROWS[c]] = resL4[c]["kn"].reshape(NKV, TPC)
    knf0 = np.ascontiguousarray(kn0.reshape(1, NKV * T))
    mapsL5 = [{"x1": resL4[c]["x1"], "QT": resL4[c]["QT"],
               "qn": resL4[c]["qn"], "K": K0, "V": V0, "knf": knf0,
               "amask": masks[c], "Wo": Wo0} for c in range(NCORES)]
    resL5 = run_bass_kernel_spmd(build_L5(), mapsL5, core_ids=cores).results
    x2s = [resL5[c]["x2"] for c in range(NCORES)]

    rp2, ysh2 = _moe_pass(x2s, gw, mb, ap_consts, bx_consts)
    mapsL8 = []
    for c in range(NCORES):
        m = {"x": x2s[c], "ysh": ysh2[c], "wguT": wguT, "wdnT": wdnT,
             "wqT": wqT1, "wkT": wkT1, "wvT": wvT1, "bq": bq1, "bk": bk1,
             "bv": bv1, "poscol": poscols[c]}
        for i in range(NCORES):
            m[f"rp{i}"] = rp2[c][i]
        mapsL8.append(m)
    resL8 = run_bass_kernel_spmd(build_L8(), mapsL8, core_ids=cores).results
    K1 = np.empty((T, 512), BFNP)
    V1 = np.empty((T, 512), BFNP)
    kn1 = np.empty((NKV, T), np.float32)
    for c in range(NCORES):
        K1[OWN_ROWS[c]] = resL8[c]["K"]
        V1[OWN_ROWS[c]] = resL8[c]["V"]
        kn1[:, OWN_ROWS[c]] = resL8[c]["kn"].reshape(NKV, TPC)
    knf1 = np.ascontiguousarray(kn1.reshape(1, NKV * T))

    mapsL9 = [{"x4": resL8[c]["x4"], "QT": resL8[c]["QT"],
               "qn": resL8[c]["qn"], "K": K1, "V": V1, "knf": knf1,
               "amask": masks[c], "woP": woP, "bo": bo1, "wguT": wguT,
               "wdnT": wdnT} for c in range(NCORES)]
    resL9 = run_bass_kernel_spmd(build_L9(), mapsL9, core_ids=cores).results
    out = unshard_rows([resL9[c]["xout"] for c in range(NCORES)])
    return np.ascontiguousarray(out.reshape(B, S, D), dtype=np.float32)


# version 03
# speedup vs baseline: 1.1049x; 1.1049x over previous
import math
import numpy as np
from contextlib import ExitStack

import concourse.bass as bass
import concourse.mybir as mybir
from concourse.bass_utils import run_bass_kernel_spmd

F32 = mybir.dt.float32
BF16 = mybir.dt.bfloat16
F16 = mybir.dt.float16
I32 = mybir.dt.int32
U16 = mybir.dt.uint16
U32 = mybir.dt.uint32
AF = mybir.ActivationFunctionType
ALU = mybir.AluOpType
AX = mybir.AxisListType.X

NCORES = 8
B, S, D = 2, 4096, 2048
T = B * S
CHUNK = 128
NCHUNK = T // CHUNK
TPC = T // NCORES
NTILE = TPC // CHUNK
NHEAD, NKV, HD = 32, 8, 64
G = NHEAD // NKV
E, K_TOP, I_MOE, CAP = 32, 6, 1024, 768
EPC = E // NCORES
MOE_SCALE = 1.5
SWIGLU_LIMIT = 10.0
I_FFN1 = 5632
EPS_LN = 1e-5
EPS_RMS = 1e-6
NSLOT = E * CAP
TRASH = NSLOT + 16
SSROWS = NSLOT + 160
GUS = 2 * I_FFN1 // NCORES
DNS = D // NCORES


def own_chunks(core):
    per_batch = sorted({core, 15 - core, 16 + core, 31 - core})
    return [32 * b + j for b in range(B) for j in per_batch]


OWN = [own_chunks(c) for c in range(NCORES)]
OWN_ROWS = [np.concatenate([np.arange(j * CHUNK, (j + 1) * CHUNK) for j in OWN[c]])
            for c in range(NCORES)]


def shard_rows(full):
    return [np.ascontiguousarray(full[OWN_ROWS[c]]) for c in range(NCORES)]


def unshard_rows(parts):
    parts = [np.asarray(p) for p in parts]
    out = np.empty((T,) + parts[0].shape[1:], dtype=parts[0].dtype)
    for c in range(NCORES):
        out[OWN_ROWS[c]] = parts[c]
    return out


class Ctx:
    def __init__(self):
        self.nc = bass.Bass("TRN2", target_bir_lowering=False)
        self.stack = ExitStack()

    def sbuf(self, name, shape, dtype):
        return self.stack.enter_context(self.nc.sbuf_tensor(name, shape, dtype))

    def psum(self, name, shape, dtype=F32):
        return self.stack.enter_context(self.nc.psum_tensor(name, shape, dtype))

    def sem(self, name):
        return self.stack.enter_context(self.nc.semaphore(name))

    def block(self):
        return self.nc.Block()

    def close(self):
        self.stack.close()


def emit_ident(g, ident, sem, cnt):
    g.memset(ident[:], 1.0).then_inc(sem, 1)
    cnt[0] += 1
    g.wait_ge(sem, cnt[0])
    g.affine_select(out=ident[:], in_=ident[:], pattern=[[1, ident.shape[1]]],
                    compare_op=ALU.is_equal, fill=0.0, base=0,
                    channel_multiplier=-1).then_inc(sem, 1)
    cnt[0] += 1



P0_FAM = [
    ("q_rows", 2, 2048), ("o_rows", 2, 2048), ("kv_rows", 1, 2048),
    ("sg_rows", 1, 2048), ("su_rows", 1, 2048), ("sd_rows", 2, 1024),
    ("wg_rows", 32, 2048), ("wu_rows", 32, 2048), ("wd_rows", 64, 1024),
]


def build_P0():
    ctx = Ctx()
    nc = ctx.nc
    d_in = {n: nc.dram_tensor(n, [nj, 128, w], F32, kind="ExternalInput").ap()
            for n, nj, w in P0_FAM}
    d_out = {n: nc.dram_tensor("T_" + n, [w // 128, 128, nj * 128], BF16,
                               kind="ExternalOutput").ap() for n, nj, w in P0_FAM}
    d_gup = nc.dram_tensor("gup", [GUS, D // 8], I32, kind="ExternalInput").ap()
    d_gus = nc.dram_tensor("gus", [D // 128, GUS], F16, kind="ExternalInput").ap()
    d_dnp = nc.dram_tensor("dnp", [DNS, I_FFN1 // 8], I32, kind="ExternalInput").ap()
    d_dns = nc.dram_tensor("dns", [I_FFN1 // 128, DNS], F16, kind="ExternalInput").ap()
    d_wgu = nc.dram_tensor("wguT", [D // 128, 128, GUS], BF16,
                           kind="ExternalOutput").ap()
    d_wdn = nc.dram_tensor("wdnT", [I_FFN1 // 128, 128, DNS], BF16,
                           kind="ExternalOutput").ap()

    jobs = []
    for n, nj, w in P0_FAM:
        for j in range(nj):
            jobs.append((d_in[n][j], d_out[n], j, w))
    NJ = len(jobs)
    cum_packs = [0]
    packs = []
    for ji, (_, _, _, w) in enumerate(jobs):
        for pi in range(w // 512):
            packs.append((ji, pi, 4))
        cum_packs.append(len(packs))
    NP = len(packs)

    dq_jobs = []
    for j in range(GUS // 128):
        dq_jobs.append((d_gup[j * 128:(j + 1) * 128, :],
                        d_gus[:, j * 128:(j + 1) * 128], d_wgu, D // 8, D // 128,
                        j * 128))
    for j in range(DNS // 128):
        dq_jobs.append((d_dnp[j * 128:(j + 1) * 128, :],
                        d_dns[:, j * 128:(j + 1) * 128], d_wdn, I_FFN1 // 8,
                        I_FFN1 // 128, j * 128))
    NDQ = len(dq_jobs)

    lds = [ctx.sem(f"ld{i}") for i in range(4)]
    mm = ctx.sem("mm")
    ev = ctx.sem("ev")
    sts = [ctx.sem(f"st{i}") for i in range(2)]
    gp = ctx.sem("gp")
    dqls = [ctx.sem(f"dql{i}") for i in range(2)]
    dqv = ctx.sem("dqv")
    dqm = ctx.sem("dqm")
    dqe = ctx.sem("dqe")
    dqss = [ctx.sem(f"dqs{i}") for i in range(2)]

    ident = ctx.sbuf("ident", [128, 128], BF16)
    ident16 = ctx.sbuf("ident16", [128, 128], F16)
    tbf = [ctx.sbuf(f"tbf{i}", [128, 2048], BF16) for i in range(2)]
    tps = [ctx.psum(f"tps{i}", [128, 512], F32) for i in range(2)]
    tev = [ctx.sbuf(f"tev{i}", [128, 512], BF16) for i in range(2)]

    W8 = I_FFN1
    dq_pk = ctx.sbuf("dq_pk", [128, I_FFN1 // 8], I32)
    dq_s16 = ctx.sbuf("dq_s16", [64, 128], F16)
    dq_sps_f = ctx.psum("dq_sps", [128, 512], F32)
    dq_sc = ctx.sbuf("dq_sc", [128, 64], F32)
    dq_scf = ctx.sbuf("dq_scf", [128, W8 // 4], F32)
    dq_n = ctx.sbuf("dq_n", [128, W8 // 4], U16)
    dq_a = ctx.sbuf("dq_a", [128, W8 // 4], U16)
    dq_A = ctx.sbuf("dq_A", [128, W8 // 4], U16)
    dq_t = ctx.sbuf("dq_t", [128, W8 // 4], BF16)
    dq_Bv = ctx.sbuf("dq_B", [128, W8 // 4], BF16)
    dq_g = ctx.sbuf("dq_g", [128, W8 // 4], U16)
    dq_w = ctx.sbuf("dq_w", [128, W8 // 4], BF16)
    dps = [ctx.psum(f"dps{i}", [128, 512], F32) for i in range(2)]
    dev = [ctx.sbuf(f"dev{i}", [128, 512], BF16) for i in range(2)]

    dq_packs = []
    for dj, (_, _, _, nw, ngrp, _) in enumerate(dq_jobs):
        nb = (2 * nw) // 128
        for pl in range(4):
            b0 = 0
            while b0 < nb:
                nb_here = min(4, nb - b0)
                dq_packs.append((dj, pl, b0, nb_here))
                b0 += nb_here
    NDP = len(dq_packs)
    plane_done = {}
    cnt = 0
    for dj in range(NDQ):
        for pl in range(4):
            cnt += 1
            plane_done[(dj, pl)] = cnt

    with ctx.block() as block:

        @block.gpsimd
        def _(g):
            gcnt = [0]
            emit_ident(g, ident, gp, gcnt)
            emit_ident(g, ident16, gp, gcnt)
            for ji, (src, _, _, w) in enumerate(jobs):
                if ji >= 2:
                    g.wait_ge(mm, cum_packs[ji - 1])
                g.dma_start(out=tbf[ji % 2][:, 0:w], in_=src).then_inc(lds[ji % 4], 16)
            for dj, (pk, sc, _, nw, ngrp, _) in enumerate(dq_jobs):
                if dj > 0:
                    g.wait_ge(dqv, plane_done[(dj - 1, 3)])
                g.dma_start(out=dq_pk[:, 0:nw], in_=pk).then_inc(dqls[0], 16)
                g.dma_start(out=dq_s16[0:ngrp, :], in_=sc).then_inc(dqls[1], 16)

        @block.tensor
        def _(t):
            t.wait_ge(gp, 4)
            for pi, (ji, pk, nb) in enumerate(packs):
                t.wait_ge(lds[ji % 4], 16 * (ji // 4 + 1))
                if pi >= 2:
                    t.wait_ge(ev, pi - 1)
                buf = tps[pi % 2][:].bitcast(BF16)
                src = tbf[ji % 2]
                for k in range(nb):
                    blk = pk * 4 + k
                    inst = t.transpose(buf[:, k * 128:(k + 1) * 128],
                                       src[:, blk * 128:(blk + 1) * 128], ident[:])
                inst.then_inc(mm, 1)
            dq_mm = 0
            for dpi, (dj, pl, b0, nbk) in enumerate(dq_packs):
                if pl == 0 and b0 == 0:
                    ngrp = dq_jobs[dj][4]
                    t.wait_ge(dqls[0], 16 * (dj + 1))
                    t.wait_ge(dqls[1], 16 * (dj + 1))
                    t.transpose(dq_sps_f[:].bitcast(F16)[:, 0:ngrp],
                                dq_s16[0:ngrp, :], ident16[0:ngrp, 0:ngrp]
                                ).then_inc(dqm, 1)
                    dq_mm += 1
                t.wait_ge(dqv, plane_done[(dj, pl)])
                if dpi >= 2:
                    t.wait_ge(dqe, dpi - 1)
                buf = dps[dpi % 2][:].bitcast(BF16)
                for k in range(nbk):
                    blk = b0 + k
                    inst = t.transpose(buf[:, k * 128:(k + 1) * 128],
                                       dq_w[:, blk * 128:(blk + 1) * 128], ident[:])
                inst.then_inc(dqm, 1)
                dq_mm += 1

        @block.vector
        def _(v):
            for pi, (ji, pk, nb) in enumerate(packs):
                v.wait_ge(mm, pi + 1)
                if pi >= 2:
                    v.wait_ge(sts[pi % 2], 16 * (pi // 2))
                v.tensor_copy(tev[pi % 2][:], tps[pi % 2][:].bitcast(BF16)[:, 0:512]
                              ).then_inc(ev, 1)
            dqm_seen = 0
            for dj, (pk, sc, _, nw, ngrp, _) in enumerate(dq_jobs):
                dqm_seen += 1
                v.wait_ge(dqm, dqm_seen)
                for gix in range(ngrp):
                    v.tensor_copy(dq_scf[:, gix * 32:(gix + 1) * 32],
                                  dq_sps_f[:].bitcast(F16)[:, gix:gix + 1]
                                  .to_broadcast([128, 32]))
                u16v = dq_pk[:, 0:nw].bitcast(U16)
                w2 = 2 * nw
                for pl in range(4):
                    v.tensor_scalar(dq_n[:, 0:w2], u16v, 4 * pl, 15,
                                    ALU.logical_shift_right, ALU.bitwise_and)
                    v.tensor_scalar(dq_a[:, 0:w2], dq_n[:, 0:w2], 7, 6,
                                    ALU.bitwise_and, ALU.logical_shift_left)
                    v.tensor_scalar(dq_A[:, 0:w2], dq_a[:, 0:w2], 0x3F80,
                                    None, ALU.add)
                    Abf = dq_A[:, 0:w2].bitcast(BF16)
                    v.tensor_scalar(dq_t[:, 0:w2], Abf, 2.0, -2.0, ALU.mult, ALU.add)
                    v.tensor_tensor(dq_Bv[:, 0:w2], dq_t[:, 0:w2], Abf, ALU.min)
                    v.tensor_scalar(dq_g[:, 0:w2], dq_n[:, 0:w2], 8, 12,
                                    ALU.bitwise_and, ALU.logical_shift_left)
                    v.tensor_scalar(dq_g[:, 0:w2], dq_g[:, 0:w2], 0x3F00,
                                    None, ALU.add)
                    v.tensor_tensor(dq_w[:, 0:w2], dq_Bv[:, 0:w2],
                                    dq_g[:, 0:w2].bitcast(BF16), ALU.mult)
                    vinst = v.tensor_tensor(dq_w[:, 0:w2], dq_w[:, 0:w2],
                                            dq_scf[:, 0:w2], ALU.mult)
                    vinst.then_inc(dqv, 1)
                    npk_plane = ((2 * nw) // 128 + 3) // 4
                    dqm_seen += npk_plane
                    if pl < 3 or dj < NDQ - 1:
                        v.wait_ge(dqm, dqm_seen)

        @block.scalar
        def _(sc):
            for pi, (ji, pk, nb) in enumerate(packs):
                sc.wait_ge(ev, pi + 1)
                dst, ob = jobs[ji][1], jobs[ji][2]
                sb_ap = tev[pi % 2][:].rearrange("p (k c) -> p k c", k=4)
                dr_ap = dst[pk * 4:pk * 4 + 4, :, ob * 128:(ob + 1) * 128
                            ].rearrange("k p c -> p k c")
                sc.dma_start(out=dr_ap, in_=sb_ap).then_inc(sts[pi % 2], 16)
            cum_par = {0: [0], 1: [0]}
            for i, (dj_, pl_, b0_, nbk_) in enumerate(dq_packs):
                cum_par[i % 2].append(cum_par[i % 2][-1] + nbk_)
            for dpi, (dj, pl, b0, nbk) in enumerate(dq_packs):
                sc.wait_ge(dqm, dq_pack_mm_target(dpi, dq_packs, dq_jobs))
                if dpi >= 2:
                    sc.wait_ge(dqss[dpi % 2], 16 * cum_par[dpi % 2][dpi // 2])
                sc.copy(dev[dpi % 2][:, 0:128 * nbk],
                        dps[dpi % 2][:].bitcast(BF16)[:, 0:128 * nbk]
                        ).then_inc(dqe, 1)
                sc.wait_ge(dqe, dpi + 1)
                dst = dq_jobs[dj][2]
                ocol = dq_jobs[dj][5]
                for k in range(nbk):
                    blk = b0 + k
                    dr = dst[4 * blk:4 * blk + 4, pl:pl + 125:4,
                             ocol:ocol + 128].rearrange("a b c -> (a b) c")
                    sc.dma_start(out=dr, in_=dev[dpi % 2][:, k * 128:(k + 1) * 128]
                                 ).then_inc(dqss[dpi % 2], 16)

    ctx.close()
    return nc


def dq_pack_mm_target(dpi, dq_packs, dq_jobs):
    tgt = 0
    for i, (dj, pl, b0, nbk) in enumerate(dq_packs):
        if pl == 0 and b0 == 0:
            tgt += 1
        tgt += 1
        if i == dpi:
            return tgt
    raise IndexError


if __name__ == "__main__":
    nc = build_P0()
    print("P0 built ok")



class FencedEngine:

    _CHAINED = {
        "activation", "sign", "square", "sqrt", "mul", "copy", "add",
        "memzero", "memset", "tensor_copy", "tensor_tensor", "tensor_add",
        "tensor_sub", "tensor_mul", "tensor_max", "tensor_scalar",
        "tensor_single_scalar", "tensor_scalar_add", "tensor_scalar_sub",
        "tensor_scalar_mul", "tensor_scalar_max", "tensor_scalar_min",
        "tensor_relu", "tensor_reduce", "reduce_max", "reduce_sum",
        "scalar_tensor_tensor", "tensor_tensor_scan", "select",
        "copy_predicated", "transpose", "bn_stats", "bn_aggr", "max",
        "max_index", "reciprocal", "stream_shuffle", "match_replace",
        "iota", "affine_select", "pool", "pool_avg", "pool_max",
    }

    def __init__(self, eng, sem, cnt):
        self._eng = eng
        self._sem = sem
        self._cnt = cnt

    def __getattr__(self, name):
        fn = getattr(self._eng, name)
        if name in self._CHAINED:
            def wrapped(*a, **k):
                if self._cnt[0] > 0:
                    self._eng.wait_ge(self._sem, self._cnt[0])
                inst = fn(*a, **k)
                inst.then_inc(self._sem, 1)
                self._cnt[0] += 1
                return inst
            return wrapped
        if name in ("dma_start", "indirect_dma_start", "dma_start_transpose"):
            def wrapped_dma(*a, **k):
                if self._cnt[0] > 0:
                    self._eng.wait_ge(self._sem, self._cnt[0])
                return fn(*a, **k)
            return wrapped_dma
        return fn


class Sched:
    def __init__(self, ctx):
        self.ctx = ctx
        self.ops = {k: [] for k in ("sync", "gpsimd", "tensor", "vector", "scalar")}
        self.count = {}
        self._autofence = {}
        for e in ("gpsimd", "vector", "scalar"):
            self._autofence[e] = (self.sem("af_" + e), [0])

    def sem(self, name):
        s = self.ctx.sem(name)
        self.count[name] = 0
        return s

    def at(self, sem):
        return (sem, self.count[sem.name])

    def add(self, engine, fn, waits=(), inc=None, n=1, is_dma=False):
        waits = [w for w in waits if w is not None and w[1] > 0]
        self.ops[engine].append((fn, waits, inc, n, is_dma))
        if inc is not None:
            self.count[inc.name] += n

    def make_fence(self, name="fence"):
        sem = self.sem(name)
        cnt = [0]
        def fence(eng, inst):
            inst.then_inc(sem, 1)
            cnt[0] += 1
            eng.wait_ge(sem, cnt[0])
        return fence

    def dma(self, engine, out, in_, waits=(), inc=None):
        def fn(eng):
            return eng.dma_start(out=out, in_=in_)
        self.add(engine, fn, waits, inc, 16, is_dma=True)

    def emit(self, eng_name, eng):
        raw = eng
        fenced = eng_name in self._autofence
        if fenced:
            afsem, afcnt = self._autofence[eng_name]
            eng = FencedEngine(raw, afsem, afcnt)
        for fn, waits, inc, n, is_dma in self.ops[eng_name]:
            for sem, val in waits:
                raw.wait_ge(sem, val)
            inst = fn(eng)
            if inc is not None:
                if is_dma:
                    inst.then_inc(inc, n)
                elif fenced:
                    if afcnt[0] > 0:
                        raw.wait_ge(afsem, afcnt[0])
                    raw.nop().then_inc(inc, n)
                else:
                    assert inst is not None, "op with inc returned no instruction"
                    inst.then_inc(inc, n)

    def build(self):
        with self.ctx.block() as block:
            @block.sync
            def _(e):
                self.emit("sync", e)

            @block.gpsimd
            def _(e):
                self.emit("gpsimd", e)

            @block.tensor
            def _(e):
                self.emit("tensor", e)

            @block.vector
            def _(e):
                self.emit("vector", e)

            @block.scalar
            def _(e):
                self.emit("scalar", e)


def emit_ln(sched, vsem, ssem, x_ap, h32_ap, scr, waits=(), apply_waits=()):
    def stats(v):
        v.reduce_sum(scr["cm"][:], x_ap, AX, negate=True)
        v.tensor_scalar_mul(scr["cm"][:], scr["cm"][:], 1.0 / D)
        v.tensor_tensor(scr["sq"][:], x_ap, x_ap, ALU.mult)
        v.reduce_sum(scr["cv"][:], scr["sq"][:], AX)
        v.tensor_scalar_mul(scr["cv"][:], scr["cv"][:], 1.0 / D)
        v.tensor_tensor(scr["ct"][:], scr["cm"][:], scr["cm"][:], ALU.mult)
        v.tensor_tensor(scr["cv"][:], scr["cv"][:], scr["ct"][:], ALU.subtract)
        v.tensor_scalar(scr["cv"][:], scr["cv"][:], EPS_LN, None, ALU.add)
        return v.reciprocal(scr["ct"][:], scr["cv"][:])
    sched.add("vector", stats, list(waits) + [sched.at(ssem)], vsem)

    def apply(sc):
        sc.activation(scr["cr"][:], scr["ct"][:], AF.Sqrt)
        sc.activation(scr["cb"][:], scr["cm"][:], AF.Copy, scale=scr["cr"][:])
        return sc.activation(h32_ap, x_ap, AF.Identity, scale=scr["cr"][:],
                             bias=scr["cb"][:])
    sched.add("scalar", apply, [sched.at(vsem)] + list(apply_waits), ssem)




def build_A(stage=9):
    ctx = Ctx()
    nc = ctx.nc
    d_x = nc.dram_tensor("x", [TPC, D], F32, kind="ExternalInput").ap()
    d_gw = nc.dram_tensor("gate_w", [E, D], F32, kind="ExternalInput").ap()
    d_mb = nc.dram_tensor("mbias", [1, E], F32, kind="ExternalInput").ap()
    d_h = nc.dram_tensor("h", [TPC, D], BF16, kind="ExternalOutput").ap()
    d_cnt = nc.dram_tensor("counts", [1, NTILE * E], F32, kind="ExternalOutput").ap()
    d_idx = nc.dram_tensor("idx6", [TPC, 8], F32, kind="ExternalOutput").ap()
    d_w6 = nc.dram_tensor("w6", [TPC, 8], F32, kind="ExternalOutput").ap()
    d_S = nc.dram_tensor("S", [TPC, E], F32, kind="ExternalOutput").ap()

    sch = Sched(ctx)
    lds = [sch.sem(f"ld{i}") for i in range(4)]
    vs = sch.sem("vs")
    ss = sch.sem("ss")
    ms = sch.sem("ms")
    gs = sch.sem("gs")
    sto = [sch.sem(f"st{i}") for i in range(2)]

    xb = [ctx.sbuf(f"xb{i}", [128, D], F32) for i in range(2)]
    h32 = ctx.sbuf("h32", [128, D], F32)
    hbf = ctx.sbuf("hbf", [128, D], BF16)
    hT = ctx.sbuf("hT", [128, D], F32)
    scr = {"sq": ctx.sbuf("sq", [128, D], F32)}
    for n in ("cm", "cv", "ct", "cr", "cb"):
        scr[n] = ctx.sbuf(n, [128, 1], F32)
    gw = ctx.sbuf("gw", [E, D], F32)
    gwT = ctx.sbuf("gwT", [128, 16 * E], F32)
    mb = ctx.sbuf("mb", [128, E], F32)
    ident = ctx.sbuf("identA", [128, 128], F32)
    iota32 = ctx.sbuf("iota32", [128, E], F32)
    iota_i = ctx.sbuf("iota_i", [128, E], I32)
    ones1 = ctx.sbuf("ones1", [128, 1], F32)
    lgT = ctx.sbuf("lgT", [E, 128], F32)
    lg = ctx.sbuf("lg", [128, E], F32)
    sa = ctx.sbuf("sa", [128, E], F32)
    se = ctx.sbuf("se", [128, E], F32)
    sr = ctx.sbuf("sr", [128, E], F32)
    sv = ctx.sbuf("sv", [128, E], F32)
    sbv = ctx.sbuf("sbv", [128, E], F32)
    Sm = ctx.sbuf("Sm", [128, E], F32)
    mr = ctx.sbuf("mr", [128, E], F32)
    tmp = ctx.sbuf("tmpA", [128, E], F32)
    mx8 = ctx.sbuf("mx8", [128, 8], F32)
    ix8 = ctx.sbuf("ix8", [128, 8], U32)
    ixf = ctx.sbuf("ixf", [128, 8], F32)
    wc = ctx.sbuf("wc", [128, 8], F32)
    w6b = ctx.sbuf("w6b", [128, 8], F32)
    wsum = ctx.sbuf("wsum", [128, 1], F32)
    winv = ctx.sbuf("winv", [128, 1], F32)
    cnt_acc = ctx.sbuf("cnt_acc", [1, NTILE * E], F32)
    dbg = ctx.sbuf("dbg", [128, 8], F32)
    ps_t = [ctx.psum(f"ps_t{i}", [128, 512], F32) for i in range(2)]
    ps_g = ctx.psum("ps_g", [128, 512], F32)
    ps_c = ctx.psum("ps_c", [128, 512], F32)

    def g_ident(g):
        g.memset(ident[:], 1.0).then_inc(gs, 1)
        sch.count["gs"] += 0
        return None
    sch.add("gpsimd", lambda g: g.memset(ident[:], 1.0), inc=gs)
    sch.add("gpsimd", lambda g: g.affine_select(
        out=ident[:], in_=ident[:], pattern=[[1, 128]],
        compare_op=ALU.is_equal, fill=0.0, base=0, channel_multiplier=-1),
        waits=[sch.at(gs)], inc=gs)
    sch.add("gpsimd", lambda g: g.iota(iota_i[:], [[0, 1], [1, E]], base=0,
                                       channel_multiplier=0), inc=gs)
    m_gp = sch.at(gs)
    sch.add("vector", lambda v: v.tensor_copy(iota32[:], iota_i[:]),
            waits=[m_gp], inc=vs)
    sch.add("vector", lambda v: v.memset(ones1[:], 1.0), inc=vs)
    sch.add("vector", lambda v: v.memset(wc[:], 0.0), inc=vs)
    sch.dma("sync", gw[:], d_gw, inc=lds[3])
    sch.dma("sync", mb[:], d_mb[0:1, :].to_broadcast([128, E]), inc=lds[3])
    m_gwload = sch.at(lds[3])
    for k in range(16):
        sch.add("tensor", (lambda kk: lambda t: t.transpose(
            ps_g[:, 0:E], gw[:, kk * 128:(kk + 1) * 128], ident[0:E, 0:E]))(k),
            waits=[m_gwload, m_gp, sch.at(vs)], inc=ms)
        sch.add("vector", (lambda kk: lambda v: v.tensor_copy(
            gwT[:, E * kk:E * (kk + 1)], ps_g[:, 0:E]))(k),
            waits=[sch.at(ms)], inc=vs)
    m_gwT = sch.at(vs)

    m_apply = [None] * NTILE
    m_store = [None, None]
    for t in range(NTILE):
        w = [m_apply[t - 2]] if t >= 2 else []
        sch.dma("sync", xb[t % 2][:], d_x[t * 128:(t + 1) * 128, :],
                waits=w, inc=lds[t % 2])
        m_load = sch.at(lds[t % 2])
        emit_ln(sch, vs, ss, xb[t % 2][:], h32[:], scr,
                waits=[m_load], apply_waits=[sch.at(ms)])
        m_apply[t] = sch.at(ss)
        if stage == 0:
            def dumpcols(v):
                for ci, b_ in enumerate((scr["cm"], scr["cv"], scr["ct"],
                                         scr["cr"], scr["cb"])):
                    inst = v.tensor_copy(dbg[:, ci:ci + 1], b_[:])
                return inst
            sch.add("vector", dumpcols, waits=[m_apply[t]], inc=vs)
            sch.dma("scalar", d_idx[t * 128:(t + 1) * 128, :], dbg[:],
                    waits=[sch.at(vs)], inc=sto[t % 2])
            m_store[t % 2] = sch.at(sto[t % 2])
        sch.add("scalar", lambda sc: sc.copy(hbf[:], h32[:]),
                waits=[sch.at(sto[0]), sch.at(sto[1])], inc=ss)
        sch.dma("scalar", d_h[t * 128:(t + 1) * 128, :], hbf[:],
                waits=[sch.at(ss)], inc=sto[t % 2])
        if stage < 2:
            continue
        for pk in range(4):
            def mk_pack(pp, tt):
                def f(t_):
                    for k in range(4):
                        blk = pp * 4 + k
                        inst = t_.transpose(ps_t[pp % 2][:, k * 128:(k + 1) * 128],
                                            h32[:, blk * 128:(blk + 1) * 128],
                                            ident[:])
                    return inst
                return f
            sch.add("tensor", mk_pack(pk, t), waits=[m_apply[t], sch.at(vs)],
                    inc=ms)
            sch.add("vector", (lambda pp: lambda v: v.tensor_copy(
                hT[:, pp * 512:(pp + 1) * 512], ps_t[pp % 2][:]))(pk),
                waits=[sch.at(ms)], inc=vs)
        m_hT = sch.at(vs)

        def gate_mm(t_):
            for k in range(16):
                inst = t_.matmul(ps_g[0:E, 0:128], gwT[:, E * k:E * (k + 1)],
                                 hT[:, k * 128:(k + 1) * 128],
                                 start=(k == 0), stop=(k == 15))
            return inst
        sch.add("tensor", gate_mm, waits=[m_hT, m_gwT, sch.at(vs)], inc=ms)
        sch.add("vector", lambda v: v.tensor_copy(lgT[:, 0:128], ps_g[0:E, 0:128]),
                waits=[sch.at(ms)], inc=vs)
        sch.add("tensor", lambda t_: t_.transpose(ps_c[:, 0:E], lgT[:, 0:128],
                                                  ident[0:E, 0:E]),
                waits=[sch.at(vs)], inc=ms)
        sch.add("vector", lambda v: v.tensor_copy(lg[:], ps_c[:, 0:E]),
                waits=[sch.at(ms)], inc=vs)
        if stage < 3:
            sch.dma("scalar", d_S[t * 128:(t + 1) * 128, :], lg[:],
                    waits=[sch.at(vs)], inc=sto[t % 2])
            m_store[t % 2] = sch.at(sto[t % 2])
            continue
        def sp1(sc):
            sc.activation(sa[:], lg[:], AF.Abs)
            sc.activation(se[:], sa[:], AF.Exp, scale=-1.0)
            return sc.activation(se[:], se[:], AF.Ln, bias=1.0)
        sch.add("scalar", sp1, waits=[sch.at(vs)], inc=ss)
        def sp2(v):
            v.tensor_scalar_max(sr[:], lg[:], 0.0)
            return v.tensor_tensor(sv[:], sr[:], se[:], ALU.add)
        sch.add("vector", sp2, waits=[sch.at(ss)], inc=vs)
        sch.add("scalar", lambda sc: sc.activation(sa[:], sv[:], AF.Sqrt),
                waits=[sch.at(vs)], inc=ss)
        if stage < 4:
            sch.dma("scalar", d_S[t * 128:(t + 1) * 128, :], sv[:],
                    waits=[sch.at(vs)], inc=sto[t % 2])
            m_store[t % 2] = sch.at(sto[t % 2])
            continue
        def route(v):
            v.tensor_tensor(sbv[:], sa[:], mb[:], ALU.add)
            v.max(mx8[:], sbv[:])
            v.max_index(ix8[:], mx8[:], sbv[:])
            v.tensor_copy(ixf[:], ix8[:])
            for r in range(K_TOP):
                v.tensor_scalar(mr[:], iota32[:], ixf[:, r:r + 1], None,
                                ALU.is_equal)
                if r == 0:
                    v.tensor_copy(Sm[:], mr[:])
                else:
                    v.tensor_tensor(Sm[:], Sm[:], mr[:], ALU.add)
                v.tensor_tensor(tmp[:], mr[:], sa[:], ALU.mult)
                v.reduce_sum(wc[:, r:r + 1], tmp[:], AX)
            v.reduce_sum(wsum[:], wc[:, 0:K_TOP], AX)
            v.reciprocal(winv[:], wsum[:])
            return v.tensor_scalar(w6b[:], wc[:], winv[:, 0:1], MOE_SCALE,
                                   ALU.mult, ALU.mult)
        sch.add("vector", route, waits=[sch.at(ss), m_store[1] or m_store[0] or None,
                                        m_store[0]], inc=vs)
        m_route = sch.at(vs)
        sch.add("tensor", lambda t_: t_.matmul(ps_c[0:1, 128:128 + E], ones1[:],
                                               Sm[:], start=True, stop=True),
                waits=[m_route], inc=ms)
        sch.add("vector", (lambda tt: lambda v: v.tensor_copy(
            cnt_acc[0:1, E * tt:E * (tt + 1)], ps_c[0:1, 128:128 + E]))(t),
            waits=[sch.at(ms)], inc=vs)
        sch.dma("scalar", d_idx[t * 128:(t + 1) * 128, :], ixf[:],
                waits=[m_route], inc=sto[t % 2])
        sch.dma("scalar", d_w6[t * 128:(t + 1) * 128, :], w6b[:],
                waits=[m_route], inc=sto[t % 2])
        sch.dma("scalar", d_S[t * 128:(t + 1) * 128, :], Sm[:],
                waits=[m_route], inc=sto[t % 2])
        m_store[t % 2] = sch.at(sto[t % 2])
    sch.dma("scalar", d_cnt, cnt_acc[:], waits=[sch.at(vs)], inc=sto[0])
    sch.add("sync", lambda e: None,
            waits=[sch.at(sto[0]), sch.at(sto[1])], inc=None)
    sch.build()
    ctx.close()
    return nc


SSROWS2 = 24704
TRASH2 = 24600


def build_AP():
    ctx = Ctx()
    nc = ctx.nc
    d_h = nc.dram_tensor("h", [TPC, D], BF16, kind="ExternalInput").ap()
    d_cnt = nc.dram_tensor("counts_g", [NCHUNK, E], F32, kind="ExternalInput").ap()
    d_S = nc.dram_tensor("S", [TPC, E], F32, kind="ExternalInput").ap()
    d_idx = nc.dram_tensor("idx6", [TPC, 8], F32, kind="ExternalInput").ap()
    d_w6 = nc.dram_tensor("w6", [TPC, 8], F32, kind="ExternalInput").ap()
    d_sgT = nc.dram_tensor("sgT", [16, 128, 1024], BF16, kind="ExternalInput").ap()
    d_suT = nc.dram_tensor("suT", [16, 128, 1024], BF16, kind="ExternalInput").ap()
    d_sdT = nc.dram_tensor("sdT", [8, 128, 2048], BF16, kind="ExternalInput").ap()
    d_sel = nc.dram_tensor("selmat", [64, NTILE], F32, kind="ExternalInput").ap()
    d_tok = nc.dram_tensor("tokcol", [TPC, 1], F32, kind="ExternalInput").ap()
    d_ss = nc.dram_tensor("sspart", [SSROWS2, 2], F32, kind="ExternalOutput").ap()
    d_ysh = nc.dram_tensor("ysh", [TPC, D], BF16, kind="ExternalOutput").ap()

    sch = Sched(ctx)
    lds = [sch.sem(f"ld{i}") for i in range(4)]
    vs = sch.sem("vs")
    ss = sch.sem("ss")
    ms = sch.sem("ms")
    gs = sch.sem("gs")
    sco = [sch.sem(f"sc{i}") for i in range(2)]
    sto = [sch.sem(f"st{i}") for i in range(2)]
    zs = sch.sem("zs")

    ident = ctx.sbuf("ident", [128, 128], F32)
    identb = ctx.sbuf("identb", [128, 128], BF16)
    T128u = ctx.sbuf("T128u", [128, 128], F32)
    T64u = ctx.sbuf("T64u", [64, 64], F32)
    onesr = ctx.sbuf("onesr", [1, 128], F32)
    iotac = ctx.sbuf("iotac", [128, 1], I32)
    iotacf = ctx.sbuf("iotacf", [128, 1], F32)
    iota32 = ctx.sbuf("iota32", [128, E], I32)
    iota32f = ctx.sbuf("iota32f", [128, E], F32)
    cnts = ctx.sbuf("cnts", [64, E], F32)
    bases = ctx.sbuf("bases", [64, E], F32)
    selm = ctx.sbuf("selm", [64, NTILE], F32)
    baser = ctx.sbuf("baser", [1, E], F32)
    Sb = [ctx.sbuf(f"Sb{i}", [128, E], F32) for i in range(2)]
    ixb = [ctx.sbuf(f"ixb{i}", [128, 8], F32) for i in range(2)]
    w6b = [ctx.sbuf(f"w6b{i}", [128, 8], F32) for i in range(2)]
    Pf = ctx.sbuf("Pf", [128, E], F32)
    mr = ctx.sbuf("mr", [128, E], F32)
    tmp = ctx.sbuf("tmp", [128, E], F32)
    posr = ctx.sbuf("posr", [128, 1], F32)
    vald = ctx.sbuf("vald", [128, 1], F32)
    acol = ctx.sbuf("acol", [128, 1], F32)
    tokc = [ctx.sbuf(f"tokc{i}", [128, 1], F32) for i in range(2)]
    pairs = [ctx.sbuf(f"pairs{i}", [128, 12], F32) for i in range(2)]
    dsts = [ctx.sbuf(f"dsts{i}", [128, 8], I32) for i in range(2)]
    dstf = ctx.sbuf("dstf", [128, 8], F32)
    zeros = ctx.sbuf("zeros", [128, 386], F32)
    ps_a = ctx.psum("ps_a", [128, 512], F32)
    ps_b = ctx.psum("ps_b", [128, 512], F32)

    hb = [ctx.sbuf(f"hb{i}", [128, D], BF16) for i in range(2)]
    hTs = ctx.sbuf("hTs", [128, 16 * 512], BF16)
    sgall = ctx.sbuf("sgall", [128, 2 * 16 * 1024], BF16)
    sdall = ctx.sbuf("sdall", [128, 8 * 2048], BF16)
    sgs = ctx.sbuf("sgs", [128, 512], F32)
    sis = ctx.sbuf("sis", [128, 512], F32)
    siT = ctx.sbuf("siT", [128, 8 * 512], BF16)
    yTt = ctx.sbuf("yTt", [128, 512], BF16)
    yshb = ctx.sbuf("yshb", [128, 4 * D], BF16)
    ps_t = [ctx.psum(f"ps_t{i}", [128, 512], F32) for i in range(2)]
    ps_g = ctx.psum("ps_g", [128, 512], F32)
    ps_u = ctx.psum("ps_u", [128, 512], F32)
    ps_d = ctx.psum("ps_d", [128, 512], F32)

    def aff(buf, n, op, base):
        def f(g):
            return g.affine_select(out=buf[:], in_=buf[:], pattern=[[1, n]],
                                   compare_op=op, fill=0.0, base=base,
                                   channel_multiplier=-1)
        return f
    for buf, n, op, base in ((ident, 128, ALU.is_equal, 0),
                             (identb, 128, ALU.is_equal, 0),
                             (T128u, 128, ALU.is_ge, -1),
                             (T64u, 64, ALU.is_ge, -1)):
        sch.add("gpsimd", (lambda b: lambda g: g.memset(b[:], 1.0))(buf), inc=gs)
        sch.add("gpsimd", aff(buf, n, op, base), waits=[sch.at(gs)], inc=gs)
    sch.add("gpsimd", lambda g: g.iota(iotac[:], [[0, 1]], base=0,
                                       channel_multiplier=1), inc=gs)
    sch.add("gpsimd", lambda g: g.iota(iota32[:], [[1, E]], base=0,
                                       channel_multiplier=0), inc=gs)
    m_gp = sch.at(gs)
    sch.add("vector", lambda v: v.tensor_copy(iotacf[:], iotac[:]),
            waits=[m_gp], inc=vs)
    sch.add("vector", lambda v: v.tensor_copy(iota32f[:], iota32[:]), inc=vs)
    sch.add("vector", lambda v: v.memset(onesr[:], 1.0), inc=vs)
    sch.add("vector", lambda v: v.memset(zeros[:], 0.0), inc=vs)
    m_const = sch.at(vs)
    sch.dma("scalar", d_ss.rearrange("(a b) c -> a (b c)", a=128), zeros[:],
            waits=[m_const], inc=zs)
    m_zdone = sch.at(zs)

    sch.dma("sync", cnts[:], d_cnt, inc=lds[3])
    sch.dma("sync", selm[:], d_sel, inc=lds[3])
    sch.add("tensor", lambda t: t.matmul(ps_a[0:64, 0:E], T64u[:], cnts[:],
                                         start=True, stop=True),
            waits=[sch.at(lds[3]), m_gp], inc=ms)
    sch.add("vector", lambda v: v.tensor_copy(bases[:], ps_a[0:64, 0:E]),
            waits=[sch.at(ms)], inc=vs)
    m_bases = sch.at(vs)

    m_pairs = [None] * NTILE
    m_scat = {0: None, 1: None}
    for t in range(NTILE):
        bb = t % 2
        sch.dma("sync", Sb[bb][:], d_S[t * 128:(t + 1) * 128, :],
                waits=[m_pairs[t - 2]] if t >= 2 else [], inc=lds[bb])
        sch.dma("sync", ixb[bb][:], d_idx[t * 128:(t + 1) * 128, :], inc=lds[bb])
        sch.dma("sync", w6b[bb][:], d_w6[t * 128:(t + 1) * 128, :], inc=lds[bb])
        sch.dma("sync", tokc[bb][:], d_tok[t * 128:(t + 1) * 128, :], inc=lds[bb])
        m_load = sch.at(lds[bb])
        sch.add("tensor", (lambda tt: lambda t_: t_.matmul(
            ps_a[0:1, 64:64 + E], selm[:, tt:tt + 1], bases[:],
            start=True, stop=True))(t), waits=[m_bases], inc=ms)
        sch.add("vector", lambda v: v.tensor_copy(baser[:], ps_a[0:1, 64:64 + E]),
                waits=[sch.at(ms)], inc=vs)
        def mk_prefix(b_):
            def f(t_):
                t_.matmul(ps_b[:, 0:E], T128u[:], Sb[b_][:], start=True,
                          stop=False)
                return t_.matmul(ps_b[:, 0:E], onesr[:], baser[:], start=False,
                                 stop=True)
            return f
        sch.add("tensor", mk_prefix(bb), waits=[m_load, sch.at(vs)], inc=ms)
        sch.add("vector", lambda v: v.tensor_copy(Pf[:], ps_b[:, 0:E]),
                waits=[sch.at(ms)], inc=vs)
        def mk_route(b_):
            def f(v):
                for r in range(K_TOP):
                    v.tensor_scalar(mr[:], iota32f[:], ixb[b_][:, r:r + 1], None,
                                    ALU.is_equal)
                    v.tensor_tensor(tmp[:], mr[:], Pf[:], ALU.mult)
                    v.reduce_sum(posr[:], tmp[:], AX)
                    v.tensor_scalar(vald[:], posr[:], float(CAP), None, ALU.is_lt)
                    v.tensor_tensor(pairs[b_][:, 2 * r + 1:2 * r + 2],
                                    w6b[b_][:, r:r + 1], vald[:], ALU.mult)
                    v.tensor_copy(pairs[b_][:, 2 * r:2 * r + 1], tokc[b_][:])
                    v.tensor_scalar(acol[:], ixb[b_][:, r:r + 1], float(CAP),
                                    None, ALU.mult)
                    v.tensor_tensor(acol[:], acol[:], posr[:], ALU.add)
                    v.scalar_tensor_tensor(acol[:], acol[:], -float(TRASH2),
                                           vald[:], ALU.add, ALU.mult)
                    v.tensor_scalar(dstf[:, r:r + 1], acol[:], float(TRASH2),
                                    None, ALU.add)
                return v.tensor_copy(dsts[b_][:, 0:K_TOP], dstf[:, 0:K_TOP])
            return f
        sch.add("vector", mk_route(bb), waits=[m_scat[bb], m_load], inc=vs)
        m_pairs[t] = sch.at(vs)
        def mk_scat(b_, r):
            def f(g):
                return g.indirect_dma_start(
                    out=d_ss, out_offset=bass.IndirectOffsetOnAxis(
                        dsts[b_][:, r:r + 1], 0),
                    in_=pairs[b_][:, 2 * r:2 * r + 2], in_offset=None)
            return f
        for r in range(K_TOP):
            sch.add("gpsimd", mk_scat(bb, r),
                    waits=[m_pairs[t], m_zdone], inc=sco[bb], is_dma=True, n=16)
        m_scat[bb] = sch.at(sco[bb])

    sch.dma("sync", sgall[:, 0:16384].rearrange("p (k c) -> p k c", k=16),
            d_sgT.rearrange("k p c -> p k c"), inc=lds[3])
    sch.dma("sync", sgall[:, 16384:32768].rearrange("p (k c) -> p k c", k=16),
            d_suT.rearrange("k p c -> p k c"), inc=lds[3])
    sch.dma("sync", sdall[:].rearrange("p (k c) -> p k c", k=8),
            d_sdT.rearrange("k p c -> p k c"), inc=lds[3])
    m_w = sch.at(lds[3])
    m_store = {0: None, 1: None}
    for b2 in range(2):
        m_hload = [None] * 4
        for tt in range(4):
            ti = b2 * 4 + tt
            sch.dma("sync", hb[tt % 2][:], d_h[ti * 128:(ti + 1) * 128, :],
                    waits=[sch.at(ms)] if tt >= 2 or b2 > 0 else [],
                    inc=lds[tt % 2])
            m_hload[tt] = sch.at(lds[tt % 2])
            for pk in range(4):
                def mk_tp(tt_, pp, b_):
                    def f(t_):
                        for k in range(4):
                            blk = pp * 4 + k
                            inst = t_.transpose(
                                ps_t[pp % 2][:].bitcast(BF16)[:, k * 128:(k + 1) * 128],
                                hb[b_][:, blk * 128:(blk + 1) * 128], identb[:])
                        return inst
                    return f
                sch.add("tensor", mk_tp(tt, pk, tt % 2),
                        waits=[m_hload[tt], m_gp, sch.at(vs)], inc=ms)
                def mk_ev(tt_, pp):
                    def f(v):
                        return v.tensor_copy(
                            hTs[:].rearrange("p (k c) -> p k c", k=16)
                            [:, pp * 4:(pp + 1) * 4, tt_ * 128:(tt_ + 1) * 128],
                            ps_t[pp % 2][:].bitcast(BF16)[:, 0:512]
                            .rearrange("p (k c) -> p k c", k=4))
                    return f
                sch.add("vector", mk_ev(tt, pk), waits=[sch.at(ms)], inc=vs)
        m_hT = sch.at(vs)
        for ic in range(8):
            def mk_gu(ic_, ps, coff):
                def f(t_):
                    for k in range(16):
                        inst = t_.matmul(
                            ps[:, 0:512],
                            sgall[:, coff + k * 1024 + ic_ * 128:
                                  coff + k * 1024 + ic_ * 128 + 128],
                            hTs[:, k * 512:(k + 1) * 512],
                            start=(k == 0), stop=(k == 15))
                    return inst
                return f
            sch.add("tensor", mk_gu(ic, ps_g, 0), waits=[m_hT, m_w, sch.at(ss),
                                                         sch.at(vs)], inc=ms)
            sch.add("tensor", mk_gu(ic, ps_u, 16384), inc=ms)
            sch.add("scalar", lambda sc: sc.activation(sgs[:], ps_g[:, 0:512],
                                                       AF.Silu),
                    waits=[sch.at(ms)], inc=ss)
            def mk_si(ic_):
                def f(v):
                    v.tensor_tensor(sis[:], sgs[:], ps_u[:, 0:512], ALU.mult)
                    return v.tensor_scalar(siT[:, ic_ * 512:(ic_ + 1) * 512],
                                           sis[:], SWIGLU_LIMIT, -SWIGLU_LIMIT,
                                           ALU.min, ALU.max)
                return f
            sch.add("vector", mk_si(ic), waits=[sch.at(ss)], inc=vs)
        m_si = sch.at(vs)
        for ob in range(16):
            def mk_dn(ob_):
                def f(t_):
                    for ib in range(8):
                        inst = t_.matmul(
                            ps_d[:, 0:512],
                            sdall[:, ib * 2048 + ob_ * 128:ib * 2048 + ob_ * 128 + 128],
                            siT[:, ib * 512:(ib + 1) * 512],
                            start=(ib == 0), stop=(ib == 7))
                    return inst
                return f
            sch.add("tensor", mk_dn(ob), waits=[m_si, m_w, sch.at(vs)], inc=ms)
            sch.add("vector", lambda v: v.tensor_copy(yTt[:], ps_d[:, 0:512]),
                    waits=[sch.at(ms)], inc=vs)
            def mk_tb(ob_):
                def f(t_):
                    for k in range(4):
                        inst = t_.transpose(
                            ps_t[0][:].bitcast(BF16)[:, k * 128:(k + 1) * 128],
                            yTt[:, k * 128:(k + 1) * 128], identb[:])
                    return inst
                return f
            sch.add("tensor", mk_tb(ob), waits=[sch.at(vs)], inc=ms)
            def mk_evy(ob_):
                def f(v):
                    return v.tensor_copy(
                        yshb[:].rearrange("p (t c) -> p t c", t=4)
                        [:, :, ob_ * 128:(ob_ + 1) * 128],
                        ps_t[0][:].bitcast(BF16)[:, 0:512]
                        .rearrange("p (k c) -> p k c", k=4))
                return f
            sch.add("vector", mk_evy(ob), waits=[sch.at(ms)], inc=vs)
        m_ysh = sch.at(vs)
        for tt in range(4):
            ti = b2 * 4 + tt
            sch.dma("scalar", d_ysh[ti * 128:(ti + 1) * 128, :],
                    yshb[:, tt * D:(tt + 1) * D],
                    waits=[m_ysh, m_store[b2]], inc=sto[b2])
        m_store[b2] = sch.at(sto[b2])

    sch.add("sync", lambda e: None,
            waits=[sch.at(sto[0]), sch.at(sto[1]), sch.at(sco[0]),
                   sch.at(sco[1]), sch.at(zs)], inc=None)
    sch.build()
    ctx.close()
    return nc


def build_BX():
    ctx = Ctx()
    nc = ctx.nc
    d_hf = nc.dram_tensor("hfull", [T + 8, D], BF16, kind="ExternalInput").ap()
    d_ssp = [nc.dram_tensor(f"ss{i}", [SSROWS2, 2], F32, kind="ExternalInput").ap()
             for i in range(NCORES)]
    d_wg = nc.dram_tensor("wgT", [16, 128, 4096], BF16, kind="ExternalInput").ap()
    d_wu = nc.dram_tensor("wuT", [16, 128, 4096], BF16, kind="ExternalInput").ap()
    d_wd = nc.dram_tensor("wdT", [8, 128, 8192], BF16, kind="ExternalInput").ap()
    d_rp = nc.dram_tensor("rp", [T + 8, D], F32, kind="ExternalOutput").ap()
    ssd = nc.dram_tensor("ssum", [SSROWS2, 2], F32, kind="Internal").ap()

    sch = Sched(ctx)
    lds = [sch.sem(f"ld{i}") for i in range(4)]
    vs = sch.sem("vs")
    ss_ = sch.sem("ss")
    ms = sch.sem("ms")
    gs = sch.sem("gs")
    sca = sch.sem("sca")
    sso = sch.sem("sso")
    gth = [sch.sem(f"gth{i}") for i in range(2)]
    identb = ctx.sbuf("identb", [128, 128], BF16)
    sspa = ctx.sbuf("sspa", [128, 386], F32)
    sspb = ctx.sbuf("sspb", [128, 386], F32)
    sl = ctx.sbuf("sl", [128, 12], F32)
    tokf = ctx.sbuf("tokf", [128, 6], F32)
    toki = ctx.sbuf("toki", [128, 6], I32)
    wsc = ctx.sbuf("wsc", [128, 6], F32)
    mcol = ctx.sbuf("mcol", [128, 6], F32)
    xb = ctx.sbuf("xb", [128, D], BF16)
    xbT = ctx.sbuf("xbT", [128, 16 * 768], BF16)
    wgb = ctx.sbuf("wgb", [128, 16 * 1024], BF16)
    wub = ctx.sbuf("wub", [128, 16 * 1024], BF16)
    wdb = ctx.sbuf("wdb", [128, 8 * 2048], BF16)
    sgsc = ctx.sbuf("sgsc", [128, 384], F32)
    sic = ctx.sbuf("sic", [128, 384], F32)
    siT = ctx.sbuf("siT", [128, 8 * 768], BF16)
    rbuf = ctx.sbuf("rbuf", [128, D], F32)
    ps_t = [ctx.psum(f"ps_t{i}", [128, 512], F32) for i in range(2)]
    ps_g = ctx.psum("ps_g", [128, 512], F32)
    ps_u = ctx.psum("ps_u", [128, 512], F32)
    ps_d = ctx.psum("ps_d", [128, 512], F32)

    sch.add("gpsimd", lambda g: g.memset(identb[:], 1.0), inc=gs)
    sch.add("gpsimd", lambda g: g.affine_select(
        out=identb[:], in_=identb[:], pattern=[[1, 128]],
        compare_op=ALU.is_equal, fill=0.0, base=0, channel_multiplier=-1),
        waits=[sch.at(gs)], inc=gs)
    m_gp = sch.at(gs)

    v_ss = d_ssp[0].rearrange("(a b) c -> a (b c)", a=128)
    sch.dma("sync", sspa[:], v_ss, inc=lds[3])
    for i in range(1, NCORES):
        sch.dma("sync", sspb[:], d_ssp[i].rearrange("(a b) c -> a (b c)", a=128),
                waits=[sch.at(vs)], inc=lds[3])
        sch.add("vector", lambda v: v.tensor_tensor(sspa[:], sspa[:], sspb[:],
                                                    ALU.add),
                waits=[sch.at(lds[3])], inc=vs)
    sch.dma("scalar", ssd.rearrange("(a b) c -> a (b c)", a=128), sspa[:],
            waits=[sch.at(vs)], inc=sso)
    m_ssum = sch.at(sso)

    m_sc_prev = None
    for el in range(EPC):
        sch.dma("sync", wgb[:].rearrange("p (k c) -> p k c", k=16),
                d_wg[:, :, 1024 * el:1024 * (el + 1)].rearrange("k p c -> p k c"),
                waits=[sch.at(ms)], inc=lds[0])
        sch.dma("sync", wub[:].rearrange("p (k c) -> p k c", k=16),
                d_wu[:, :, 1024 * el:1024 * (el + 1)].rearrange("k p c -> p k c"),
                inc=lds[1])
        sch.dma("sync", wdb[:].rearrange("p (k c) -> p k c", k=8),
                d_wd[:, :, 2048 * el:2048 * (el + 1)].rearrange("k p c -> p k c"),
                inc=lds[2])
        m_wload = [sch.at(lds[0]), sch.at(lds[1]), sch.at(lds[2])]
        e0 = CAP * el
        sch.dma("sync", sl[:].rearrange("p (k c) -> p k c", k=6),
                ssd[e0:e0 + CAP, :].rearrange("(k p) c -> p k c", k=6),
                waits=[m_ssum], inc=lds[3])
        def mk_cols(v):
            v.tensor_scalar(mcol[:], sl[:, 0:12:2], 0.0, None, ALU.is_equal)
            v.tensor_scalar(tokf[:], sl[:, 0:12:2], -1.0, None, ALU.add)
            v.scalar_tensor_tensor(tokf[:], mcol[:], float(T + 1), tokf[:],
                                   ALU.mult, ALU.add)
            v.tensor_copy(toki[:], tokf[:])
            return v.tensor_copy(wsc[:], sl[:, 1:12:2])
        sch.add("vector", mk_cols, waits=[sch.at(lds[3]), sch.at(sca)], inc=vs)
        m_cols = sch.at(vs)
        for blk in range(6):
            def mk_gath(b_):
                def f(g):
                    return g.indirect_dma_start(
                        out=xb[:], out_offset=None, in_=d_hf,
                        in_offset=bass.IndirectOffsetOnAxis(toki[:, b_:b_ + 1], 0))
                return f
            sch.add("gpsimd", mk_gath(blk), waits=[m_cols, sch.at(ms)],
                    inc=gth[blk % 2], is_dma=True, n=16)
            m_g = sch.at(gth[blk % 2])
            for pk in range(4):
                def mk_tp(pp, b_):
                    def f(t_):
                        for k in range(4):
                            kk = pp * 4 + k
                            inst = t_.transpose(
                                ps_t[pp % 2][:].bitcast(BF16)[:, k * 128:(k + 1) * 128],
                                xb[:, kk * 128:(kk + 1) * 128], identb[:])
                        return inst
                    return f
                sch.add("tensor", mk_tp(pk, blk), waits=[m_g, m_gp, sch.at(vs)],
                        inc=ms)
                def mk_ev(pp, b_):
                    def f(v):
                        return v.tensor_copy(
                            xbT[:].rearrange("p (k c) -> p k c", k=16)
                            [:, pp * 4:(pp + 1) * 4, b_ * 128:(b_ + 1) * 128],
                            ps_t[pp % 2][:].bitcast(BF16)[:, 0:512]
                            .rearrange("p (k c) -> p k c", k=4))
                    return f
                sch.add("vector", mk_ev(pk, blk), waits=[sch.at(ms)], inc=vs)
        m_xbT = sch.at(vs)
        for ic in range(8):
            for hh in range(2):
                s0 = 384 * hh
                def mk_gu(ic_, s0_, ps, wb):
                    def f(t_):
                        for k in range(16):
                            inst = t_.matmul(
                                ps[:, 0:384],
                                wb[:, k * 1024 + ic_ * 128:k * 1024 + ic_ * 128 + 128],
                                xbT[:, k * 768 + s0_:k * 768 + s0_ + 384],
                                start=(k == 0), stop=(k == 15))
                        return inst
                    return f
                sch.add("tensor", mk_gu(ic, s0, ps_g, wgb),
                        waits=[m_xbT] + m_wload + [sch.at(ss_), sch.at(vs)], inc=ms)
                sch.add("tensor", mk_gu(ic, s0, ps_u, wub), inc=ms)
                sch.add("scalar", lambda sc: sc.activation(
                    sgsc[:], ps_g[:, 0:384], AF.Silu), waits=[sch.at(ms)], inc=ss_)
                def mk_si(ic_, s0_):
                    def f(v):
                        v.tensor_tensor(sic[:], sgsc[:], ps_u[:, 0:384], ALU.mult)
                        return v.tensor_copy(
                            siT[:, ic_ * 768 + s0_:ic_ * 768 + s0_ + 384], sic[:])
                    return f
                sch.add("vector", mk_si(ic, s0), waits=[sch.at(ss_)], inc=vs)
        m_si = sch.at(vs)
        for blk in range(6):
            for oq in range(4):
                def mk_dn(b_, oq_):
                    def f(t_):
                        for ib in range(8):
                            inst = t_.matmul(
                                ps_d[:, 0:512],
                                siT[:, ib * 768 + b_ * 128:ib * 768 + (b_ + 1) * 128],
                                wdb[:, ib * 2048 + oq_ * 512:ib * 2048 + (oq_ + 1) * 512],
                                start=(ib == 0), stop=(ib == 7))
                        return inst
                    return f
                sch.add("tensor", mk_dn(blk, oq), waits=[m_si, sch.at(vs)], inc=ms)
                def mk_evr(b_, oq_):
                    def f(v):
                        return v.tensor_scalar(rbuf[:, oq_ * 512:(oq_ + 1) * 512],
                                               ps_d[:, 0:512], wsc[:, b_:b_ + 1],
                                               None, ALU.mult)
                    return f
                sch.add("vector", mk_evr(blk, oq),
                        waits=[sch.at(ms), sch.at(sca)], inc=vs)
            def mk_sca(b_):
                def f(g):
                    return g.indirect_dma_start(
                        out=d_rp, out_offset=bass.IndirectOffsetOnAxis(
                            toki[:, b_:b_ + 1], 0),
                        in_=rbuf[:], in_offset=None, compute_op=ALU.add)
                return f
            sch.add("gpsimd", mk_sca(blk), waits=[sch.at(vs), sch.at(sca)],
                    inc=sca, is_dma=True, n=16)
        m_sc_prev = sch.at(sca)
    sch.add("sync", lambda e: None, waits=[sch.at(sca), sch.at(sso)], inc=None)
    sch.build()
    ctx.close()
    return nc


def build_L4():
    ctx = Ctx()
    nc = ctx.nc
    d_x = nc.dram_tensor("x", [TPC, D], F32, kind="ExternalInput").ap()
    d_rp = [nc.dram_tensor(f"rp{i}", [TPC, D], F32, kind="ExternalInput").ap()
            for i in range(NCORES)]
    d_ysh = nc.dram_tensor("ysh", [TPC, D], BF16, kind="ExternalInput").ap()
    d_wq = nc.dram_tensor("Wq", [D, D], F32, kind="ExternalInput").ap()
    d_wk = nc.dram_tensor("Wk", [D, 512], F32, kind="ExternalInput").ap()
    d_wv = nc.dram_tensor("Wv", [D, 512], F32, kind="ExternalInput").ap()
    d_x1 = nc.dram_tensor("x1", [TPC, D], F32, kind="ExternalOutput").ap()
    d_QT = nc.dram_tensor("QT", [D, TPC], BF16, kind="ExternalOutput").ap()
    d_K = nc.dram_tensor("K", [TPC, 512], BF16, kind="ExternalOutput").ap()
    d_V = nc.dram_tensor("V", [TPC, 512], BF16, kind="ExternalOutput").ap()
    d_qn = nc.dram_tensor("qn", [1, NHEAD * TPC], F32, kind="ExternalOutput").ap()
    d_kn = nc.dram_tensor("kn", [1, NKV * TPC], F32, kind="ExternalOutput").ap()

    sch = Sched(ctx)
    lds = [sch.sem(f"ld{i}") for i in range(4)]
    vs = sch.sem("vs")
    ss = sch.sem("ss")
    ms = sch.sem("ms")
    gs = sch.sem("gs")
    sto = [sch.sem(f"st{i}") for i in range(2)]

    identb = ctx.sbuf("identb", [128, 128], BF16)
    ident32 = ctx.sbuf("ident32", [128, 128], F32)
    e2 = ctx.sbuf("e2", [128, 2], F32)
    xb = [ctx.sbuf(f"xb{i}", [128, D], F32) for i in range(2)]
    rb = ctx.sbuf("rb", [128, D], F32)
    racc = ctx.sbuf("racc", [128, D], F32)
    yshb = ctx.sbuf("yshb", [128, D], BF16)
    x1b = ctx.sbuf("x1b", [128, D], F32)
    h32 = ctx.sbuf("h32", [128, D], F32)
    hbf = ctx.sbuf("hbf", [128, D], BF16)
    scr = {"sq": ctx.sbuf("sq", [128, D], F32)}
    for n in ("cm", "cv", "ct", "cr", "cb"):
        scr[n] = ctx.sbuf(n, [128, 1], F32)
    hT = ctx.sbuf("hT", [128, 16 * 512], BF16)
    wbuf = ctx.sbuf("wbuf", [128, 16 * 2048], BF16)
    qtb = ctx.sbuf("qtb", [128, 512], BF16)
    sqq = ctx.sbuf("sqq", [128, 512], F32)
    qn2 = ctx.sbuf("qn2", [2, 512], F32)
    kvT = ctx.sbuf("kvT", [128, 512], BF16)
    kvtm = ctx.sbuf("kvtm", [128, 4 * 512], BF16)
    ps_t = [ctx.psum(f"ps_t{i}", [128, 512], F32) for i in range(2)]
    ps_p = ctx.psum("ps_p", [128, 512], F32)
    ps_n = ctx.psum("ps_n", [128, 512], F32)

    sch.add("gpsimd", lambda g: g.memset(identb[:], 1.0), inc=gs)
    sch.add("gpsimd", lambda g: g.affine_select(
        out=identb[:], in_=identb[:], pattern=[[1, 128]],
        compare_op=ALU.is_equal, fill=0.0, base=0, channel_multiplier=-1),
        waits=[sch.at(gs)], inc=gs)
    sch.add("gpsimd", lambda g: g.memset(ident32[:], 1.0), inc=gs)
    sch.add("gpsimd", lambda g: g.affine_select(
        out=ident32[:], in_=ident32[:], pattern=[[1, 128]],
        compare_op=ALU.is_equal, fill=0.0, base=0, channel_multiplier=-1),
        waits=[sch.at(gs)], inc=gs)
    sch.add("gpsimd", lambda g: g.memset(e2[:], 0.0), inc=gs)
    sch.add("gpsimd", lambda g: g.memset(e2[0:64, 0:1], 1.0), inc=gs)
    sch.add("gpsimd", lambda g: g.memset(e2[64:128, 1:2], 1.0), inc=gs)
    m_gp = sch.at(gs)

    sch.add("gpsimd", lambda g: g.dma_start(
        out=wbuf[:].rearrange("p (k c) -> p k c", k=16),
        in_=d_wq.rearrange("(k p) c -> p k c", p=128)), inc=lds[3], is_dma=True,
        n=16)
    m_wq = sch.at(lds[3])

    m_apply = [None] * NTILE
    m_hdone = [None] * NTILE
    for t in range(NTILE):
        bb = t % 2
        w0 = [m_apply[t - 2]] if t >= 2 else []
        sch.dma("sync", xb[bb][:], d_x[t * 128:(t + 1) * 128, :], waits=w0,
                inc=lds[bb])
        sch.dma("sync", yshb[:], d_ysh[t * 128:(t + 1) * 128, :],
                waits=[sch.at(vs)], inc=lds[bb])
        for i in range(NCORES):
            sch.dma("sync", rb[:], d_rp[i][t * 128:(t + 1) * 128, :],
                    waits=[sch.at(vs)], inc=lds[bb])
            m_l = sch.at(lds[bb])
            if i == 0:
                sch.add("vector", lambda v: v.tensor_copy(racc[:], rb[:]),
                        waits=[m_l], inc=vs)
            else:
                sch.add("vector", lambda v: v.tensor_tensor(racc[:], racc[:],
                                                            rb[:], ALU.add),
                        waits=[m_l], inc=vs)
        def mk_x1(b_):
            def f(v):
                v.tensor_tensor(racc[:], racc[:], yshb[:], ALU.add)
                return v.scalar_tensor_tensor(x1b[:], racc[:], 0.5, xb[b_][:],
                                              ALU.mult, ALU.add)
            return f
        sch.add("vector", mk_x1(bb), waits=[sch.at(lds[bb]), sch.at(sto[0]),
                                            sch.at(sto[1])], inc=vs)
        m_x1 = sch.at(vs)
        sch.dma("scalar", d_x1[t * 128:(t + 1) * 128, :], x1b[:],
                waits=[m_x1], inc=sto[bb])
        emit_ln(sch, vs, ss, x1b[:], h32[:], scr, waits=[m_x1],
                apply_waits=[sch.at(ms)])
        m_apply[t] = sch.at(ss)
        sch.add("scalar", lambda sc: sc.copy(hbf[:], h32[:]), inc=ss)
        m_hbf = sch.at(ss)
        for pk in range(4):
            def mk_tp(pp, tt):
                def f(t_):
                    for k in range(4):
                        kk = pp * 4 + k
                        inst = t_.transpose(
                            ps_t[pp % 2][:].bitcast(BF16)[:, k * 128:(k + 1) * 128],
                            hbf[:, kk * 128:(kk + 1) * 128], identb[:])
                    return inst
                return f
            sch.add("tensor", mk_tp(pk, t), waits=[m_hbf, m_gp, sch.at(vs)],
                    inc=ms)
            def mk_ev(pp, tt):
                def f(v):
                    return v.tensor_copy(
                        hT[:].rearrange("p (k c) -> p k c", k=16)
                        [:, pp * 4:(pp + 1) * 4, (tt % 4) * 128:(tt % 4) * 128 + 128],
                        ps_t[pp % 2][:].bitcast(BF16)[:, 0:512]
                        .rearrange("p (k c) -> p k c", k=4))
                return f
            sch.add("vector", mk_ev(pk, t), waits=[sch.at(ms)], inc=vs)
        m_hdone[t] = sch.at(vs)

        if t % 4 != 3:
            continue
        b2 = t // 4
        for oc in range(16):
            def mk_q(oc_):
                def f(t_):
                    for k in range(16):
                        inst = t_.matmul(
                            ps_p[:, 0:512],
                            wbuf[:, k * 2048 + oc_ * 128:k * 2048 + oc_ * 128 + 128],
                            hT[:, k * 512:(k + 1) * 512],
                            start=(k == 0), stop=(k == 15))
                    return inst
                return f
            sch.add("tensor", mk_q(oc), waits=[m_hdone[t], m_wq, sch.at(ss),
                                               sch.at(vs)], inc=ms)
            sch.add("scalar", lambda sc: sc.activation(qtb[:], ps_p[:, 0:512],
                                                       AF.Copy, scale=0.125),
                    waits=[sch.at(ms), sch.at(sto[0]), sch.at(sto[1])], inc=ss)
            def mk_store_q(oc_, b2_):
                def f(sc):
                    return sc.dma_start(
                        out=d_QT[oc_ * 128:(oc_ + 1) * 128,
                                 b2_ * 512:(b2_ + 1) * 512], in_=qtb[:])
                return f
            sch.add("scalar", mk_store_q(oc, b2), waits=[sch.at(ss)],
                    inc=sto[oc % 2], is_dma=True, n=16)
            sch.add("scalar", lambda sc: sc.activation(sqq[:], qtb[:], AF.Square),
                    inc=ss)
            def mk_qn(oc_, b2_):
                def f(t_):
                    return t_.matmul(ps_n[0:2, 0:512], e2[:], sqq[:],
                                     start=True, stop=True)
                return f
            sch.add("tensor", mk_qn(oc, b2), waits=[sch.at(ss), sch.at(vs)], inc=ms)
            sch.add("vector", lambda v: v.tensor_copy(qn2[:], ps_n[0:2, 0:512]),
                    waits=[sch.at(ms), sch.at(sto[0]), sch.at(sto[1])], inc=vs)
            for hh in range(2):
                hq = oc * 2 + hh
                def mk_sqn(hh_, hq_, b2_):
                    def f(sc):
                        return sc.dma_start(
                            out=d_qn[0:1, hq_ * TPC + b2_ * 512:
                                     hq_ * TPC + b2_ * 512 + 512],
                            in_=qn2[hh_:hh_ + 1, :])
                    return f
                sch.add("scalar", mk_sqn(hh, hq, b2), waits=[sch.at(vs)],
                        inc=sto[hh], is_dma=True, n=16)
        for which, dw, dout in (("k", d_wk, d_K), ("v", d_wv, d_V)):
            def mk_wload(dw_):
                def f(g):
                    return g.dma_start(
                        out=wbuf[:, 0:16 * 512].rearrange("p (k c) -> p k c", k=16),
                        in_=dw_.rearrange("(k p) c -> p k c", p=128))
                return f
            sch.add("gpsimd", mk_wload(dw), waits=[sch.at(ms)], inc=lds[3],
                    is_dma=True, n=16)
            m_wkv = sch.at(lds[3])
            for oc in range(4):
                def mk_kv(oc_):
                    def f(t_):
                        for k in range(16):
                            inst = t_.matmul(
                                ps_p[:, 0:512],
                                wbuf[:, k * 512 + oc_ * 128:k * 512 + oc_ * 128 + 128],
                                hT[:, k * 512:(k + 1) * 512],
                                start=(k == 0), stop=(k == 15))
                        return inst
                    return f
                sch.add("tensor", mk_kv(oc), waits=[m_wkv, sch.at(ss), sch.at(vs)],
                        inc=ms)
                sch.add("vector", lambda v: v.tensor_copy(kvT[:], ps_p[:, 0:512]),
                        waits=[sch.at(ms)], inc=vs)
                if which == "k":
                    sch.add("scalar", lambda sc: sc.activation(sqq[:], kvT[:],
                                                               AF.Square),
                            waits=[sch.at(vs)], inc=ss)
                    sch.add("tensor", lambda t_: t_.matmul(
                        ps_n[0:2, 0:512], e2[:], sqq[:], start=True, stop=True),
                        waits=[sch.at(ss), sch.at(vs)], inc=ms)
                    sch.add("vector", lambda v: v.tensor_copy(qn2[:],
                                                              ps_n[0:2, 0:512]),
                            waits=[sch.at(ms), sch.at(sto[0]), sch.at(sto[1])],
                            inc=vs)
                    for hh in range(2):
                        hk = oc * 2 + hh
                        def mk_skn(hh_, hk_, b2_):
                            def f(sc):
                                return sc.dma_start(
                                    out=d_kn[0:1, hk_ * TPC + b2_ * 512:
                                             hk_ * TPC + b2_ * 512 + 512],
                                    in_=qn2[hh_:hh_ + 1, :])
                            return f
                        sch.add("scalar", mk_skn(hh, hk, b2),
                                waits=[sch.at(vs)], inc=sto[hh], is_dma=True,
                                n=16)
                def mk_tb(oc_):
                    def f(t_):
                        for k in range(4):
                            inst = t_.transpose(
                                ps_t[0][:].bitcast(BF16)[:, k * 128:(k + 1) * 128],
                                kvT[:, k * 128:(k + 1) * 128], identb[:])
                        return inst
                    return f
                sch.add("tensor", mk_tb(oc), waits=[sch.at(vs)], inc=ms)
                def mk_evk(oc_):
                    def f(v):
                        return v.tensor_copy(
                            kvtm[:].rearrange("p (t c) -> p t c", t=4)
                            [:, :, oc_ * 128:(oc_ + 1) * 128],
                            ps_t[0][:].bitcast(BF16)[:, 0:512]
                            .rearrange("p (k c) -> p k c", k=4))
                    return f
                sch.add("vector", mk_evk(oc), waits=[sch.at(ms), sch.at(sto[0]),
                                                     sch.at(sto[1])], inc=vs)
            m_kv = sch.at(vs)
            for tt in range(4):
                ti = b2 * 4 + tt
                def mk_skv(dout_, ti_, tt_):
                    def f(sc):
                        return sc.dma_start(
                            out=dout_[ti_ * 128:(ti_ + 1) * 128, :],
                            in_=kvtm[:, tt_ * 512:(tt_ + 1) * 512])
                    return f
                sch.add("scalar", mk_skv(dout, ti, tt), waits=[m_kv],
                        inc=sto[tt % 2], is_dma=True, n=16)
        if b2 == 0:
            sch.add("gpsimd", lambda g: g.dma_start(
                out=wbuf[:].rearrange("p (k c) -> p k c", k=16),
                in_=d_wq.rearrange("(k p) c -> p k c", p=128)),
                waits=[sch.at(ms)], inc=lds[3], is_dma=True, n=16)
            m_wq = sch.at(lds[3])

    sch.add("sync", lambda e: None, waits=[sch.at(sto[0]), sch.at(sto[1])],
            inc=None)
    sch.build()
    ctx.close()
    return nc



BOUND = [8, 16, 24, 32]
PACKS = [bb // 4 for bb in BOUND]
PACK_OFF = [0, 2, 6, 12]
NPACKS = sum(PACKS)


def emit_attention(sch, ctx, d_QT, d_K, d_V, d_qn, d_knf, d_mask, prefix=""):
    P = prefix
    lda = [sch.sem(P + f"lda{i}") for i in range(2)]
    vs = sch.sem(P + "avs")
    ss = sch.sem(P + "ass")
    ms = sch.sem(P + "ams")

    qt_sb = ctx.sbuf(P + "qt_sb", [128, 16 * TPC], BF16)
    attn_sb = ctx.sbuf(P + "attn_sb", [128, 16 * TPC], BF16)
    _af = attn_sb[:].bitcast(F32)
    qn_sb = _af[0:NHEAD, 0:TPC]
    knf = _af[0:16, 2048:2048 + S]
    qnmaxc = ctx.sbuf(P + "qnmaxc", [NHEAD, 1], F32)
    knmaxc = ctx.sbuf(P + "knmaxc", [16, 1], F32)
    qnmax = ctx.sbuf(P + "qnmax", [1, NHEAD], F32)
    knmax = ctx.sbuf(P + "knmax", [1, 16], F32)
    identf = ctx.sbuf(P + "identf", [32, 32], F32)
    vplus = ctx.sbuf(P + "vplus", [128, 32 * 520], BF16)
    ktb = ctx.sbuf(P + "ktb", [128, S], BF16)
    masks = ctx.sbuf(P + "masks", [128, NPACKS * 512], BF16)
    vstage = ctx.sbuf(P + "vstage", [128, 512], BF16)
    mcol = ctx.sbuf(P + "mcol", [1, 4], F32)
    negM = ctx.sbuf(P + "negM", [128, 1], F32)
    identbA = ctx.sbuf(P + "identbA", [128, 128], BF16)
    negMall = ctx.sbuf(P + "negMall", [128, 64], F32)
    mr1 = ctx.sbuf(P + "mr1", [1, 64], F32)
    mr2 = ctx.sbuf(P + "mr2", [1, 16], F32)
    onesr1 = ctx.sbuf(P + "onesr1", [1, 128], F32)
    ones64r = ctx.sbuf(P + "ones64r", [1, 64], F32)
    pbuf = [ctx.sbuf(P + f"pbuf{i}", [128, 512], BF16) for i in range(2)]
    rcol = ctx.sbuf(P + "rcol", [1, 128], F32)
    obu = ctx.sbuf(P + "obu", [64, 128], F32)
    ps_s = [ctx.psum(P + f"ps_s{i}", [128, 512], F32) for i in range(2)]
    ps_o = [ctx.psum(P + f"ps_o{i}", [128, 512], F32) for i in range(2)]
    ps_r = ctx.psum(P + "ps_r", [128, 512], F32)

    gsA = sch.sem(P + "gsA")
    sch.add("gpsimd", lambda g: g.memset(identf[:], 1.0), inc=gsA)
    sch.add("gpsimd", lambda g: g.affine_select(
        out=identf[:], in_=identf[:], pattern=[[1, 32]],
        compare_op=ALU.is_equal, fill=0.0, base=0, channel_multiplier=-1),
        inc=gsA)
    sch.add("gpsimd", lambda g: g.memset(identbA[:], 1.0), inc=gsA)
    sch.add("gpsimd", lambda g: g.affine_select(
        out=identbA[:], in_=identbA[:], pattern=[[1, 128]],
        compare_op=ALU.is_equal, fill=0.0, base=0, channel_multiplier=-1),
        waits=[sch.at(gsA)], inc=gsA)
    m_idf = sch.at(gsA)
    sch.add("vector", lambda v: v.memset(onesr1[:], 1.0), inc=vs)
    sch.add("vector", lambda v: v.memset(ones64r[:], 1.0), inc=vs)

    def mk_vones(v):
        inst = None
        for ch in range(32):
            for h in range(8):
                inst = v.memset(vplus[:, ch * 520 + h * 65 + 64:
                                      ch * 520 + h * 65 + 65], 1.0)
        return inst
    sch.add("vector", mk_vones, inc=vs)
    for h in range(NKV):
        for g in range(G):
            cb = 4 * (h // 2) + g
            rb = 64 * (h % 2)
            f0 = 256 * h + 64 * g
            sch.dma("sync", qt_sb[rb:rb + 64, cb * TPC:(cb + 1) * TPC],
                    d_QT[f0:f0 + 64, :], inc=lda[0])
    sch.dma("sync", qn_sb, d_qn.rearrange("a (h q) -> (a h) q", q=TPC),
            inc=lda[0])
    sch.dma("sync", knf, d_knf.rearrange("a (r q) -> (a r) q", q=S),
            inc=lda[0])
    sch.dma("sync", masks[:], d_mask, inc=lda[0])
    m_qt = sch.at(lda[0])
    sch.add("vector", lambda v: v.reduce_max(qnmaxc[:], qn_sb, AX),
            waits=[m_qt], inc=vs)
    sch.add("vector", lambda v: v.reduce_max(knmaxc[:], knf, AX), inc=vs)
    sch.add("tensor", lambda t_: t_.transpose(ps_r[0:1, 0:NHEAD], qnmaxc[:],
                                              identf[:]),
            waits=[sch.at(vs), m_idf], inc=ms)
    sch.add("vector", lambda v: v.tensor_copy(qnmax[:], ps_r[0:1, 0:NHEAD]),
            waits=[sch.at(ms)], inc=vs)
    sch.add("tensor", lambda t_: t_.transpose(ps_r[0:1, 64:64 + 16], knmaxc[:],
                                              identf[0:16, 0:16]),
            waits=[sch.at(vs)], inc=ms)
    sch.add("vector", lambda v: v.tensor_copy(knmax[:], ps_r[0:1, 64:64 + 16]),
            waits=[sch.at(ms)], inc=vs)
    m_maxes = sch.at(vs)
    def mk_mrows(v):
        v.tensor_copy(mr1[:].rearrange("a (b q) -> a b q", b=2),
                      qnmax[:].rearrange("a (c q) -> a c q", c=1)
                      .to_broadcast([1, 2, NHEAD]))
        return v.tensor_copy(mr2[:].rearrange("a (b h) -> a b h", b=2),
                             knmax[:].rearrange("a (h b) -> a b h", b=2))
    sch.add("vector", mk_mrows, waits=[m_maxes], inc=vs)
    def mk_mprod(v):
        return v.tensor_tensor(
            mr1[:].rearrange("a (p g) -> a p g", g=G),
            mr1[:].rearrange("a (p g) -> a p g", g=G),
            mr2[:].rearrange("a (p g) -> a p g", g=1).to_broadcast([1, 16, G]),
            ALU.mult)
    sch.add("vector", mk_mprod, inc=vs)
    sch.add("scalar", lambda sc: sc.activation(mr1[:], mr1[:], AF.Sqrt),
            waits=[sch.at(vs)], inc=ss)
    sch.add("vector", lambda v: v.tensor_scalar(mr1[:], mr1[:], -1.0001, None,
                                                ALU.mult),
            waits=[sch.at(ss)], inc=vs)
    sch.add("tensor", lambda t_: t_.matmul(ps_r[0:128, 0:64], onesr1[:],
                                           mr1[:], start=True, stop=True),
            waits=[sch.at(vs)], inc=ms)
    sch.add("vector", lambda v: v.tensor_copy(negMall[:], ps_r[0:128, 0:64]),
            waits=[sch.at(ms)], inc=vs)
    m_negmall = sch.at(vs)

    m_exp2 = {0: None, 1: None}
    m_pv2 = {0: None, 1: None}
    m_oev2 = {0: None, 1: None}
    otile = 0
    for b in range(B):
        for ch in range(32):
            def mk_vl(b_, ch_):
                def f(sy):
                    return sy.dma_start(
                        out=vstage[:],
                        in_=d_V[b_ * S + ch_ * 128:b_ * S + (ch_ + 1) * 128, :])
                return f
            sch.add("sync", mk_vl(b, ch), waits=[sch.at(vs), sch.at(ms)],
                    inc=lda[1], is_dma=True, n=16)
            def mk_vc(ch_):
                def f(v):
                    inst = None
                    for h in range(8):
                        inst = v.tensor_copy(
                            vplus[:, ch_ * 520 + h * 65:ch_ * 520 + h * 65 + 64],
                            vstage[:, h * 64:(h + 1) * 64])
                    return inst
                return f
            sch.add("vector", mk_vc(ch), waits=[sch.at(lda[1])], inc=vs)
        m_vplus = sch.at(vs)
        for hp in range(4):
            def mk_kt(b_, hp_):
                def f(sy):
                    return sy.dma_start_transpose(
                        ktb[:], d_K[b_ * S:(b_ + 1) * S,
                                    hp_ * 128:(hp_ + 1) * 128])
                return f
            sch.add("sync", mk_kt(b, hp), waits=[sch.at(ms)], inc=lda[0],
                    is_dma=True, n=16)
            m_kt = sch.at(lda[0])
            for h2 in range(2):
                h = 2 * hp + h2
                krow = 64 * h2
                for g in range(4):
                    hq = 4 * h + g
                    prow = 64 * (h % 2)
                    qdb = 4 * (h // 2) + g
                    midx = b * 32 + hq
                    for s in range(4):
                        qc0 = (4 * b + s) * 128
                        po = otile % 2
                        otile += 1
                        npk = PACKS[s]
                        for pg in range(npk):
                            pp = pg % 2
                            def mk_sc(pp_, pg_, krow_, prow_, qdb_, qc0_, s_):
                                def f(t_):
                                    inst = None
                                    for kk in range(4):
                                        ch = pg_ * 4 + kk
                                        t_.matmul(
                                            ps_s[pp_][:, kk * 128:(kk + 1) * 128],
                                            ktb[krow_:krow_ + 64,
                                                ch * 128:(ch + 1) * 128],
                                            qt_sb[prow_:prow_ + 64,
                                                  qdb_ * TPC + qc0_:
                                                  qdb_ * TPC + qc0_ + 128],
                                            start=True, stop=False)
                                        inst = t_.matmul(
                                            ps_s[pp_][:, kk * 128:(kk + 1) * 128],
                                            identbA[:],
                                            masks[:, (PACK_OFF[s_] + pg_) * 512 +
                                                  kk * 128:
                                                  (PACK_OFF[s_] + pg_) * 512 +
                                                  (kk + 1) * 128],
                                            start=False, stop=True)
                                    return inst
                                return f
                            w0 = [m_kt, m_negmall, m_vplus, m_idf]
                            if m_exp2[pp] is not None:
                                w0.append(m_exp2[pp])
                            sch.add("tensor",
                                    mk_sc(pp, pg, krow, prow, qdb, qc0, s),
                                    waits=w0, inc=ms)
                            def mk_exp(pp_, midx_):
                                def f(sc):
                                    return sc.activation(
                                        pbuf[pp_][:], ps_s[pp_][:], AF.Exp,
                                        bias=negMall[:, midx_:midx_ + 1])
                                return f
                            w1 = [sch.at(ms)]
                            if m_pv2[pp] is not None:
                                w1.append(m_pv2[pp])
                            sch.add("scalar", mk_exp(pp, midx), waits=w1,
                                    inc=ss)
                            m_exp2[pp] = sch.at(ss)
                            def mk_pv(po_, pp_, pg_, h_, npk_):
                                def f(t_):
                                    inst = None
                                    for kk in range(4):
                                        ch = pg_ * 4 + kk
                                        inst = t_.matmul(
                                            ps_o[po_][0:65, 0:128],
                                            vplus[:, ch * 520 + h_ * 65:
                                                  ch * 520 + (h_ + 1) * 65],
                                            pbuf[pp_][:, kk * 128:(kk + 1) * 128],
                                            start=(pg_ == 0 and kk == 0),
                                            stop=(pg_ == npk_ - 1 and kk == 3))
                                    return inst
                                return f
                            w2 = [sch.at(ss)]
                            if pg == 0 and m_oev2[po] is not None:
                                w2.append(m_oev2[po])
                            sch.add("tensor", mk_pv(po, pp, pg, h, npk),
                                    waits=w2, inc=ms)
                            m_pv2[pp] = sch.at(ms)
                        def mk_rc(po_):
                            def f(v):
                                return v.reciprocal(rcol[:],
                                                    ps_o[po_][64:65, 0:128])
                            return f
                        sch.add("vector", mk_rc(po), waits=[sch.at(ms)], inc=vs)
                        sch.add("tensor", lambda t_: t_.matmul(
                            ps_r[0:64, 0:128], ones64r[:], rcol[:],
                            start=True, stop=True), waits=[sch.at(vs)], inc=ms)
                        def mk_ev(po_):
                            def f(v):
                                return v.tensor_copy(obu[:],
                                                     ps_o[po_][0:64, 0:128])
                            return f
                        sch.add("vector", mk_ev(po), waits=[sch.at(ms)], inc=vs)
                        m_oev2[po] = sch.at(vs)
                        def mk_nrm(prow_, qdb_, qc0_):
                            def f(v):
                                return v.tensor_tensor(
                                    attn_sb[prow_:prow_ + 64,
                                            qdb_ * TPC + qc0_:
                                            qdb_ * TPC + qc0_ + 128],
                                    obu[:], ps_r[0:64, 0:128], ALU.mult)
                            return f
                        sch.add("vector", mk_nrm(prow, qdb, qc0),
                                waits=[sch.at(ms)], inc=vs)
    bufs = {"qt_sb": qt_sb, "vplus": vplus, "ktb": ktb, "masks": masks,
            "ps_s": ps_s, "ps_o": ps_o, "ps_r": ps_r, "identf": identf}
    return attn_sb, sch.at(vs), (vs, ss, ms), bufs


def build_L5():
    ctx = Ctx()
    nc = ctx.nc
    d_x1 = nc.dram_tensor("x1", [TPC, D], F32, kind="ExternalInput").ap()
    d_QT = nc.dram_tensor("QT", [D, TPC], BF16, kind="ExternalInput").ap()
    d_qn = nc.dram_tensor("qn", [1, NHEAD * TPC], F32, kind="ExternalInput").ap()
    d_K = nc.dram_tensor("K", [T, 512], BF16, kind="ExternalInput").ap()
    d_V = nc.dram_tensor("V", [T, 512], BF16, kind="ExternalInput").ap()
    d_knf = nc.dram_tensor("knf", [1, NKV * T], F32, kind="ExternalInput").ap()
    d_msk = nc.dram_tensor("amask", [128, NPACKS * 512], BF16,
                           kind="ExternalInput").ap()
    d_wo = nc.dram_tensor("Wo", [D, D], F32, kind="ExternalInput").ap()
    d_x2 = nc.dram_tensor("x2", [TPC, D], F32, kind="ExternalOutput").ap()

    sch = Sched(ctx)
    attn_sb, m_attn, (avs, ass, ams), _ab = emit_attention(
        sch, ctx, d_QT, d_K, d_V, d_qn, d_knf, d_msk)

    ldo = [sch.sem(f"ldo{i}") for i in range(2)]
    vso = sch.sem("vso")
    sso = sch.sem("sso")
    mso = sch.sem("mso")
    sto = [sch.sem(f"sto{i}") for i in range(2)]
    identb = ctx.sbuf("identbO", [128, 128], BF16)
    wbuf = ctx.sbuf("wbufO", [128, 16 * 1024], BF16)
    x1b = ctx.sbuf("x1bO", [128, D], F32)
    oT = ctx.sbuf("oTO", [128, 512], BF16)
    otm = ctx.sbuf("otmO", [128, 4 * D], BF16)
    otf = ctx.sbuf("otfO", [128, D], F32)
    ps_p = ctx.psum("ps_pO", [128, 512], F32)
    ps_t2 = ctx.psum("ps_t2O", [128, 512], F32)

    gso = sch.sem("gso")
    sch.add("gpsimd", lambda g: g.memset(identb[:], 1.0), inc=gso)
    sch.add("gpsimd", lambda g: g.affine_select(
        out=identb[:], in_=identb[:], pattern=[[1, 128]],
        compare_op=ALU.is_equal, fill=0.0, base=0, channel_multiplier=-1),
        inc=gso)
    m_id = sch.at(gso)
    for b2 in range(2):
        for oh in range(2):
            for cb in range(16):
                a_, g_ = cb // 4, cb % 4
                for h2 in range(2):
                    f0 = 512 * a_ + 256 * h2 + 64 * g_
                    def mk_wl(cb_, h2_, f0_, oh_):
                        def f(g):
                            return g.dma_start(
                                out=wbuf[64 * h2_:64 * h2_ + 64,
                                         cb_ * 1024:(cb_ + 1) * 1024],
                                in_=d_wo[f0_:f0_ + 64,
                                         oh_ * 1024:(oh_ + 1) * 1024])
                        return f
                    sch.add("gpsimd", mk_wl(cb, h2, f0, oh),
                            waits=[sch.at(mso)], inc=ldo[0], is_dma=True, n=16)
            m_wo = sch.at(ldo[0])
            for oc8 in range(8):
                oc = oh * 8 + oc8
                def mk_o(oc8_, b2_):
                    def f(t_):
                        inst = None
                        for k in range(16):
                            inst = t_.matmul(
                                ps_p[:, 0:512],
                                wbuf[:, k * 1024 + oc8_ * 128:
                                     k * 1024 + oc8_ * 128 + 128],
                                attn_sb[:, k * TPC + b2_ * 512:
                                        k * TPC + b2_ * 512 + 512],
                                start=(k == 0), stop=(k == 15))
                        return inst
                    return f
                sch.add("tensor", mk_o(oc8, b2), waits=[m_attn, m_wo, m_id,
                                                        sch.at(vso)], inc=mso)
                sch.add("vector", lambda v: v.tensor_copy(oT[:], ps_p[:, 0:512]),
                        waits=[sch.at(mso)], inc=vso)
                def mk_tb(oc_):
                    def f(t_):
                        inst = None
                        for k in range(4):
                            inst = t_.transpose(
                                ps_t2[:].bitcast(BF16)[:, k * 128:(k + 1) * 128],
                                oT[:, k * 128:(k + 1) * 128], identb[:])
                        return inst
                    return f
                sch.add("tensor", mk_tb(oc), waits=[sch.at(vso)], inc=mso)
                def mk_evo(oc_):
                    def f(v):
                        return v.tensor_copy(
                            otm[:].rearrange("p (t c) -> p t c", t=4)
                            [:, :, oc_ * 128:(oc_ + 1) * 128],
                            ps_t2[:].bitcast(BF16)[:, 0:512]
                            .rearrange("p (k c) -> p k c", k=4))
                    return f
                sch.add("vector", mk_evo(oc), waits=[sch.at(mso), sch.at(sto[0]),
                                                     sch.at(sto[1])], inc=vso)
        m_otm = sch.at(vso)
        for tt in range(4):
            ti = b2 * 4 + tt
            def mk_x2(tt_):
                def f(v):
                    v.tensor_copy(otf[:], otm[:, tt_ * D:(tt_ + 1) * D])
                    return v.tensor_tensor(otf[:], otf[:], x1b[:], ALU.add)
                return f
            sch.dma("sync", x1b[:], d_x1[ti * 128:(ti + 1) * 128, :],
                    waits=[m_otm, sch.at(vso)], inc=ldo[1])
            sch.add("vector", mk_x2(tt), waits=[sch.at(ldo[1]),
                                                sch.at(sto[0]), sch.at(sto[1])],
                    inc=vso)
            def mk_sx2(ti_):
                def f(sc):
                    return sc.dma_start(out=d_x2[ti_ * 128:(ti_ + 1) * 128, :],
                                        in_=otf[:])
                return f
            sch.add("scalar", mk_sx2(ti), waits=[sch.at(vso)],
                    inc=sto[tt % 2], is_dma=True, n=16)
    sch.add("sync", lambda e: None, waits=[sch.at(sto[0]), sch.at(sto[1])],
            inc=None)
    sch.build()
    ctx.close()
    return nc


def mk_ffn1_bufs(ctx, sch, prefix):
    P = prefix
    b = {}
    b["lds"] = [sch.sem(P + f"fld{i}") for i in range(2)]
    b["vs"] = sch.sem(P + "fvs")
    b["ss"] = sch.sem(P + "fss")
    b["ms"] = sch.sem(P + "fms")
    b["wslab"] = [ctx.sbuf(P + f"wslab{i}", [128, 2 * 16 * 128], BF16)
                  for i in range(2)]
    b["dslab"] = [ctx.sbuf(P + f"dslab{i}", [128, 44 * 128], BF16)
                  for i in range(2)]
    interT = ctx.sbuf(P + "interT", [128, 44 * 512], BF16)
    b["ic"] = lambda c: interT[:, c * 512:(c + 1) * 512]
    b["sgs"] = ctx.sbuf(P + "fsgs", [128, 512], F32)
    b["sis"] = ctx.sbuf(P + "fsis", [128, 512], F32)
    b["ybf"] = ctx.sbuf(P + "fybf", [128, 16 * 512], BF16)
    b["sq"] = ctx.sbuf(P + "fsq", [128, 512], F32)
    b["ones1"] = ctx.sbuf(P + "fones", [128, 1], F32)
    b["onesr"] = ctx.sbuf(P + "fonesr", [1, 128], F32)
    b["rrow"] = ctx.sbuf(P + "frrow", [1, 512], F32)
    b["ysc"] = ctx.sbuf(P + "fysc", [128, 512], BF16)
    b["ytm"] = ctx.sbuf(P + "fytm", [128, 4 * D], BF16)
    b["ps_g"] = ctx.psum(P + "fps_g", [128, 512], F32)
    b["ps_u"] = ctx.psum(P + "fps_u", [128, 512], F32)
    b["ps_d"] = ctx.psum(P + "fps_d", [128, 512], F32)
    b["ps_r"] = ctx.psum(P + "fps_r", [128, 512], F32)
    b["ps_t"] = ctx.psum(P + "fps_t", [128, 512], F32)
    sch.add("vector", lambda v: v.memset(b["ones1"][:], 1.0), inc=b["vs"])
    sch.add("vector", lambda v: v.memset(b["onesr"][:], 1.0), inc=b["vs"])
    return b


def emit_ffn1_block(sch, b, identb, m_ident, d_wgu, d_wdn, m_hT, hT,
                    m_prev=None, m_start=None):
    vs, ss, ms, lds = b["vs"], b["ss"], b["ms"], b["lds"]
    m_wmm = b.setdefault("m_wmm", {0: None, 1: None})
    m_dmm = b.setdefault("m_dmm", {0: None, 1: None})
    NOC = I_FFN1 // 128
    for ocp in range(NOC):
        sl = b["wslab"][ocp % 2]
        w0 = [sch.at(ms)] if m_wmm[ocp % 2] is None else [m_wmm[ocp % 2]]
        if ocp < 2 and m_start is not None:
            w0.extend(m_start if isinstance(m_start, list) else [m_start])
        sch.dma("gpsimd", sl[:, 0:2048].rearrange("p (k c) -> p k c", k=16),
                d_wgu[:, :, 128 * ocp:128 * (ocp + 1)].rearrange("k p c -> p k c"),
                waits=w0, inc=lds[ocp % 2])
        sch.dma("gpsimd", sl[:, 2048:4096].rearrange("p (k c) -> p k c", k=16),
                d_wgu[:, :, I_FFN1 + 128 * ocp:I_FFN1 + 128 * (ocp + 1)]
                .rearrange("k p c -> p k c"), inc=lds[ocp % 2])
        m_w = sch.at(lds[ocp % 2])
        m_silu_p = b.get("m_silu_p")
        m_si_p = b.get("m_si_p")
        def mk_mm(sl_, ps_, coff_):
            def f(t_):
                inst = None
                for k in range(16):
                    inst = t_.matmul(
                        ps_[:, 0:512],
                        sl_[:, coff_ + k * 128:coff_ + (k + 1) * 128],
                        hT[:, k * 512:(k + 1) * 512],
                        start=(k == 0), stop=(k == 15))
                return inst
            return f
        wg0 = [m_w, m_hT]
        if m_silu_p is not None:
            wg0.append(m_silu_p)
        sch.add("tensor", mk_mm(sl, b["ps_g"], 0), waits=wg0, inc=ms)
        m_mmg = sch.at(ms)
        wu0 = [] if m_si_p is None else [m_si_p]
        sch.add("tensor", mk_mm(sl, b["ps_u"], 2048), waits=wu0, inc=ms)
        m_mmu = sch.at(ms)
        m_wmm[ocp % 2] = m_mmu
        sch.add("scalar", lambda sc: sc.activation(b["sgs"][:], b["ps_g"][:, 0:512],
                                                   AF.Silu),
                waits=([m_mmg] if m_si_p is None else [m_mmg, m_si_p]),
                inc=ss)
        b["m_silu_p"] = sch.at(ss)
        def mk_si(ocp_):
            def f(v):
                v.tensor_tensor(b["sis"][:], b["sgs"][:], b["ps_u"][:, 0:512],
                                ALU.mult)
                return v.tensor_copy(b["ic"](ocp_), b["sis"][:])
            return f
        sch.add("vector", mk_si(ocp), waits=[sch.at(ss), m_mmu], inc=vs)
        b["m_si_p"] = sch.at(vs)
    m_int = sch.at(vs)
    for oc in range(16):
        dsl = b["dslab"][oc % 2]
        def mk_dl(oc_, dsl_):
            def f(g):
                return g.dma_start(
                    out=dsl_[:].rearrange("p (k c) -> p k c", k=44),
                    in_=d_wdn[:, :, 128 * oc_:128 * (oc_ + 1)]
                    .rearrange("k p c -> p k c"))
            return f
        sch.add("gpsimd", mk_dl(oc, dsl),
                waits=[sch.at(ms)] if m_dmm[oc % 2] is None
                else [m_dmm[oc % 2]],
                inc=lds[oc % 2], is_dma=True, n=16)
        m_w = sch.at(lds[oc % 2])
        def mk_dn(dsl_):
            def f(t_):
                inst = None
                for ib in range(44):
                    inst = t_.matmul(b["ps_d"][:, 0:512],
                                     dsl_[:, ib * 128:(ib + 1) * 128],
                                     b["ic"](ib),
                                     start=(ib == 0), stop=(ib == 43))
                return inst
            return f
        sch.add("tensor", mk_dn(dsl), waits=[m_w, m_int, sch.at(vs)], inc=ms)
        m_dmm[oc % 2] = sch.at(ms)
        def mk_ev(oc_):
            def f(v):
                return v.tensor_copy(b["ybf"][:, oc_ * 512:(oc_ + 1) * 512],
                                     b["ps_d"][:, 0:512])
            return f
        sch.add("vector", mk_ev(oc), waits=[sch.at(ms)], inc=vs)
        sch.add("scalar", (lambda oc_: lambda sc: sc.activation(
            b["sq"][:], b["ybf"][:, oc_ * 512:(oc_ + 1) * 512], AF.Square))(oc),
            waits=[sch.at(vs)], inc=ss)
        def mk_rs(oc_):
            def f(t_):
                return t_.matmul(b["ps_r"][0:1, 0:512], b["ones1"][:],
                                 b["sq"][:], start=(oc_ == 0), stop=(oc_ == 15))
            return f
        sch.add("tensor", mk_rs(oc), waits=[sch.at(ss), sch.at(vs)], inc=ms)
    def mk_rms(v):
        v.tensor_scalar(b["rrow"][:], b["ps_r"][0:1, 0:512], 1.0 / D, EPS_RMS,
                        ALU.mult, ALU.add)
        return v.reciprocal(b["rrow"][:], b["rrow"][:])
    sch.add("vector", mk_rms, waits=[sch.at(ms)], inc=vs)
    sch.add("scalar", lambda sc: sc.activation(b["sq"][:, 0:512][0:1, 0:512],
                                               b["rrow"][:], AF.Sqrt),
            waits=[sch.at(vs)], inc=ss)
    sch.add("tensor", lambda t_: t_.matmul(
        b["ps_r"][0:128, 0:512], b["onesr"][:], b["sq"][0:1, 0:512],
        start=True, stop=True), waits=[sch.at(ss)], inc=ms)
    m_rbc = sch.at(ms)
    for oc in range(16):
        def mk_sc2(oc_):
            def f(v):
                return v.tensor_tensor(b["ysc"][:],
                                       b["ybf"][:, oc_ * 512:(oc_ + 1) * 512],
                                       b["ps_r"][0:128, 0:512], ALU.mult)
            return f
        sch.add("vector", mk_sc2(oc), waits=[m_rbc, sch.at(ms)], inc=vs)
        def mk_tb(oc_):
            def f(t_):
                inst = None
                for k in range(4):
                    inst = t_.transpose(
                        b["ps_t"][:].bitcast(BF16)[:, k * 128:(k + 1) * 128],
                        b["ysc"][:, k * 128:(k + 1) * 128], identb[:])
                return inst
            return f
        sch.add("tensor", mk_tb(oc), waits=[sch.at(vs), m_ident], inc=ms)
        def mk_evy(oc_):
            def f(v):
                return v.tensor_copy(
                    b["ytm"][:].rearrange("p (t c) -> p t c", t=4)
                    [:, :, oc_ * 128:(oc_ + 1) * 128],
                    b["ps_t"][:].bitcast(BF16)[:, 0:512]
                    .rearrange("p (k c) -> p k c", k=4))
            return f
        sch.add("vector", mk_evy(oc), waits=[sch.at(ms), m_prev], inc=vs)
    return sch.at(vs)



TWO_PI = 6.283185307179586
CW1 = float(np.float32(6.28125))
CW2 = float(np.float32(TWO_PI - 6.28125))
CW3 = TWO_PI - CW1 - float(np.float32(TWO_PI - 6.28125))
THETA_SCALE = -math.log(10000.0) / 32.0


def build_L8():
    ctx = Ctx()
    nc = ctx.nc
    d_x = nc.dram_tensor("x", [TPC, D], F32, kind="ExternalInput").ap()
    d_rp = [nc.dram_tensor(f"rp{i}", [TPC, D], F32, kind="ExternalInput").ap()
            for i in range(NCORES)]
    d_ysh = nc.dram_tensor("ysh", [TPC, D], BF16, kind="ExternalInput").ap()
    d_wgu = nc.dram_tensor("wguT", [16, 128, 2 * I_FFN1], BF16,
                           kind="ExternalInput").ap()
    d_wdn = nc.dram_tensor("wdnT", [44, 128, D], BF16, kind="ExternalInput").ap()
    d_wq = nc.dram_tensor("wqT", [16, 128, D], BF16, kind="ExternalInput").ap()
    d_wk = nc.dram_tensor("wkT", [16, 128, 512], BF16, kind="ExternalInput").ap()
    d_wv = nc.dram_tensor("wvT", [16, 128, 512], BF16, kind="ExternalInput").ap()
    d_bq = nc.dram_tensor("bq", [D, 1], F32, kind="ExternalInput").ap()
    d_bk = nc.dram_tensor("bk", [512, 1], F32, kind="ExternalInput").ap()
    d_bv = nc.dram_tensor("bv", [512, 1], F32, kind="ExternalInput").ap()
    d_pos = nc.dram_tensor("poscol", [TPC, 1], F32, kind="ExternalInput").ap()
    d_x4 = nc.dram_tensor("x4", [TPC, D], F32, kind="ExternalOutput").ap()
    d_QT = nc.dram_tensor("QT", [D, TPC], BF16, kind="ExternalOutput").ap()
    d_K = nc.dram_tensor("K", [TPC, 512], BF16, kind="ExternalOutput").ap()
    d_V = nc.dram_tensor("V", [TPC, 512], BF16, kind="ExternalOutput").ap()
    d_qn = nc.dram_tensor("qn", [1, NHEAD * TPC], F32, kind="ExternalOutput").ap()
    d_kn = nc.dram_tensor("kn", [1, NKV * TPC], F32, kind="ExternalOutput").ap()
    ds_x3 = nc.dram_tensor("x3scr", [TPC, D], F32, kind="Internal").ap()

    sch = Sched(ctx)
    lds = [sch.sem(f"ld{i}") for i in range(4)]
    vs = sch.sem("vs")
    ss = sch.sem("ss")
    ms = sch.sem("ms")
    gs = sch.sem("gs")
    sto = [sch.sem(f"st{i}") for i in range(2)]
    gld = [sch.sem(f"gld{i}") for i in range(2)]

    identb = ctx.sbuf("identb", [128, 128], BF16)
    e2 = ctx.sbuf("e2", [128, 2], F32)
    xb = ctx.sbuf("xb", [128, D], F32)
    rb = ctx.sbuf("rb", [128, D], F32)
    racc = ctx.sbuf("racc", [128, D], F32)
    h32 = ctx.sbuf("h32", [128, D], F32)
    hbf = ctx.sbuf("hbf", [128, D], BF16)
    scr = {"sq": ctx.sbuf("sq", [128, D], F32)}
    for n in ("cm", "cv", "ct", "cr", "cb"):
        scr[n] = ctx.sbuf(n, [128, 1], F32)
    hT = ctx.sbuf("hT", [128, 16 * 512], BF16)
    fb = mk_ffn1_bufs(ctx, sch, "f_")
    ytm = fb["ytm"]
    qslab = [fb["wslab"][i][:, 0:2048] for i in range(2)]
    qT_sb = ctx.sbuf("qT_sb", [128, 16 * 512], BF16)
    biasc = ctx.sbuf("biasc", [128, 16], F32)
    bql = ctx.sbuf("bql", [16, 128], F32)
    identf = ctx.sbuf("identf", [128, 128], F32)
    posc = ctx.sbuf("posc", [128, 1], F32)
    thetaM = ctx.sbuf("thetaM", [128, 32], F32)
    angs = ctx.sbuf("angs", [128, 32], F32)
    kwb = ctx.sbuf("kwb", [128, 32], F32)
    cosM = ctx.sbuf("cosM", [128, 32], F32)
    sinM = ctx.sbuf("sinM", [128, 32], F32)
    rtmp = h32[:, 0:1024]
    rtm2 = h32[:, 1024:2048]
    kvtm = ctx.sbuf("kvtm", [128, 2 * 512], BF16)
    qn2 = ctx.sbuf("qn2", [2, 512], F32)
    sqq = scr["sq"][:, 0:512]
    qfm = ctx.sbuf("qfm", [128, 512], BF16)
    ps_t = [ctx.psum(f"ps_t{i}", [128, 512], F32) for i in range(2)]
    ps_p = fb["ps_u"]
    ps_n = ctx.psum("ps_n", [128, 512], F32)

    sch.add("gpsimd", lambda g: g.memset(identb[:], 1.0), inc=gs)
    sch.add("gpsimd", lambda g: g.affine_select(
        out=identb[:], in_=identb[:], pattern=[[1, 128]],
        compare_op=ALU.is_equal, fill=0.0, base=0, channel_multiplier=-1),
        waits=[sch.at(gs)], inc=gs)
    sch.add("gpsimd", lambda g: g.memset(e2[:], 0.0), inc=gs)
    sch.add("gpsimd", lambda g: g.memset(e2[0:64, 0:1], 1.0), inc=gs)
    sch.add("gpsimd", lambda g: g.memset(e2[64:128, 1:2], 1.0), inc=gs)
    sch.add("gpsimd", lambda g: g.iota(thetaM[:].bitcast(I32), [[1, 32]],
                                       base=0, channel_multiplier=0), inc=gs)
    sch.add("gpsimd", lambda g: g.memset(identf[:], 1.0), inc=gs)
    sch.add("gpsimd", lambda g: g.affine_select(
        out=identf[:], in_=identf[:], pattern=[[1, 128]],
        compare_op=ALU.is_equal, fill=0.0, base=0, channel_multiplier=-1),
        waits=[sch.at(gs)], inc=gs)
    m_gp = sch.at(gs)
    sch.add("vector", lambda v: v.tensor_copy(thetaM[:], thetaM[:].bitcast(I32)),
            waits=[m_gp], inc=vs)
    sch.add("scalar", lambda sc: sc.activation(thetaM[:], thetaM[:], AF.Exp,
                                               scale=THETA_SCALE),
            waits=[sch.at(vs)], inc=ss)
    m_theta = sch.at(ss)

    def rope_tables(tile):
        sch.dma("sync", posc[:], d_pos[tile * 128:(tile + 1) * 128, :],
                waits=[sch.at(vs)], inc=lds[3])
        def red(v):
            v.tensor_scalar(angs[:], thetaM[:], posc[:], None, ALU.mult)
            v.tensor_scalar(kwb[:], angs[:], 1.0 / TWO_PI, None, ALU.mult)
            v.tensor_copy(kwb[:].bitcast(I32), kwb[:])
            v.tensor_copy(kwb[:], kwb[:].bitcast(I32))
            v.scalar_tensor_tensor(angs[:], kwb[:], -CW1, angs[:], ALU.mult,
                                   ALU.add)
            v.scalar_tensor_tensor(angs[:], kwb[:], -CW2, angs[:], ALU.mult,
                                   ALU.add)
            v.scalar_tensor_tensor(angs[:], kwb[:], -CW3, angs[:], ALU.mult,
                                   ALU.add)
            v.tensor_scalar(kwb[:], angs[:], math.pi, None, ALU.is_gt)
            return v.scalar_tensor_tensor(angs[:], kwb[:], -TWO_PI, angs[:],
                                          ALU.mult, ALU.add)
        sch.add("vector", red, waits=[sch.at(lds[3]), m_theta], inc=vs)
        sch.add("scalar", lambda sc: sc.activation(sinM[:], angs[:], AF.Sin),
                waits=[sch.at(vs)], inc=ss)
        def c2(v):
            v.tensor_scalar(angs[:], angs[:], math.pi / 2, None, ALU.add)
            v.tensor_scalar(kwb[:], angs[:], math.pi, None, ALU.is_gt)
            return v.scalar_tensor_tensor(angs[:], kwb[:], -TWO_PI, angs[:],
                                          ALU.mult, ALU.add)
        sch.add("vector", c2, waits=[sch.at(ss)], inc=vs)
        sch.add("scalar", lambda sc: sc.activation(cosM[:], angs[:], AF.Sin),
                waits=[sch.at(vs)], inc=ss)
        return sch.at(ss)

    def rope_apply(buf_ap, ngrp, m_ready):
        lo = buf_ap.rearrange("p (g d) -> p g d", d=64)[:, :, 0:32]
        hi = buf_ap.rearrange("p (g d) -> p g d", d=64)[:, :, 32:64]
        t1 = rtmp[:, 0:ngrp * 32].rearrange("p (g d) -> p g d", d=32)
        t2 = rtm2[:, 0:ngrp * 32].rearrange("p (g d) -> p g d", d=32)
        def f(v):
            cB = cosM[:].rearrange("p (g d) -> p g d", g=1).to_broadcast(
                [128, ngrp, 32])
            sB = sinM[:].rearrange("p (g d) -> p g d", g=1).to_broadcast(
                [128, ngrp, 32])
            v.tensor_tensor(t1, lo, cB, ALU.mult)
            v.tensor_tensor(t2, hi, sB, ALU.mult)
            v.tensor_tensor(t1, t1, t2, ALU.subtract)
            v.tensor_tensor(t2, hi, cB, ALU.mult)
            v.tensor_tensor(hi, lo, sB, ALU.mult)
            v.tensor_tensor(hi, t2, hi, ALU.add)
            return v.tensor_copy(lo, t1)
        sch.add("vector", f, waits=[m_ready], inc=vs)
        return sch.at(vs)

    def emit_tp_block(m_hbf, tt):
        for pk in range(4):
            def mk_tp(pp):
                def f(t_):
                    inst = None
                    for k in range(4):
                        kk = pp * 4 + k
                        inst = t_.transpose(
                            ps_t[pp % 2][:].bitcast(BF16)[:, k * 128:(k + 1) * 128],
                            hbf[:, kk * 128:(kk + 1) * 128], identb[:])
                    return inst
                return f
            sch.add("tensor", mk_tp(pk), waits=[m_hbf, m_gp, sch.at(vs)],
                    inc=ms)
            def mk_ev(pp, tt_):
                def f(v):
                    return v.tensor_copy(
                        hT[:].rearrange("p (k c) -> p k c", k=16)
                        [:, pp * 4:(pp + 1) * 4, tt_ * 128:(tt_ + 1) * 128],
                        ps_t[pp % 2][:].bitcast(BF16)[:, 0:512]
                        .rearrange("p (k c) -> p k c", k=4))
                return f
            sch.add("vector", mk_ev(pk, tt), waits=[sch.at(ms)], inc=vs)

    m_prev_ytm = None
    for b2 in range(2):
        for tt in range(4):
            t = b2 * 4 + tt
            sch.dma("sync", xb[:], d_x[t * 128:(t + 1) * 128, :],
                    waits=[sch.at(vs)], inc=lds[0])
            for i in range(NCORES):
                sch.dma("sync", rb[:], d_rp[i][t * 128:(t + 1) * 128, :],
                        waits=[sch.at(vs)], inc=lds[0])
                m_l = sch.at(lds[0])
                if i == 0:
                    sch.add("vector", lambda v: v.tensor_copy(racc[:], rb[:]),
                            waits=[m_l], inc=vs)
                else:
                    sch.add("vector", lambda v: v.tensor_tensor(
                        racc[:], racc[:], rb[:], ALU.add), waits=[m_l], inc=vs)
            sch.dma("sync", rb[:].bitcast(BF16)[:, 0:D],
                    d_ysh[t * 128:(t + 1) * 128, :], waits=[sch.at(vs)],
                    inc=lds[0])
            def mk_x3(v):
                v.tensor_tensor(racc[:], racc[:], rb[:].bitcast(BF16)[:, 0:D],
                                ALU.add)
                return v.scalar_tensor_tensor(racc[:], racc[:], 0.5, xb[:],
                                              ALU.mult, ALU.add)
            sch.add("vector", mk_x3, waits=[sch.at(lds[0])], inc=vs)
            m_x3 = sch.at(vs)
            sch.dma("scalar", ds_x3[t * 128:(t + 1) * 128, :], racc[:],
                    waits=[m_x3, sch.at(ss)], inc=sto[0])
            emit_ln(sch, vs, ss, racc[:], h32[:], scr, waits=[m_x3],
                    apply_waits=[sch.at(ms)])
            sch.add("scalar", lambda sc: sc.copy(hbf[:], h32[:]), inc=ss)
            emit_tp_block(sch.at(ss), tt)
        m_hT = sch.at(vs)
        m_ytm = emit_ffn1_block(sch, fb, identb, m_gp, d_wgu, d_wdn, m_hT, hT,
                                m_prev=m_prev_ytm,
                                m_start=(sch.at(ms) if b2 else None))
        for tt in range(4):
            t = b2 * 4 + tt
            sch.dma("sync", xb[:], ds_x3[t * 128:(t + 1) * 128, :],
                    waits=[m_ytm, sch.at(sto[0]), sch.at(vs)], inc=lds[0])
            def mk_x4(tt_):
                def f(v):
                    v.tensor_copy(racc[:], ytm[:, tt_ * D:(tt_ + 1) * D])
                    return v.scalar_tensor_tensor(racc[:], racc[:], 0.5, xb[:],
                                                  ALU.mult, ALU.add)
                return f
            sch.add("vector", mk_x4(tt), waits=[sch.at(lds[0])], inc=vs)
            m_x4 = sch.at(vs)
            sch.dma("scalar", d_x4[t * 128:(t + 1) * 128, :], racc[:],
                    waits=[m_x4, sch.at(ss)], inc=sto[1])
            emit_ln(sch, vs, ss, racc[:], h32[:], scr, waits=[m_x4],
                    apply_waits=[sch.at(ms)])
            sch.add("scalar", lambda sc: sc.copy(hbf[:], h32[:]), inc=ss)
            emit_tp_block(sch.at(ss), tt)
        m_h2T = sch.at(vs)
        if b2 == 0:
            m_qmm = {0: None, 1: None}
        sch.dma("sync", bql[0:16, :], d_bq.rearrange("(k c) a -> k (c a)", c=128),
                waits=[sch.at(ss)], inc=lds[3])
        sch.add("tensor", lambda t_: t_.transpose(ps_n[:, 0:16], bql[0:16, :],
                                                  identf[0:16, 0:16]),
                waits=[sch.at(lds[3]), m_gp, sch.at(vs)], inc=ms)
        sch.add("vector", lambda v: v.tensor_copy(biasc[:], ps_n[:, 0:16]),
                waits=[sch.at(ms)], inc=vs)
        m_bias = sch.at(vs)
        for oc in range(16):
            sl = qslab[oc % 2]
            def mk_ql(oc_, sl_):
                def f(g):
                    return g.dma_start(
                        out=sl_.rearrange("p (k c) -> p k c", k=16),
                        in_=d_wq[:, :, 128 * oc_:128 * (oc_ + 1)]
                        .rearrange("k p c -> p k c"))
                return f
            sch.add("gpsimd", mk_ql(oc, sl),
                    waits=([sch.at(ms), sch.at(fb["ms"])]
                           if m_qmm[oc % 2] is None else [m_qmm[oc % 2]]),
                    inc=gld[oc % 2], is_dma=True, n=16)
            m_w = sch.at(gld[oc % 2])
            def mk_qm(sl_):
                def f(t_):
                    inst = None
                    for k in range(16):
                        inst = t_.matmul(ps_p[:, 0:512],
                                         sl_[:, k * 128:(k + 1) * 128],
                                         hT[:, k * 512:(k + 1) * 512],
                                         start=(k == 0), stop=(k == 15))
                    return inst
                return f
            sch.add("tensor", mk_qm(sl), waits=[m_w, m_h2T, m_bias,
                                                sch.at(ss), sch.at(vs),
                                                sch.at(fb["vs"])], inc=ms)
            m_qmm[oc % 2] = sch.at(ms)
            def mk_qe(oc_):
                def f(sc):
                    return sc.activation(qT_sb[:, oc_ * 512:(oc_ + 1) * 512],
                                         ps_p[:, 0:512], AF.Identity,
                                         scale=1.0, bias=biasc[:, oc_:oc_ + 1])
                return f
            sch.add("scalar", mk_qe(oc), waits=[sch.at(ms)], inc=ss)
        m_qproj = sch.at(ss)
        for tt in range(4):
            t = b2 * 4 + tt
            m_tab = rope_tables(t)
            for oc in range(16):
                def mk_tq(oc_, tt_):
                    def f(t_):
                        return t_.transpose(
                            ps_t[0][:].bitcast(BF16)[:, (oc_ % 4) * 128:
                                                     (oc_ % 4) * 128 + 128],
                            qT_sb[:, oc_ * 512 + tt_ * 128:
                                  oc_ * 512 + tt_ * 128 + 128], identb[:])
                    return f
                sch.add("tensor", mk_tq(oc, tt), waits=[m_qproj, sch.at(vs)],
                        inc=ms)
                if oc % 4 == 3:
                    def mk_evq(oc_, tt_):
                        def f(v):
                            return v.tensor_copy(
                                ytm[:, tt_ * D + (oc_ - 3) * 128:
                                    tt_ * D + (oc_ + 1) * 128],
                                ps_t[0][:].bitcast(BF16)[:, 0:512])
                        return f
                    sch.add("vector", mk_evq(oc, tt), waits=[sch.at(ms)],
                            inc=vs)
            m_rope = rope_apply(ytm[:, tt * D:(tt + 1) * D], 32, m_tab)
            for oc in range(16):
                def mk_tb(oc_, tt_):
                    def f(t_):
                        return t_.transpose(
                            ps_t[1][:].bitcast(BF16)[:, (oc_ % 4) * 128:
                                                     (oc_ % 4) * 128 + 128],
                            ytm[:, tt_ * D + oc_ * 128:tt_ * D + (oc_ + 1) * 128],
                            identb[:])
                    return f
                sch.add("tensor", mk_tb(oc, tt), waits=[m_rope, sch.at(ss)],
                        inc=ms)
                if oc % 4 == 3:
                    def mk_eq(sc):
                        return sc.activation(qfm[:],
                                             ps_t[1][:].bitcast(BF16)[:, 0:512],
                                             AF.Copy, scale=0.125)
                    sch.add("scalar", mk_eq, waits=[sch.at(ms), sch.at(sto[0])],
                            inc=ss)
                    def mk_sq2(oc_, t_glob):
                        def f(sc):
                            return sc.dma_start(
                                out=d_QT[(oc_ - 3) * 128:(oc_ + 1) * 128,
                                         t_glob * 128:(t_glob + 1) * 128]
                                .rearrange("(k p) c -> p k c", p=128),
                                in_=qfm[:].rearrange("p (k c) -> p k c", k=4))
                        return f
                    sch.add("scalar", mk_sq2(oc, t), waits=[sch.at(ss)],
                            inc=sto[0], is_dma=True, n=16)
                    sch.add("scalar", lambda sc: sc.activation(sqq, qfm[:],
                                                               AF.Square),
                            inc=ss)
                    def mk_qn(t_):
                        return t_.matmul(ps_n[0:2, 0:512], e2[:], sqq,
                                         start=True, stop=True)
                    sch.add("tensor", mk_qn, waits=[sch.at(ss), sch.at(vs)],
                            inc=ms)
                    def mk_qnev(v):
                        return v.tensor_copy(qn2[:], ps_n[0:2, 0:512])
                    sch.add("vector", mk_qnev, waits=[sch.at(ms),
                                                      sch.at(sto[1])], inc=vs)
                    def mk_sqn(oc_, t_glob):
                        def f(sc):
                            return sc.dma_start(
                                out=d_qn.rearrange("a (f h t c) -> h (a f) t c",
                                                   f=16, h=2, t=8)
                                [:, oc_ - 3:oc_ + 1, t_glob, :],
                                in_=qn2[:].rearrange("h (k c) -> h k c", k=4))
                        return f
                    sch.add("scalar", mk_sqn(oc, t), waits=[sch.at(vs)],
                            inc=sto[1], is_dma=True, n=16)
        for which, dw, dbias, dout in (("k", d_wk, d_bk, d_K),
                                       ("v", d_wv, d_bv, d_V)):
            sch.dma("sync", bql[0:4, :],
                    dbias.rearrange("(k c) a -> k (c a)", c=128),
                    waits=[sch.at(ss)], inc=lds[3])
            sch.add("tensor", (lambda t_: t_.transpose(
                ps_n[:, 0:4], bql[0:4, :], identf[0:4, 0:4])),
                waits=[sch.at(lds[3]), sch.at(vs)], inc=ms)
            sch.add("vector", lambda v: v.tensor_copy(biasc[:, 0:4],
                                                      ps_n[:, 0:4]),
                    waits=[sch.at(ms), sch.at(sto[1])], inc=vs)
            m_bias = sch.at(vs)
            for oc in range(4):
                sl = qslab[oc % 2]
                def mk_kl(oc_, sl_, dw_):
                    def f(g):
                        return g.dma_start(
                            out=sl_.rearrange("p (k c) -> p k c", k=16),
                            in_=dw_[:, :, 128 * oc_:128 * (oc_ + 1)]
                            .rearrange("k p c -> p k c"))
                    return f
                sch.add("gpsimd", mk_kl(oc, sl, dw),
                        waits=([sch.at(ms)] if m_qmm[oc % 2] is None
                               else [m_qmm[oc % 2]]),
                        inc=gld[oc % 2], is_dma=True, n=16)
                m_w = sch.at(gld[oc % 2])
                def mk_km(sl_):
                    def f(t_):
                        inst = None
                        for k in range(16):
                            inst = t_.matmul(ps_p[:, 0:512],
                                             sl_[:, k * 128:(k + 1) * 128],
                                             hT[:, k * 512:(k + 1) * 512],
                                             start=(k == 0), stop=(k == 15))
                        return inst
                    return f
                sch.add("tensor", mk_km(sl), waits=[m_w, m_h2T, m_bias,
                                                    sch.at(ss), sch.at(vs)],
                        inc=ms)
                m_qmm[oc % 2] = sch.at(ms)
                def mk_ke(oc_):
                    def f(sc):
                        return sc.activation(qT_sb[:, oc_ * 512:(oc_ + 1) * 512],
                                             ps_p[:, 0:512], AF.Identity,
                                             scale=1.0,
                                             bias=biasc[:, oc_:oc_ + 1])
                    return f
                sch.add("scalar", mk_ke(oc), waits=[sch.at(ms)], inc=ss)
                if which == "k":
                    sch.add("scalar", (lambda oc_: lambda sc: sc.activation(
                        sqq, qT_sb[:, oc_ * 512:(oc_ + 1) * 512], AF.Square)
                        )(oc), inc=ss)
                    sch.add("tensor", lambda t_: t_.matmul(
                        ps_n[0:2, 0:512], e2[:], sqq, start=True, stop=True),
                        waits=[sch.at(ss), sch.at(vs)], inc=ms)
                    sch.add("vector", lambda v: v.tensor_copy(
                        qn2[:], ps_n[0:2, 0:512]),
                        waits=[sch.at(ms), sch.at(sto[1])], inc=vs)
                    def mk_skn(oc_, b2_):
                        def f(sc):
                            return sc.dma_start(
                                out=d_kn.rearrange("a (o h b c) -> h (a o) b c",
                                                   o=4, h=2, b=2)
                                [:, oc_, b2_, :], in_=qn2[:])
                        return f
                    sch.add("scalar", mk_skn(oc, b2), waits=[sch.at(vs)],
                            inc=sto[1], is_dma=True, n=16)
            m_kproj = sch.at(ss)
            for tt in range(4):
                t = b2 * 4 + tt
                for oc in range(4):
                    def mk_tk(oc_, tt_):
                        def f(t_):
                            return t_.transpose(
                                ps_t[0][:].bitcast(BF16)[:, oc_ * 128:
                                                         (oc_ + 1) * 128],
                                qT_sb[:, oc_ * 512 + tt_ * 128:
                                      oc_ * 512 + tt_ * 128 + 128], identb[:])
                        return f
                    sch.add("tensor", mk_tk(oc, tt), waits=[m_kproj,
                                                            sch.at(vs)], inc=ms)
                def mk_evk(tt_):
                    def f(v):
                        return v.tensor_copy(
                            kvtm[:, (tt_ % 2) * 512:(tt_ % 2) * 512 + 512],
                            ps_t[0][:].bitcast(BF16)[:, 0:512])
                    return f
                sch.add("vector", mk_evk(tt), waits=[sch.at(ms),
                                                     sch.at(sto[tt % 2])],
                        inc=vs)
                if which == "k":
                    m_tab = rope_tables(t)
                    rope_apply(kvtm[:, (tt % 2) * 512:(tt % 2) * 512 + 512],
                               8, m_tab)
                m_kv = sch.at(vs)
                def mk_skv(dout_, t_glob, tt_):
                    def f(sc):
                        return sc.dma_start(
                            out=dout_[t_glob * 128:(t_glob + 1) * 128, :],
                            in_=kvtm[:, (tt_ % 2) * 512:(tt_ % 2) * 512 + 512])
                    return f
                sch.add("scalar", mk_skv(dout, t, tt),
                        waits=[m_kv, sch.at(ss)], inc=sto[tt % 2],
                        is_dma=True, n=16)
        m_prev_ytm = sch.at(vs)
    sch.add("vector", lambda v: v.memset(posc[:], 0.0),
            waits=[sch.at(sto[0]), sch.at(sto[1]), sch.at(lds[0]),
                   sch.at(lds[1])], inc=vs)
    sch.build()
    ctx.close()
    return nc



def build_L9():
    ctx = Ctx()
    nc = ctx.nc
    d_x4 = nc.dram_tensor("x4", [TPC, D], F32, kind="ExternalInput").ap()
    d_QT = nc.dram_tensor("QT", [D, TPC], BF16, kind="ExternalInput").ap()
    d_qn = nc.dram_tensor("qn", [1, NHEAD * TPC], F32, kind="ExternalInput").ap()
    d_K = nc.dram_tensor("K", [T, 512], BF16, kind="ExternalInput").ap()
    d_V = nc.dram_tensor("V", [T, 512], BF16, kind="ExternalInput").ap()
    d_knf = nc.dram_tensor("knf", [1, NKV * T], F32, kind="ExternalInput").ap()
    d_msk = nc.dram_tensor("amask", [128, NPACKS * 512], BF16,
                           kind="ExternalInput").ap()
    d_wo = nc.dram_tensor("woP", [16, 128, D], BF16, kind="ExternalInput").ap()
    d_bo = nc.dram_tensor("bo", [D, 1], F32, kind="ExternalInput").ap()
    d_wgu = nc.dram_tensor("wguT", [16, 128, 2 * I_FFN1], BF16,
                           kind="ExternalInput").ap()
    d_wdn = nc.dram_tensor("wdnT", [44, 128, D], BF16, kind="ExternalInput").ap()
    d_xout = nc.dram_tensor("xout", [TPC, D], F32, kind="ExternalOutput").ap()
    ds_x5 = nc.dram_tensor("x5scr", [TPC, D], F32, kind="Internal").ap()

    sch = Sched(ctx)
    attn_sb, m_attn, (avs, ass, ams), ab = emit_attention(
        sch, ctx, d_QT, d_K, d_V, d_qn, d_knf, d_msk)
    qt_sb, vplus, ktb, masks = (ab["qt_sb"], ab["vplus"], ab["ktb"],
                                ab["masks"])

    lds = [sch.sem(f"ld9{i}") for i in range(2)]
    gld = [sch.sem(f"gld9{i}") for i in range(2)]
    vs = sch.sem("vs9")
    ss = sch.sem("ss9")
    ms = sch.sem("ms9")
    gs = sch.sem("gs9")
    sto = [sch.sem(f"st9{i}") for i in range(2)]

    identb = ctx.sbuf("identb9", [128, 128], BF16)
    xb = ctx.sbuf("xb9", [128, D], F32)
    h32 = ctx.sbuf("h329", [128, D], F32)
    hbf = ctx.sbuf("hbf9", [128, D], BF16)
    scr = {"sq": ctx.sbuf("sq9", [128, D], F32)}
    for n in ("cm", "cv", "ct", "cr", "cb"):
        scr[n] = ctx.sbuf(n + "9", [128, 1], F32)
    hT = ctx.sbuf("hT9", [128, 16 * 512], BF16)
    oT = ctx.sbuf("oT9", [128, 512], BF16)
    otf = ctx.sbuf("otf9", [128, D], F32)
    biasc = ctx.sbuf("biasc9", [128, 16], F32)
    bql = ctx.sbuf("bql9", [16, 128], F32)
    f_ones1 = ctx.sbuf("f_ones19", [128, 1], F32)
    f_onesr = ctx.sbuf("f_onesr9", [1, 128], F32)
    f_rrow = ctx.sbuf("f_rrow9", [1, 512], F32)
    ps_f = [ctx.psum(f"ps_f9{i}", [128, 512], F32) for i in range(3)]

    qf32 = qt_sb[:].bitcast(F32)
    def ic(c):
        if c < 16:
            return vplus[:, 8192 + c * 512:8192 + (c + 1) * 512]
        if c < 36:
            return masks[:, (c - 16) * 512:(c - 15) * 512]
        return ktb[:, (c - 36) * 512:(c - 35) * 512]
    fb = {
        "lds": [sch.sem("f9ld0"), sch.sem("f9ld1")],
        "vs": sch.sem("f9vs"), "ss": sch.sem("f9ss"), "ms": sch.sem("f9ms"),
        "wslab": [vplus[:, 0:4096], vplus[:, 4096:8192]],
        "dslab": [qt_sb[:, 0:5632], qt_sb[:, 5632:11264]],
        "ic": ic,
        "sgs": qf32[:, 5632:6144], "sis": qf32[:, 6144:6656],
        "sq": qf32[:, 6656:7168], "ysc": qt_sb[:, 14336:14848],
        "ones1": f_ones1, "onesr": f_onesr, "rrow": f_rrow,
        "ybf": attn_sb[:, 0:8192], "ytm": attn_sb[:, 8192:16384],
        "ps_g": ab["ps_s"][0], "ps_u": ab["ps_s"][1],
        "ps_d": ps_f[0], "ps_r": ps_f[1], "ps_t": ps_f[2],
    }
    otm = masks[:, 0:8192]

    sch.add("gpsimd", lambda g: g.memset(identb[:], 1.0), inc=gs)
    sch.add("gpsimd", lambda g: g.affine_select(
        out=identb[:], in_=identb[:], pattern=[[1, 128]],
        compare_op=ALU.is_equal, fill=0.0, base=0, channel_multiplier=-1),
        waits=[sch.at(gs)], inc=gs)
    m_gp = sch.at(gs)
    sch.add("vector", lambda v: v.memset(f_ones1[:], 1.0), inc=vs)
    sch.add("vector", lambda v: v.memset(f_onesr[:], 1.0), inc=vs)
    sch.dma("sync", bql[0:16, :], d_bo.rearrange("(k c) a -> k (c a)", c=128),
            inc=lds[1])
    sch.add("tensor", lambda t_: t_.transpose(ps_f[0][:, 0:16], bql[0:16, :],
                                              ab["identf"][0:16, 0:16]),
            waits=[sch.at(lds[1]), m_attn], inc=ms)
    sch.add("vector", lambda v: v.tensor_copy(biasc[:], ps_f[0][:, 0:16]),
            waits=[sch.at(ms)], inc=vs)
    m_bias = sch.at(vs)

    ps_p = ab["ps_s"][0]
    ps_t2 = ab["ps_s"][1]
    m_omm = {0: None, 1: None}
    for b2 in range(2):
        for oc in range(16):
            sl = fb["wslab"][oc % 2]
            def mk_wl(oc_, sl_):
                def f(g):
                    return g.dma_start(
                        out=sl_[:, 0:2048].rearrange("p (k c) -> p k c", k=16),
                        in_=d_wo[:, :, 128 * oc_:128 * (oc_ + 1)]
                        .rearrange("k p c -> p k c"))
                return f
            w0 = [sch.at(ms)] if m_omm[oc % 2] is None else [m_omm[oc % 2]]
            if b2 == 0 and oc < 2:
                w0.append(m_attn)
            sch.add("gpsimd", mk_wl(oc, sl), waits=w0, inc=gld[oc % 2],
                    is_dma=True, n=16)
            m_w = sch.at(gld[oc % 2])
            def mk_o(sl_, b2_):
                def f(t_):
                    inst = None
                    for k in range(16):
                        inst = t_.matmul(
                            ps_p[:, 0:512], sl_[:, k * 128:(k + 1) * 128],
                            attn_sb[:, k * TPC + b2_ * 512:
                                    k * TPC + b2_ * 512 + 512],
                            start=(k == 0), stop=(k == 15))
                    return inst
                return f
            sch.add("tensor", mk_o(sl, b2), waits=[m_w, m_attn, m_bias,
                                                   sch.at(ss), sch.at(vs)],
                    inc=ms)
            m_omm[oc % 2] = sch.at(ms)
            def mk_oe(oc_):
                def f(sc):
                    return sc.activation(oT[:], ps_p[:, 0:512], AF.Identity,
                                         scale=1.0, bias=biasc[:, oc_:oc_ + 1])
                return f
            sch.add("scalar", mk_oe(oc), waits=[sch.at(ms)], inc=ss)
            def mk_tb(t_):
                inst = None
                for k in range(4):
                    inst = t_.transpose(
                        ps_t2[:].bitcast(BF16)[:, k * 128:(k + 1) * 128],
                        oT[:, k * 128:(k + 1) * 128], identb[:])
                return inst
            sch.add("tensor", mk_tb, waits=[sch.at(ss), m_gp], inc=ms)
            def mk_evo(oc_):
                def f(v):
                    return v.tensor_copy(
                        otm.rearrange("p (t c) -> p t c", t=4)
                        [:, :, oc_ * 128:(oc_ + 1) * 128],
                        ps_t2[:].bitcast(BF16)[:, 0:512]
                        .rearrange("p (k c) -> p k c", k=4))
                return f
            sch.add("vector", mk_evo(oc), waits=[sch.at(ms)], inc=vs)
        m_otm = sch.at(vs)
        for tt in range(4):
            ti = b2 * 4 + tt
            sch.dma("sync", xb[:], d_x4[ti * 128:(ti + 1) * 128, :],
                    waits=[m_otm, sch.at(vs)], inc=lds[0])
            def mk_x5(tt_):
                def f(v):
                    v.tensor_copy(otf[:], otm[:, tt_ * D:(tt_ + 1) * D])
                    return v.tensor_tensor(otf[:], otf[:], xb[:], ALU.add)
                return f
            sch.add("vector", mk_x5(tt), waits=[sch.at(lds[0]), sch.at(sto[0])],
                    inc=vs)
            def mk_sx5(ti_):
                def f(sc):
                    return sc.dma_start(out=ds_x5[ti_ * 128:(ti_ + 1) * 128, :],
                                        in_=otf[:])
                return f
            sch.add("scalar", mk_sx5(ti), waits=[sch.at(vs), sch.at(ss)],
                    inc=sto[0], is_dma=True, n=16)
    m_x5done = sch.at(sto[0])

    def emit_tp9(m_hbf, tt):
        for pk in range(4):
            def mk_tp(pp):
                def f(t_):
                    inst = None
                    for k in range(4):
                        kk = pp * 4 + k
                        inst = t_.transpose(
                            ab["ps_o"][pp % 2][:].bitcast(BF16)
                            [:, k * 128:(k + 1) * 128],
                            hbf[:, kk * 128:(kk + 1) * 128], identb[:])
                    return inst
                return f
            sch.add("tensor", mk_tp(pk), waits=[m_hbf, m_gp, sch.at(vs)],
                    inc=ms)
            def mk_ev(pp, tt_):
                def f(v):
                    return v.tensor_copy(
                        hT[:].rearrange("p (k c) -> p k c", k=16)
                        [:, pp * 4:(pp + 1) * 4, tt_ * 128:(tt_ + 1) * 128],
                        ab["ps_o"][pp % 2][:].bitcast(BF16)[:, 0:512]
                        .rearrange("p (k c) -> p k c", k=4))
                return f
            sch.add("vector", mk_ev(pk, tt), waits=[sch.at(ms)], inc=vs)

    m_prev_ytm = None
    for b2 in range(2):
        for tt in range(4):
            ti = b2 * 4 + tt
            w0 = [m_x5done, sch.at(vs)]
            if m_prev_ytm is not None:
                w0.append(m_prev_ytm)
            sch.dma("sync", xb[:], ds_x5[ti * 128:(ti + 1) * 128, :],
                    waits=w0, inc=lds[0])
            emit_ln(sch, vs, ss, xb[:], h32[:], scr,
                    waits=[sch.at(lds[0])], apply_waits=[sch.at(ms)])
            sch.add("scalar", lambda sc: sc.copy(hbf[:], h32[:]), inc=ss)
            emit_tp9(sch.at(ss), tt)
        m_hT = sch.at(vs)
        m_ytm = emit_ffn1_block(sch, fb, identb, m_gp, d_wgu, d_wdn, m_hT, hT,
                                m_prev=m_prev_ytm,
                                m_start=([m_attn, sch.at(ms)] if b2 == 0
                                         else None))
        for tt in range(4):
            ti = b2 * 4 + tt
            sch.dma("sync", xb[:], ds_x5[ti * 128:(ti + 1) * 128, :],
                    waits=[m_ytm, sch.at(vs)], inc=lds[0])
            def mk_x6(tt_):
                def f(v):
                    v.tensor_copy(otf[:], fb["ytm"][:, tt_ * D:(tt_ + 1) * D])
                    return v.scalar_tensor_tensor(otf[:], otf[:], 0.5, xb[:],
                                                  ALU.mult, ALU.add)
                return f
            sch.add("vector", mk_x6(tt), waits=[sch.at(lds[0]), sch.at(sto[1])],
                    inc=vs)
            def mk_sx6(ti_):
                def f(sc):
                    return sc.dma_start(out=d_xout[ti_ * 128:(ti_ + 1) * 128, :],
                                        in_=otf[:])
                return f
            sch.add("scalar", mk_sx6(ti), waits=[sch.at(vs), sch.at(ss)],
                    inc=sto[1], is_dma=True, n=16)
        m_prev_ytm = sch.at(vs)
    sch.add("vector", lambda v: v.memset(f_ones1[:], 1.0),
            waits=[sch.at(sto[0]), sch.at(sto[1])], inc=vs)
    sch.build()
    ctx.close()
    return nc

import ml_dtypes

BFNP = ml_dtypes.bfloat16


def attn_masks(core):
    NEG = -30000.0
    out = np.full((128, NPACKS * 512), NEG, np.float32)
    per_batch = sorted({core, 15 - core, 16 + core, 31 - core})
    tri = np.where(np.arange(128)[None, :] >= np.arange(128)[:, None],
                   0.0, NEG).astype(np.float32)
    for s4 in range(4):
        jq = per_batch[s4]
        for pg in range(PACKS[s4]):
            for kk in range(4):
                ch = 4 * pg + kk
                col0 = (PACK_OFF[s4] + pg) * 512 + kk * 128
                if ch < jq:
                    out[:, col0:col0 + 128] = 0.0
                elif ch == jq:
                    out[:, col0:col0 + 128] = tri
    return out.astype(BFNP)


def _f32(a):
    return np.ascontiguousarray(np.asarray(a), dtype=np.float32)


def _p0_maps(inp):
    Wq, Wo = inp["L1_attn_Wq"], inp["L1_attn_Wo"]
    Wk, Wv = inp["L1_attn_Wk"], inp["L1_attn_Wv"]
    sg, su, sd = inp["L0_ffn_sg"], inp["L0_ffn_su"], inp["L0_ffn_sd"]
    wg, wu, wd = inp["L0_ffn_w_gate"], inp["L0_ffn_w_up"], inp["L0_ffn_w_down"]
    gup = np.ascontiguousarray(np.asarray(inp["L1_ffn_gate_up_packed"]),
                               dtype=np.int32)
    gus = np.ascontiguousarray(np.asarray(inp["L1_ffn_gate_up_scales"]),
                               dtype=np.float16)
    dnp = np.ascontiguousarray(np.asarray(inp["L1_ffn_down_packed"]),
                               dtype=np.int32)
    dns = np.ascontiguousarray(np.asarray(inp["L1_ffn_down_scales"]),
                               dtype=np.float16)
    maps = []
    for c in range(NCORES):
        rows2 = lambda W: np.stack([_f32(W[128 * j:128 * (j + 1)])
                                    for j in (c, c + 8)])
        maps.append({
            "q_rows": rows2(Wq), "o_rows": rows2(Wo),
            "kv_rows": _f32(Wk[128 * c:128 * (c + 1)])[None] if c < 4
                       else _f32(Wv[128 * (c - 4):128 * (c - 3)])[None],
            "sg_rows": _f32(sg[128 * c:128 * (c + 1)])[None],
            "su_rows": _f32(su[128 * c:128 * (c + 1)])[None],
            "sd_rows": rows2(sd),
            "wg_rows": _f32(wg[4 * c:4 * c + 4]).reshape(32, 128, 2048),
            "wu_rows": _f32(wu[4 * c:4 * c + 4]).reshape(32, 128, 2048),
            "wd_rows": _f32(wd[4 * c:4 * c + 4]).reshape(64, 128, 1024),
            "gup": gup[GUS * c:GUS * (c + 1)],
            "gus": np.ascontiguousarray(gus[:, GUS * c:GUS * (c + 1)]),
            "dnp": dnp[DNS * c:DNS * (c + 1)],
            "dns": np.ascontiguousarray(dns[:, DNS * c:DNS * (c + 1)]),
        })
    return maps


def _asm_2blk(resP, key, w):
    out = np.empty((w // 128, 128, 2048), BFNP)
    for c in range(NCORES):
        t = resP[c]["T_" + key]
        for jj, J in enumerate((c, c + 8)):
            out[:, :, 128 * J:128 * (J + 1)] = t[:, :, 128 * jj:128 * (jj + 1)]
    return out


def _moe_pass(x_shards, gw, mb, ap_consts, bx_consts):
    mapsA = [{"x": x_shards[c], "gate_w": gw, "mbias": mb}
             for c in range(NCORES)]
    resA = run_bass_kernel_spmd(build_A(), mapsA,
                                core_ids=list(range(NCORES))).results
    counts_g = np.zeros((NCHUNK, E), np.float32)
    for c in range(NCORES):
        cc = resA[c]["counts"].reshape(NTILE, E)
        for t, j in enumerate(OWN[c]):
            counts_g[j] = cc[t]
    mapsAP = []
    for c in range(NCORES):
        m = {"h": resA[c]["h"], "counts_g": counts_g, "S": resA[c]["S"],
             "idx6": resA[c]["idx6"], "w6": resA[c]["w6"]}
        m.update(ap_consts[c])
        mapsAP.append(m)
    resAP = run_bass_kernel_spmd(build_AP(), mapsAP,
                                 core_ids=list(range(NCORES))).results
    hfull = np.zeros((T + 8, D), BFNP)
    for c in range(NCORES):
        hfull[OWN_ROWS[c]] = resA[c]["h"]
    mapsBX = []
    nsl = EPC * CAP
    for c in range(NCORES):
        m = {"hfull": hfull}
        for i in range(NCORES):
            buf = np.zeros((SSROWS2, 2), np.float32)
            buf[0:nsl] = resAP[i]["sspart"][nsl * c:nsl * (c + 1)]
            m[f"ss{i}"] = buf
        m.update(bx_consts[c])
        mapsBX.append(m)
    resBX = run_bass_kernel_spmd(build_BX(), mapsBX,
                                 core_ids=list(range(NCORES))).results
    rp_slices = [[np.ascontiguousarray(resBX[i]["rp"][OWN_ROWS[c]])
                  for i in range(NCORES)] for c in range(NCORES)]
    ysh = [resAP[c]["ysh"] for c in range(NCORES)]
    return rp_slices, ysh


def kernel(**inputs):
    cores = list(range(NCORES))
    X = _f32(inputs["L0_attn_x"]).reshape(T, D)
    xs = shard_rows(X)
    gw = _f32(inputs["L0_ffn_gate_w"])
    mb = _f32(inputs["L0_ffn_bias"]).reshape(1, E)
    Wq0 = _f32(inputs["L0_attn_Wq"])
    Wk0 = _f32(inputs["L0_attn_Wk"])
    Wv0 = _f32(inputs["L0_attn_Wv"])
    Wo0 = _f32(inputs["L0_attn_Wo"])
    bq1 = _f32(inputs["L1_attn_bq"]).reshape(D, 1)
    bk1 = _f32(inputs["L1_attn_bk"]).reshape(512, 1)
    bv1 = _f32(inputs["L1_attn_bv"]).reshape(512, 1)
    bo1 = _f32(inputs["L1_attn_bo"]).reshape(D, 1)

    resP = run_bass_kernel_spmd(build_P0(), _p0_maps(inputs),
                                core_ids=cores).results
    wqT1 = _asm_2blk(resP, "q_rows", D)
    woT1 = _asm_2blk(resP, "o_rows", D)
    sdT = _asm_2blk(resP, "sd_rows", I_MOE)
    wkT1 = np.empty((16, 128, 512), BFNP)
    wvT1 = np.empty((16, 128, 512), BFNP)
    sgT = np.empty((16, 128, 1024), BFNP)
    suT = np.empty((16, 128, 1024), BFNP)
    for c in range(NCORES):
        if c < 4:
            wkT1[:, :, 128 * c:128 * (c + 1)] = resP[c]["T_kv_rows"]
        else:
            wvT1[:, :, 128 * (c - 4):128 * (c - 3)] = resP[c]["T_kv_rows"]
        sgT[:, :, 128 * c:128 * (c + 1)] = resP[c]["T_sg_rows"]
        suT[:, :, 128 * c:128 * (c + 1)] = resP[c]["T_su_rows"]
    wguT = np.empty((16, 128, 2 * I_FFN1), BFNP)
    wdnT = np.empty((44, 128, D), BFNP)
    for c in range(NCORES):
        wguT[:, :, GUS * c:GUS * (c + 1)] = resP[c]["wguT"]
        wdnT[:, :, DNS * c:DNS * (c + 1)] = resP[c]["wdnT"]
    woP = np.empty((16, 128, D), BFNP)
    for cb in range(16):
        a_, g_ = cb // 4, cb % 4
        for h2 in range(2):
            f0 = 512 * a_ + 256 * h2 + 64 * g_
            for ff in range(64):
                f = f0 + ff
                woP[cb, 64 * h2 + ff, :] = woT1[f // 128, f % 128, :]

    ap_consts, bx_consts = [], []
    masks = [attn_masks(c) for c in range(NCORES)]
    poscols = [((OWN_ROWS[c] % S) + 1).astype(np.float32).reshape(TPC, 1)
               for c in range(NCORES)]
    for c in range(NCORES):
        sel = np.zeros((NCHUNK, NTILE), np.float32)
        for t, j in enumerate(OWN[c]):
            sel[j, t] = 1.0
        tokcol = (OWN_ROWS[c] + 1).astype(np.float32).reshape(TPC, 1)
        ap_consts.append({"sgT": sgT, "suT": suT, "sdT": sdT, "selmat": sel,
                          "tokcol": tokcol})
        bx_consts.append({"wgT": resP[c]["T_wg_rows"],
                          "wuT": resP[c]["T_wu_rows"],
                          "wdT": resP[c]["T_wd_rows"]})

    rp1, ysh1 = _moe_pass(xs, gw, mb, ap_consts, bx_consts)
    mapsL4 = []
    for c in range(NCORES):
        m = {"x": xs[c], "ysh": ysh1[c], "Wq": Wq0, "Wk": Wk0, "Wv": Wv0}
        for i in range(NCORES):
            m[f"rp{i}"] = rp1[c][i]
        mapsL4.append(m)
    resL4 = run_bass_kernel_spmd(build_L4(), mapsL4, core_ids=cores).results
    K0 = np.empty((T, 512), BFNP)
    V0 = np.empty((T, 512), BFNP)
    kn0 = np.empty((NKV, T), np.float32)
    for c in range(NCORES):
        K0[OWN_ROWS[c]] = resL4[c]["K"]
        V0[OWN_ROWS[c]] = resL4[c]["V"]
        kn0[:, OWN_ROWS[c]] = resL4[c]["kn"].reshape(NKV, TPC)
    knf0 = np.ascontiguousarray(kn0.reshape(1, NKV * T))
    mapsL5 = [{"x1": resL4[c]["x1"], "QT": resL4[c]["QT"],
               "qn": resL4[c]["qn"], "K": K0, "V": V0, "knf": knf0,
               "amask": masks[c], "Wo": Wo0} for c in range(NCORES)]
    resL5 = run_bass_kernel_spmd(build_L5(), mapsL5, core_ids=cores).results
    x2s = [resL5[c]["x2"] for c in range(NCORES)]

    rp2, ysh2 = _moe_pass(x2s, gw, mb, ap_consts, bx_consts)
    mapsL8 = []
    for c in range(NCORES):
        m = {"x": x2s[c], "ysh": ysh2[c], "wguT": wguT, "wdnT": wdnT,
             "wqT": wqT1, "wkT": wkT1, "wvT": wvT1, "bq": bq1, "bk": bk1,
             "bv": bv1, "poscol": poscols[c]}
        for i in range(NCORES):
            m[f"rp{i}"] = rp2[c][i]
        mapsL8.append(m)
    resL8 = run_bass_kernel_spmd(build_L8(), mapsL8, core_ids=cores).results
    K1 = np.empty((T, 512), BFNP)
    V1 = np.empty((T, 512), BFNP)
    kn1 = np.empty((NKV, T), np.float32)
    for c in range(NCORES):
        K1[OWN_ROWS[c]] = resL8[c]["K"]
        V1[OWN_ROWS[c]] = resL8[c]["V"]
        kn1[:, OWN_ROWS[c]] = resL8[c]["kn"].reshape(NKV, TPC)
    knf1 = np.ascontiguousarray(kn1.reshape(1, NKV * T))

    mapsL9 = [{"x4": resL8[c]["x4"], "QT": resL8[c]["QT"],
               "qn": resL8[c]["qn"], "K": K1, "V": V1, "knf": knf1,
               "amask": masks[c], "woP": woP, "bo": bo1, "wguT": wguT,
               "wdnT": wdnT} for c in range(NCORES)]
    resL9 = run_bass_kernel_spmd(build_L9(), mapsL9, core_ids=cores).results
    out = unshard_rows([resL9[c]["xout"] for c in range(NCORES)])
    return np.ascontiguousarray(out.reshape(B, S, D), dtype=np.float32)
